# version 17
# baseline (speedup 1.0000x reference)
import numpy as np
from contextlib import ExitStack

import concourse.bass as bass
import concourse.mybir as mybir
from concourse.bass_utils import run_bass_kernel_spmd

F32 = mybir.dt.float32
BF16 = mybir.dt.bfloat16
I8 = mybir.dt.int8
ALU = mybir.AluOpType
AF = mybir.ActivationFunctionType
AX = mybir.AxisListType.X

MAGIC = 12582912.0
QMAX = 127.0

D, DI, DS, DTR, DC, IF, NH, HD = 1024, 2048, 16, 64, 4, 2816, 16, 64
FT, NDT, NIT = D // 128, DI // 128, IF // 128



def _dap(src, offset, dims):
    return bass.AP(src.tensor, src.offset + offset, [list(d) for d in dims])


def build_nc(T=2048, n_cores=8, dbg=False):
    TC = min(512, T)
    NCH = T // TC
    NTT = T // 128
    NKT = T // 128
    N5 = T // 512 if T >= 512 else 1
    C5 = min(512, T)

    nc = bass.Bass("TRN2", target_bir_lowering=False, debug=False)

    x_h = nc.dram_tensor("x", [T, D], F32, kind="ExternalInput").ap()
    out_h = nc.dram_tensor("out", [T, D], F32, kind="ExternalOutput").ap()
    W = {}
    for name, shape, dt in [
        ("W_in", [2 * DI, D], F32), ("conv_w", [DI, 1, DC], F32), ("conv_b", [DI], F32),
        ("Wx", [DTR + 2 * DS, DI], F32), ("Wdt", [DI, DTR], F32), ("bdt", [DI], F32),
        ("A_log", [DI, DS], F32), ("Dd", [DI], F32),
        ("conv_w_b", [DI, 1, DC], F32), ("conv_b_b", [DI], F32),
        ("Wx_b", [DTR + 2 * DS, DI], F32), ("Wdt_b", [DI, DTR], F32), ("bdt_b", [DI], F32),
        ("A_b_log", [DI, DS], F32), ("Dd_b", [DI], F32),
        ("W_out", [D, DI], F32),
        ("clip_x", [1], F32), ("clip_down", [1], F32),
        ("w_gate", [IF, D], F32), ("w_up", [IF, D], F32), ("w_down", [D, IF], F32),
        ("w_attn", [D, D + 2 * HD], F32), ("w_proj", [D, D], F32),
        ("fw_int8", [D, D], I8), ("fw_scale", [D, 1], F32), ("fbias", [D], F32),
    ]:
        W[name] = nc.dram_tensor("i_" + name, shape, dt, kind="ExternalInput").ap()

    S = {
        "WinT": nc.dram_tensor("s_WinT", [FT, 128, 2 * DI], BF16).ap(),
        "wgT": nc.dram_tensor("s_wgT", [FT, 128, IF], BF16).ap(),
        "wuT": nc.dram_tensor("s_wuT", [FT, 128, IF], BF16).ap(),
        "wdT": nc.dram_tensor("s_wdT", [NIT, 128, D], BF16).ap(),
        "woutT": nc.dram_tensor("s_woutT", [NDT, 128, D], BF16).ap(),
        "wattnT": nc.dram_tensor("s_wattnT", [FT, 128, D + 2 * HD], BF16).ap(),
        "wprojT": nc.dram_tensor("s_wprojT", [FT, 128, D], BF16).ap(),
        "fwT": nc.dram_tensor("s_fwT", [FT, 128, D], BF16).ap(),
        "wxs": nc.dram_tensor("s_wxs", [128, NDT * 96 * 2], BF16).ap(),
        "wdts": nc.dram_tensor("s_wdts", [64, NDT * 128 * 2], BF16).ap(),
        "xz": nc.dram_tensor("s_xz", [2 * NDT, 128, T], BF16).ap(),
        "hpre": nc.dram_tensor("s_hpre", [NIT, 128, T], BF16).ap(),
        "outf": nc.dram_tensor("s_outf", [NDT, 128, T], BF16).ap(),
        "outc": nc.dram_tensor("s_outc", [NDT, 128, T], BF16).ap(),
    }
    for i in range(3):
        S[f"cci{i}"] = nc.dram_tensor(f"s_cci{i}", [1, 8], F32).ap()
        S[f"cco{i}"] = nc.dram_tensor(f"s_cco{i}", [1, 8], F32).ap()
    if dbg:
        for i in range(8):
            S[f"dbg{i}"] = nc.dram_tensor(f"s_dbg{i}", [FT, 128, T], F32).ap()

    ctx = ExitStack()
    sems = {}

    class Sem:
        def __init__(self, name):
            self.h = ctx.enter_context(nc.semaphore(name))
            self.n = 0

        def inc(self, inst, k=1):
            inst.then_inc(self.h, k)
            self.n += k
            return inst

        def dma(self, inst):
            return self.inc(inst, 16)

    def sem(name):
        if name not in sems:
            sems[name] = Sem(name)
        return sems[name]

    s_ld = sem("ld"); s_lw = sem("lw"); s_pe = sem("pe"); s_ve = sem("ve")
    s_ac = sem("ac"); s_gp = sem("gp"); s_sp = sem("sp"); s_wp = sem("wp")
    s_bc = sem("bc"); s_cc = sem("cc"); s_st = sem("st"); s_ax = sem("ax")
    s_l2 = sem("l2"); s_w2 = sem("w2"); s_p2 = sem("p2"); s_s2 = sem("s2")
    s_cx = sem("cx"); s_gd = sem("gd"); s_g2 = sem("g2")
    s_o1 = sem("o1"); s_o2 = sem("o2")

    dmk = {}

    def dma_mark(key, semx, inst):
        semx.dma(inst)
        dmk[key] = (semx.h, semx.n)

    def dma_wait(eng, key):
        h, n = dmk[key]
        eng.wait_ge(h, n)

    def fwd_marks(key, sems2, base, count, k=16):
        cnt = list(base)
        for i in range(count):
            p = i % 2
            cnt[p] += k
            dmk[key + (i,)] = (sems2[p].h, cnt[p])

    _snap = [[]]

    def bk():
        _snap[0] = [(s.h, s.n) for s in sems.values()
                    if s.n > 0 and s is not s_cx]
        return nc.Block()

    def gate(*engines):
        for eng in engines:
            for h, n in _snap[0]:
                eng.wait_ge(h, n)

    def sb(name, shape, dt, st=None):
        return (st or ctx).enter_context(nc.sbuf_tensor(name, list(shape), dt))

    with ctx:
        res = [sb(f"res{i}", [128, T], F32) for i in range(FT)]
        stat = sb("stat", [128, 64], F32)
        ident_b = sb("ident_b", [128, 128], BF16)
        ident_f = sb("ident_f", [128, 128], F32)
        onesq_b = sb("onesq_b", [128, 128], BF16)
        onecol_b = sb("onecol_b", [128, 1], BF16)
        scm = sb("scm", [1, 16], F32)
        colA = sb("colA", [128, 8], F32)
        a_cols = sb("a_cols", [128, NDT * DS * 2], F32)
        convw = sb("convw", [128, NDT * DC * 2], F32)
        convb = sb("convb", [128, NDT * 2], F32)
        dcol = sb("dcol", [128, NDT * 2], F32)
        bdtc = sb("bdtc", [128, NDT * 2], F32)
        wgs = sb("wgs", [128, 2 * NIT], F32)
        wds = sb("wds", [128, FT], F32)
        fsc = sb("fsc", [128, FT], F32)
        fbc = sb("fbc", [128, FT], F32)
        scmc = sb("scmc", [128, 16], F32)
        psum = [ctx.enter_context(nc.psum_tensor(f"ps{i}", [128, 512], F32))
                for i in range(8)]

        def psb(i, r, w):
            return psum[i][:].bitcast(BF16)[0:r, 0:w]

        def dbg_dump(idx):
            if not dbg:
                return
            with bk() as blk:
                @blk.gpsimd
                def _(g):
                    gate(g)
                    for j in range(FT):
                        s_gd.dma(g.dma_start(S[f"dbg{idx}"][j], res[j][:]))
                    g.wait_ge(s_gd.h, s_gd.n)

        for s in sems.values():
            nc.gpsimd.sem_clear(s.h)
        nc.all_engine_barrier()

        with bk() as blk:
            @blk.gpsimd
            def _(g):
                s_gp.inc(g.memset(scm[:], 0.0))
                s_gp.inc(g.memset(stat[:], 0.0))
                s_gp.inc(g.memset(onesq_b[:], 1.0))
                s_gp.inc(g.memset(onecol_b[:], 1.0))
                s_gp.inc(g.memset(ident_b[:], 0.0))
                s_gp.inc(g.memset(ident_f[:], 0.0))
                g.wait_ge(s_gp.h, s_gp.n)
                s_gp.inc(g.affine_select(ident_b[:], onesq_b[:], [[1, 128]], ALU.is_equal,
                                         0.0, base=0, channel_multiplier=-1))
                s_gp.inc(g.affine_select(ident_f[:], onesq_b[:], [[1, 128]], ALU.is_equal,
                                         0.0, base=0, channel_multiplier=-1))

            @blk.sync
            def _(sy):
                sy.wait_ge(s_gp.h, 2)
                for dirn in range(2):
                    cw = W["conv_w"] if dirn == 0 else W["conv_w_b"]
                    cb = W["conv_b"] if dirn == 0 else W["conv_b_b"]
                    al = W["A_log"] if dirn == 0 else W["A_b_log"]
                    dd = W["Dd"] if dirn == 0 else W["Dd_b"]
                    bd = W["bdt"] if dirn == 0 else W["bdt_b"]
                    o = dirn * NDT
                    s_lw.dma(sy.dma_start(convw[:, o * DC:(o + NDT) * DC],
                                          _dap(cw, 0, [[4, 128], [512, NDT], [1, 4]])))
                    s_lw.dma(sy.dma_start(convb[:, o:o + NDT],
                                          _dap(cb, 0, [[1, 128], [128, NDT]]),
                                          allow_slow_non_contiguous=True))
                    s_lw.dma(sy.dma_start(a_cols[:, o * DS:(o + NDT) * DS],
                                          _dap(al, 0, [[DS, 128], [128 * DS, NDT], [1, DS]])))
                    s_lw.dma(sy.dma_start(dcol[:, o:o + NDT],
                                          _dap(dd, 0, [[1, 128], [128, NDT]]),
                                          allow_slow_non_contiguous=True))
                    s_lw.dma(sy.dma_start(bdtc[:, o:o + NDT],
                                          _dap(bd, 0, [[1, 128], [128, NDT]]),
                                          allow_slow_non_contiguous=True))
                s_lw.dma(sy.dma_start(scm[:, 0:1], W["clip_x"].unsqueeze(0)))
                s_lw.dma(sy.dma_start(scm[:, 1:2], W["clip_down"].unsqueeze(0)))
                s_lw.dma(sy.dma_start(fsc[:], _dap(W["fw_scale"], 0, [[1, 128], [128, FT]]),
                                      allow_slow_non_contiguous=True))
                s_lw.dma(sy.dma_start(fbc[:], _dap(W["fbias"], 0, [[1, 128], [128, FT]]),
                                      allow_slow_non_contiguous=True))

        with bk() as blk:
            @blk.scalar
            def _(sc):
                gate(sc)
                s_ac.inc(sc.activation(a_cols[:], a_cols[:], AF.Exp))
                s_ac.inc(sc.activation(scm[:, 2:4], scm[:, 0:2], AF.Exp, scale=-1.0))

            @blk.vector
            def _(v):
                gate(v)
                v.wait_ge(s_ac.h, s_ac.n)
                s_ve.inc(v.tensor_scalar(a_cols[:], a_cols[:], -1.0, None, ALU.mult))
                s_ve.inc(v.tensor_scalar(scm[:, 2:4], scm[:, 2:4], 1.0, None, ALU.add))
                v.wait_ge(s_ve.h, s_ve.n)
                s_ve.inc(v.reciprocal(scm[:, 2:4], scm[:, 2:4]))

        with ExitStack() as ph:
            xstg = [sb(f"xstg{i}", [128, D], F32, ph) for i in range(2)]
            with bk() as blk:
                ld0, ve0, pe0 = s_ld.n, s_ve.n, s_pe.n

                @blk.sync
                def _(sy):
                    for i in range(NTT):
                        if i >= 2:
                            sy.wait_ge(s_pe.h, pe0 + (i - 1) * FT)
                        dma_mark(("x", i), s_ld if i % 2 == 0 else s_l2,
                                 sy.dma_start(xstg[i % 2][:], x_h[i * 128:(i + 1) * 128, :]))

                @blk.tensor
                def _(t):
                    t.wait_ge(s_gp.h, 8)
                    for i in range(NTT):
                        dma_wait(t, ("x", i))
                        for j in range(FT):
                            q = i * FT + j
                            if q >= 4:
                                t.wait_ge(s_ve.h, ve0 + q - 3)
                            s_pe.inc(t.transpose(psum[q % 4][:, 0:128],
                                                 xstg[i % 2][:, j * 128:(j + 1) * 128],
                                                 ident_f[:]))

                @blk.vector
                def _(v):
                    for i in range(NTT):
                        for j in range(FT):
                            q = i * FT + j
                            v.wait_ge(s_pe.h, pe0 + q + 1)
                            s_ve.inc(v.tensor_copy(res[j][:, i * 128:(i + 1) * 128],
                                                   psum[q % 4][:, 0:128]))

        with ExitStack() as ph:
            wld = [sb(f"wld{i}", [128, 3072], F32, ph) for i in range(2)]
            wstg = [sb(f"wstg{i}", [128, 3072], BF16, ph) for i in range(2)]
            wstr = [sb(f"wstr{i}", [128, 3072], BF16, ph) for i in range(2)]

            def prep_weight(src_ap, n_rt, n_cols, dst, quant=False, rs_store=None,
                            scale_const=None, src_i8=False):
                NC = n_cols // 128
                with bk() as blk:
                    lw0, ve1, pe1, ac1, wp1 = s_lw.n, s_ve.n, s_pe.n, s_ac.n, s_wp.n
                    p21 = s_p2.n
                    fwd_marks(("pws",), [s_wp, s_p2], [s_wp.n, s_p2.n], n_rt)

                    @blk.sync
                    def _(sy):
                        for r in range(n_rt):
                            sy.wait_ge(s_ve.h, ve1 + ((r - 1) * (5 if quant else 1)
                                                      if r >= 2 else 0))
                            dstb = (wld[r % 2][:, 0:n_cols] if not src_i8 else
                                    wld[r % 2].bitcast(I8)[:, 0:n_cols])
                            dma_mark(("pw", r), s_lw if r % 2 == 0 else s_w2,
                                     sy.dma_start(dstb, src_ap[r * 128:(r + 1) * 128, :]))

                    @blk.vector
                    def _(v):
                        for r in range(n_rt):
                            dma_wait(v, ("pw", r))
                            if r >= 2:
                                v.wait_ge(s_pe.h, pe1 + (r - 1) * NC)
                            src = (wld[r % 2][:, 0:n_cols] if not src_i8 else
                                   wld[r % 2].bitcast(I8)[:, 0:n_cols])
                            if quant:
                                s_ve.inc(v.tensor_reduce(colA[:, 0:1], src, AX, ALU.max,
                                                         apply_absolute_value=True))
                                v.wait_ge(s_ve.h, s_ve.n)
                                s_ve.inc(v.reciprocal(colA[:, 1:2], colA[:, 0:1]))
                                v.wait_ge(s_ve.h, s_ve.n)
                                s_ve.inc(v.tensor_scalar(colA[:, 1:2], colA[:, 1:2],
                                                         QMAX, None, ALU.mult))
                                if rs_store is not None:
                                    buf, c0 = rs_store
                                    v.tensor_scalar(buf[:, c0 + r:c0 + r + 1], colA[:, 0:1],
                                                    1.0 / QMAX, None, ALU.mult)
                                v.wait_ge(s_ve.h, s_ve.n)
                                s_ve.inc(v.tensor_scalar(wld[r % 2][:, 0:n_cols], src,
                                                         colA[:, 1:2], MAGIC, ALU.mult, ALU.add))
                                v.wait_ge(s_ve.h, s_ve.n)
                                i = v.tensor_scalar(wstg[r % 2][:, 0:n_cols],
                                                    wld[r % 2][:, 0:n_cols],
                                                    MAGIC, None, ALU.subtract)
                                s_ve.inc(i)
                            elif scale_const is not None:
                                s_ve.inc(v.tensor_scalar(wstg[r % 2][:, 0:n_cols], src,
                                                         scale_const, None, ALU.mult))
                            else:
                                s_ve.inc(v.tensor_copy(wstg[r % 2][:, 0:n_cols], src))

                    k_ve = 5 if quant else 1
                    @blk.tensor
                    def _(t):
                        for r in range(n_rt):
                            t.wait_ge(s_ve.h, ve1 + (r + 1) * k_ve)
                            if r >= 2:
                                dma_wait(t, ("pws", r - 2))
                            else:
                                t.wait_ge(s_wp.h, wp1)
                                t.wait_ge(s_p2.h, p21)
                            for c in range(NC):
                                q = r * NC + c
                                if q >= 2:
                                    t.wait_ge(s_ac.h, ac1 + q - 1)
                                s_pe.inc(t.transpose(psb(4 + q % 2, 128, 128),
                                                     wstg[r % 2][:, c * 128:(c + 1) * 128],
                                                     ident_b[:]))

                    @blk.scalar
                    def _(sc):
                        for r in range(n_rt):
                            for c in range(NC):
                                q = r * NC + c
                                sc.wait_ge(s_pe.h, pe1 + q + 1)
                                s_ac.inc(sc.copy(wstr[r % 2][:, c * 128:(c + 1) * 128],
                                                 psb(4 + q % 2, 128, 128)))

                    @blk.gpsimd
                    def _(g):
                        for r in range(n_rt):
                            g.wait_ge(s_ac.h, ac1 + (r + 1) * NC)
                            WT = dst.shape[2]
                            (s_wp if r % 2 == 0 else s_p2).dma(g.dma_start(
                                _dap(dst, r * 128, [[WT, 128], [128 * WT, NC], [1, 128]]),
                                wstr[r % 2][:, 0:NC * 128]))

            def prep_cast(src_ap, n_rt, n_cols, dst):
                with bk() as blk:
                    lw0, ve1, wp1 = s_lw.n, s_ve.n, s_wp.n
                    p21 = s_p2.n
                    fwd_marks(("pcs",), [s_wp, s_p2], [s_wp.n, s_p2.n], n_rt)

                    @blk.sync
                    def _(sy):
                        for r in range(n_rt):
                            sy.wait_ge(s_ve.h, ve1 + (r - 1 if r >= 2 else 0))
                            dma_mark(("pc", r), s_lw if r % 2 == 0 else s_w2,
                                     sy.dma_start(wld[r % 2][:, 0:n_cols],
                                                  src_ap[r * 128:(r + 1) * 128, :]))

                    @blk.vector
                    def _(v):
                        for r in range(n_rt):
                            dma_wait(v, ("pc", r))
                            if r >= 2:
                                dma_wait(v, ("pcs", r - 2))
                            else:
                                v.wait_ge(s_wp.h, wp1)
                                v.wait_ge(s_p2.h, p21)
                            s_ve.inc(v.tensor_copy(wstg[r % 2][:, 0:n_cols],
                                                   wld[r % 2][:, 0:n_cols]))

                    @blk.gpsimd
                    def _(g):
                        for r in range(n_rt):
                            g.wait_ge(s_ve.h, ve1 + r + 1)
                            (s_wp if r % 2 == 0 else s_p2).dma(
                                g.dma_start(dst[r], wstg[r % 2][:, 0:n_cols]))

            prep_weight(W["W_in"], 2 * DI // 128, D, S["WinT"])
            prep_weight(W["w_gate"], NIT, D, S["wgT"], quant=True, rs_store=(wgs, 0))
            prep_weight(W["w_up"], NIT, D, S["wuT"], quant=True, rs_store=(wgs, NIT))
            prep_weight(W["w_down"], FT, IF, S["wdT"], quant=True, rs_store=(wds, 0))
            prep_weight(W["W_out"], FT, DI, S["woutT"], scale_const=0.5)
            prep_weight(W["fw_int8"], FT, D, S["fwT"], src_i8=True)
            prep_cast(W["w_attn"], FT, D + 2 * HD, S["wattnT"])
            prep_cast(W["w_proj"], FT, D, S["wprojT"])

            wxTp = sb("wxTp", [128, NDT * 96 * 2], BF16, ph)
            wdtTp = sb("wdtTp", [64, NDT * 128 * 2], BF16, ph)
            for dirn in range(2):
                wx_src = W["Wx"] if dirn == 0 else W["Wx_b"]
                wd_src = W["Wdt"] if dirn == 0 else W["Wdt_b"]
                with bk() as blk:
                    ve1, pe1, ac1w = s_ve.n, s_pe.n, s_ac.n

                    @blk.sync
                    def _(sy):
                        gate(sy)
                        s_lw.dma(sy.dma_start(wld[0][0:96, 0:DI], wx_src))
                        s_lw.dma(sy.dma_start(
                            wld[1][:, 0:NDT * DTR],
                            _dap(wd_src, 0, [[DTR, 128], [128 * DTR, NDT], [1, DTR]])))

                    @blk.vector
                    def _(v):
                        v.wait_ge(s_lw.h, s_lw.n)
                        s_ve.inc(v.tensor_copy(wstg[0][0:96, 0:DI], wld[0][0:96, 0:DI]))
                        s_ve.inc(v.tensor_copy(wstg[1][:, 0:NDT * DTR],
                                               wld[1][:, 0:NDT * DTR]))

                    @blk.tensor
                    def _(t):
                        t.wait_ge(s_ve.h, ve1 + 2)
                        for u in range(NDT):
                            if u >= 2:
                                t.wait_ge(s_ac.h, ac1w + u - 1)
                            s_pe.inc(t.transpose(psb(4 + u % 2, 128, 96),
                                                 wstg[0][0:96, u * 128:(u + 1) * 128],
                                                 ident_b[0:96, 0:96]))
                        for u in range(NDT):
                            if u >= 2:
                                t.wait_ge(s_ac.h, ac1w + NDT + u - 1)
                            s_pe.inc(t.transpose(psb(6 + u % 2, 64, 128),
                                                 wstg[1][:, u * DTR:(u + 1) * DTR],
                                                 ident_b[:]))

                    @blk.scalar
                    def _(sc):
                        k = pe1
                        for u in range(NDT):
                            k += 1
                            sc.wait_ge(s_pe.h, k)
                            s_ac.inc(sc.copy(wxTp[:, (dirn * NDT + u) * 96:
                                                   (dirn * NDT + u + 1) * 96],
                                             psb(4 + u % 2, 128, 96)))
                        for u in range(NDT):
                            k += 1
                            sc.wait_ge(s_pe.h, k)
                            s_ac.inc(sc.copy(wdtTp[0:64, (dirn * NDT + u) * 128:
                                                    (dirn * NDT + u + 1) * 128],
                                             psb(6 + u % 2, 64, 128)))

            with bk() as blk:
                @blk.gpsimd
                def _(g):
                    gate(g)
                    s_wp.dma(g.dma_start(S["wxs"], wxTp[:]))
                    s_wp.dma(g.dma_start(S["wdts"], wdtTp[0:64, :]))

        def ln_stats_scale(xn_t, sqt, big32, bc_m, bc_r, grow, rowm, rowr, rowt,
                           cc_idx, sig_col, sinv_slot, s_slot, finish=True):
            with bk() as blk:
                ve1, ac1, pe1 = s_ve.n, s_ac.n, s_pe.n

                @blk.vector
                def _(v):
                    gate(v)
                    for j in range(FT):
                        s_ve.inc(v.tensor_copy(xn_t[j][:], res[j][:]))

                @blk.scalar
                def _(sc):
                    gate(sc)
                    for j in range(FT):
                        sc.wait_ge(s_ve.h, ve1 + j + 1)
                        if j >= 1:
                            sc.wait_ge(s_pe.h, pe1 + j * (2 * N5))
                        s_ac.inc(sc.activation(sqt[:], xn_t[j][:], AF.Square))
                        sc.wait_ge(s_ac.h, s_ac.n)

                @blk.tensor
                def _(t):
                    gate(t)
                    for j in range(FT):
                        t.wait_ge(s_ve.h, ve1 + j + 1)
                        t.wait_ge(s_ac.h, ac1 + j + 1)
                        for b in range(N5):
                            t.matmul(psum[b][0:1, 0:C5], onecol_b[:],
                                     xn_t[j][:, b * C5:(b + 1) * C5],
                                     start=(j == 0), stop=(j == FT - 1))
                            i = t.matmul(psum[4 + b][0:1, 0:C5], onecol_b[:],
                                         sqt[:, b * C5:(b + 1) * C5],
                                         start=(j == 0), stop=(j == FT - 1))
                            s_pe.inc(i, 2)

                @blk.vector
                def _(v):
                    v.wait_ge(s_pe.h, pe1 + FT * 2 * N5)
                    for b in range(N5):
                        s_ve.inc(v.tensor_scalar(rowm[:, b * C5:(b + 1) * C5],
                                                 psum[b][0:1, 0:C5], 1.0 / D, None, ALU.mult))
                        s_ve.inc(v.tensor_scalar(rowt[:, b * C5:(b + 1) * C5],
                                                 psum[4 + b][0:1, 0:C5], 1.0 / D, None, ALU.mult))
                    v.wait_ge(s_ve.h, s_ve.n)
                    s_ve.inc(v.tensor_tensor(rowr[:], rowm[:], rowm[:], ALU.mult))
                    v.wait_ge(s_ve.h, s_ve.n)
                    s_ve.inc(v.tensor_tensor(rowr[:], rowt[:, 0:T], rowr[:], ALU.subtract))
                    v.wait_ge(s_ve.h, s_ve.n)
                    s_ve.inc(v.tensor_scalar(rowr[:], rowr[:], 1e-5, None, ALU.add))

                @blk.scalar
                def _(sc):
                    sc.wait_ge(s_ve.h, s_ve.n)
                    s_ac.inc(sc.activation(rowr[:], rowr[:], AF.Ln))
                    sc.wait_ge(s_ac.h, s_ac.n)
                    s_ac.inc(sc.activation(rowr[:], rowr[:], AF.Exp, scale=-0.5))

                vem = s_ve.n
                acm = s_ac.n

                @blk.sync
                def _(sy):
                    sy.wait_ge(s_ve.h, vem)
                    sy.wait_ge(s_ac.h, acm)
                    dma_mark(("bcm", cc_idx), s_w2, sy.dma_start(
                        bc_m[:], bass.AP(rowm, 0, [[1, 1], [0, 128], [1, T]])))

                @blk.gpsimd
                def _(g):
                    g.wait_ge(s_ac.h, acm)
                    s_bc.dma(g.dma_start(bc_r[:], bass.AP(rowr, 0, [[1, 1], [0, 128], [1, T]])))

                @blk.vector
                def _(v):
                    v.wait_ge(s_bc.h, s_bc.n)
                    dma_wait(v, ("bcm", cc_idx))
                    for j in range(FT):
                        v.wait_ge(s_ve.h, s_ve.n)
                        s_ve.inc(v.tensor_tensor(big32[0][:], res[j][:], bc_m[:], ALU.subtract))
                        v.wait_ge(s_ve.h, s_ve.n)
                        s_ve.inc(v.tensor_tensor(big32[0][:], big32[0][:], bc_r[:], ALU.mult))
                        v.wait_ge(s_ve.h, s_ve.n)
                        s_ve.inc(v.tensor_reduce(stat[:, j:j + 1], big32[0][:], AX, ALU.max,
                                                 apply_absolute_value=True))
                        v.wait_ge(s_ve.h, s_ve.n)
                        s_ve.inc(v.tensor_copy(xn_t[j][:], big32[0][:]))

                @blk.gpsimd
                def _(g):
                    g.wait_ge(s_ve.h, s_ve.n)
                    s_ax.dma(g.dma_start(grow[:, 0:128 * FT], stat[:, 0:FT]))

                @blk.vector
                def _(v):
                    v.wait_ge(s_ax.h, s_ax.n)
                    s_ve.inc(v.tensor_reduce(rowt[:, 0:1], grow[:, 0:128 * FT],
                                             AX, ALU.max))
                marks = {"ve": s_ve.n, "rowr": rowr, "rowt": rowt}
            if finish:
                ln_cc_start(marks, cc_idx)
                ln_scale_calc(marks, sig_col, sinv_slot, s_slot)
            return marks

        def ln_cc_start(marks, cc_idx):
            rowt = marks["rowt"]
            with bk() as blk:
                @blk.gpsimd
                def _(g):
                    g.wait_ge(s_ve.h, marks["ve"])
                    s_ax.dma(g.dma_start(rowt[:, 8:16],
                                         bass.AP(rowt, 0, [[1, 1], [0, 8], [1, 1]])))
                    g.wait_ge(s_ax.h, s_ax.n)
                    s_cc.dma(g.dma_start(S[f"cci{cc_idx}"], rowt[:, 8:16]))
                    g.wait_ge(s_cc.h, s_cc.n)
                    s_cx.inc(g.collective_compute(
                        "AllReduce", ALU.max, replica_groups=[list(range(n_cores))],
                        ins=[S[f"cci{cc_idx}"].opt()], outs=[S[f"cco{cc_idx}"].opt()]))
                marks["cx"] = s_cx.n
                marks["cc_idx"] = cc_idx

        def ln_scale_calc(marks, sig_col, sinv_slot, s_slot):
            rowr = marks["rowr"]
            rowt = marks["rowt"]
            with bk() as blk:
                @blk.gpsimd
                def _(g):
                    g.wait_ge(s_cx.h, marks["cx"])
                    s_cc.dma(g.dma_start(rowt[:, 0:8], S[f"cco{marks['cc_idx']}"]))
                cc2 = s_cc.n

                @blk.vector
                def _(v):
                    v.wait_ge(s_cc.h, cc2)
                    v.wait_ge(s_ve.h, marks["ve"])
                    if sig_col is not None:
                        s_ve.inc(v.tensor_tensor(rowt[:, 1:2], rowt[:, 0:1],
                                                 scm[:, sig_col:sig_col + 1], ALU.mult))
                        v.wait_ge(s_ve.h, s_ve.n)
                        s_ve.inc(v.tensor_scalar(rowt[:, 1:2], rowt[:, 1:2],
                                                 1.0 / QMAX, None, ALU.mult))
                    else:
                        s_ve.inc(v.tensor_scalar(rowt[:, 1:2], rowt[:, 0:1], 1.0 / QMAX, 1e-5,
                                                 ALU.mult, ALU.max))
                    v.wait_ge(s_ve.h, s_ve.n)
                    s_ve.inc(v.tensor_copy(scm[:, s_slot:s_slot + 1], rowt[:, 1:2]))
                    s_ve.inc(v.reciprocal(scm[:, sinv_slot:sinv_slot + 1], rowt[:, 1:2]))
                    v.wait_ge(s_ve.h, s_ve.n)
                    s_ve.inc(v.tensor_scalar(rowt[:, 0:T], rowr[:],
                                             scm[0:1, sinv_slot:sinv_slot + 1], None, ALU.mult))

                @blk.gpsimd
                def _(g):
                    g.wait_ge(s_ve.h, s_ve.n)
                    s_bc.dma(g.dma_start(scmc[:], bass.AP(scm, 0, [[1, 1], [0, 128], [1, 16]])))

        def ln_quant_sweep(xq_t, big32, bc_m, bc_r, sinv_col=None, rowt_src=None):
            with bk() as blk:
                if rowt_src is not None:
                    @blk.gpsimd
                    def _(g):
                        gate(g)
                        s_bc.dma(g.dma_start(
                            bc_r[:], bass.AP(rowt_src, 0, [[1, 1], [0, 128], [1, T]])))

                @blk.vector
                def _(v):
                    gate(v)
                    if rowt_src is not None:
                        v.wait_ge(s_bc.h, s_bc.n)
                    for j in range(FT):
                        v.wait_ge(s_ve.h, s_ve.n)
                        s_ve.inc(v.tensor_tensor(big32[0][:], res[j][:], bc_m[:], ALU.subtract))
                        v.wait_ge(s_ve.h, s_ve.n)
                        s_ve.inc(v.tensor_tensor(big32[0][:], big32[0][:], bc_r[:], ALU.mult))
                        v.wait_ge(s_ve.h, s_ve.n)
                        if sinv_col is not None:
                            s_ve.inc(v.tensor_scalar(big32[0][:], big32[0][:], sinv_col,
                                                     MAGIC, ALU.mult, ALU.add))
                        else:
                            s_ve.inc(v.tensor_scalar(big32[0][:], big32[0][:], MAGIC,
                                                     MAGIC, ALU.add, ALU.subtract))
                        v.wait_ge(s_ve.h, s_ve.n)
                        if sinv_col is not None:
                            s_ve.inc(v.tensor_scalar(big32[0][:], big32[0][:], MAGIC, QMAX,
                                                     ALU.subtract, ALU.min))
                            v.wait_ge(s_ve.h, s_ve.n)
                            s_ve.inc(v.tensor_scalar(xq_t[j][:], big32[0][:],
                                                     -QMAX - 1.0, None, ALU.max))
                        else:
                            s_ve.inc(v.tensor_scalar(big32[0][:], big32[0][:], QMAX,
                                                     -QMAX - 1.0, ALU.min, ALU.max))
                            v.wait_ge(s_ve.h, s_ve.n)
                            s_ve.inc(v.tensor_copy(xq_t[j][:], big32[0][:]))

        with ExitStack() as pl0:
            pbc = ExitStack()
            bc_m0 = sb("bc_m", [128, T], F32, pbc)
            prow = ExitStack()
            rowm = sb("rowm", [1, T], F32, prow)
            rowr = sb("rowr", [1, T], F32, prow)
            rowt = sb("rowt", [1, 2048], F32, prow)
            with ExitStack() as pxz:
                xn_bf = [sb(f"xnbf{i}", [128, T], BF16, pxz) for i in range(FT)]
                with ExitStack() as ph:
                    sqt = sb("sqt0", [128, T], BF16, ph)
                    big32 = [sb(f"bg{i}", [128, T], F32, ph) for i in range(1)]
                    bc_r0 = sb("bc_r", [128, T], F32, ph)
                    grow = sb("grow0", [1, 128 * FT], F32, ph)
                    marks0 = ln_stats_scale(xn_bf, sqt, big32, bc_m0, bc_r0, grow,
                                            rowm, rowr, rowt, 0, 2, 4, 5, finish=False)
                    ln_cc_start(marks0, 0)

                with ExitStack() as p2:
                    winT = [sb(f"winT{k}", [128, 2 * DI], BF16, p2) for k in range(FT)]
                    xzst = [sb(f"xzst{i}", [128, T], BF16, p2) for i in range(2)]
                    with bk() as blk:
                        lw0, pe1, ac1, sp1 = s_lw.n, s_pe.n, s_ac.n, s_sp.n
                        fwd_marks(("xzs",), [s_sp, s_p2], [s_sp.n, s_p2.n], 2 * NDT)

                        @blk.sync
                        def _(sy):
                            gate(sy)
                            for k in range(FT):
                                s_lw.dma(sy.dma_start(winT[k][:], S["WinT"][k]))

                        @blk.tensor
                        def _(t):
                            gate(t)
                            t.wait_ge(s_lw.h, s_lw.n)
                            for ct in range(2 * NDT):
                                for ch in range(N5):
                                    q = ct * N5 + ch
                                    if q >= 4:
                                        t.wait_ge(s_ac.h, ac1 + q - 3)
                                    for k in range(FT):
                                        i = t.matmul(psum[q % 4][:, 0:C5],
                                                     winT[k][:, ct * 128:(ct + 1) * 128],
                                                     xn_bf[k][:, ch * C5:(ch + 1) * C5],
                                                     start=(k == 0), stop=(k == FT - 1))
                                    s_pe.inc(i)

                        @blk.scalar
                        def _(sc):
                            for ct in range(2 * NDT):
                                if ct >= 2:
                                    dma_wait(sc, ("xzs", ct - 2))
                                for ch in range(N5):
                                    q = ct * N5 + ch
                                    sc.wait_ge(s_pe.h, pe1 + q + 1)
                                    s_ac.inc(sc.copy(xzst[ct % 2][:, ch * C5:(ch + 1) * C5],
                                                     psum[q % 4][:, 0:C5]))

                        @blk.gpsimd
                        def _(g):
                            for ct in range(2 * NDT):
                                g.wait_ge(s_ac.h, ac1 + (ct + 1) * N5)
                                (s_sp if ct % 2 == 0 else s_p2).dma(
                                    g.dma_start(S["xz"][ct], xzst[ct % 2][:]))

            ln_scale_calc(marks0, 2, 4, 5)

            with ExitStack() as pffn:
                xq0 = [sb(f"xq{i}", [128, T], BF16, pffn) for i in range(FT)]
                with ExitStack() as ph:
                    big32 = [sb(f"bgq{i}", [128, T], F32, ph) for i in range(1)]
                    bc_rq = sb("bc_rq", [128, T], F32, ph)
                    ln_quant_sweep(xq0, big32, bc_m0, bc_rq, rowt_src=rowt)

                with ExitStack() as p2:
                    wgu = [sb(f"wgu{i}", [128, 2048], BF16, p2) for i in range(2)]
                    hst = [sb(f"hst{i}", [128, T], BF16, p2) for i in range(2)]
                    sgt = [sb(f"sgt{i}", [128, 512], BF16, p2) for i in range(2)]
                    with bk() as blk:
                        lw0, pe1, ve1, ac1, sp1 = s_lw.n, s_pe.n, s_ve.n, s_ac.n, s_sp.n
                        KVE = N5 + 3
                        fwd_marks(("hsp",), [s_sp, s_p2], [s_sp.n, s_p2.n], NIT)

                        @blk.sync
                        def _(sy):
                            gate(sy)
                            for it in range(NIT):
                                if it >= 2:
                                    sy.wait_ge(s_pe.h, pe1 + (it - 1) * N5)
                                sx = s_lw if it % 2 == 0 else s_w2
                                sx.dma(sy.dma_start(
                                    wgu[it % 2][:, 0:1024],
                                    _dap(S["wgT"], it * 128,
                                         [[IF, 128], [128 * IF, FT], [1, 128]])))
                                dma_mark(("wgu", it), sx, sy.dma_start(
                                    wgu[it % 2][:, 1024:2048],
                                    _dap(S["wuT"], it * 128,
                                         [[IF, 128], [128 * IF, FT], [1, 128]])))

                        @blk.tensor
                        def _(t):
                            gate(t)
                            for it in range(NIT):
                                dma_wait(t, ("wgu", it))
                                if it >= 1:
                                    t.wait_ge(s_ac.h, ac1 + it * N5)
                                    t.wait_ge(s_ve.h, ve1 + it * KVE)
                                for ch in range(N5):
                                    for k in range(FT):
                                        t.matmul(psum[ch % 4][:, 0:C5],
                                                 wgu[it % 2][:, k * 128:(k + 1) * 128],
                                                 xq0[k][:, ch * C5:(ch + 1) * C5],
                                                 start=(k == 0), stop=(k == FT - 1))
                                    for k in range(FT):
                                        i = t.matmul(psum[4 + ch % 4][:, 0:C5],
                                                     wgu[it % 2][:, 1024 + k * 128:1024 + (k + 1) * 128],
                                                     xq0[k][:, ch * C5:(ch + 1) * C5],
                                                     start=(k == 0), stop=(k == FT - 1))
                                    s_pe.inc(i)

                        @blk.vector
                        def _(v):
                            for it in range(NIT):
                                s_ve.inc(v.tensor_tensor(colA[:, 2:3], wgs[:, it:it + 1],
                                                         scmc[:, 5:6], ALU.mult))
                                s_ve.inc(v.tensor_tensor(colA[:, 3:4], wgs[:, NIT + it:NIT + it + 1],
                                                         scmc[:, 5:6], ALU.mult))
                                v.wait_ge(s_ve.h, s_ve.n)
                                for ch in range(N5):
                                    v.wait_ge(s_ac.h, ac1 + it * N5 + ch + 1)
                                    if it >= 2:
                                        dma_wait(v, ("hsp", it - 2))
                                    s_ve.inc(v.scalar_tensor_tensor(
                                        hst[it % 2][:, ch * C5:(ch + 1) * C5],
                                        psum[4 + ch % 4][:, 0:C5], colA[:, 3:4],
                                        sgt[ch % 2][:, 0:C5], ALU.mult, ALU.mult))
                                v.wait_ge(s_ve.h, s_ve.n)
                                s_ve.inc(v.tensor_reduce(stat[:, 8 + it:9 + it], hst[it % 2][:],
                                                         AX, ALU.max, apply_absolute_value=True))

                        @blk.scalar
                        def _(sc):
                            for it in range(NIT):
                                sc.wait_ge(s_ve.h, ve1 + it * KVE + 2)
                                for ch in range(N5):
                                    sc.wait_ge(s_pe.h, pe1 + it * N5 + ch + 1)
                                    if it * N5 + ch >= 2:
                                        sc.wait_ge(s_ve.h, ve1 + (it * N5 + ch - 1) // N5 * KVE
                                                   + 2 + (it * N5 + ch - 1) % N5 + 1)
                                    s_ac.inc(sc.activation(sgt[ch % 2][:, 0:C5],
                                                           psum[ch % 4][:, 0:C5],
                                                           AF.Silu, scale=colA[:, 2:3]))

                        @blk.gpsimd
                        def _(g):
                            for it in range(NIT):
                                g.wait_ge(s_ve.h, ve1 + (it + 1) * KVE)
                                (s_sp if it % 2 == 0 else s_p2).dma(
                                    g.dma_start(S["hpre"][it], hst[it % 2][:]))

            prow.close()
            dbg_dump(0)

            with ExitStack() as ph:
                hch = sb("hch", [128, NIT * 512], BF16, ph)
                grow2 = sb("grow2", [1, 128 * NIT + 16], F32, ph)
                xq2 = sb("xqh2", [128, NIT * 512], BF16, ph)
                q32 = sb("q32", [128, 512], F32, ph)
                wdst = [sb(f"wdst{i}", [128, NIT * 128], BF16, ph) for i in range(FT)]

                with bk() as blk:
                    @blk.gpsimd
                    def _(g):
                        gate(g)
                        s_ax.dma(g.dma_start(grow2[:, 0:128 * NIT], stat[:, 8:8 + NIT]))

                    @blk.vector
                    def _(v):
                        gate(v)
                        v.wait_ge(s_ax.h, s_ax.n)
                        s_ve.inc(v.tensor_reduce(grow2[:, 128 * NIT:128 * NIT + 1],
                                                 grow2[:, 0:128 * NIT], AX, ALU.max))

                    @blk.gpsimd
                    def _(g):
                        g.wait_ge(s_ve.h, s_ve.n)
                        s_ax.dma(g.dma_start(
                            grow2[:, 128 * NIT + 8:128 * NIT + 16],
                            grow2[:, 128 * NIT:128 * NIT + 1].unsqueeze(1)
                            .broadcast_to([1, 8, 1])))
                        g.wait_ge(s_ax.h, s_ax.n)
                        s_cc.dma(g.dma_start(S["cci1"], grow2[:, 128 * NIT + 8:128 * NIT + 16]))
                        g.wait_ge(s_cc.h, s_cc.n)
                        s_cx.inc(g.collective_compute(
                            "AllReduce", ALU.max, replica_groups=[list(range(n_cores))],
                            ins=[S["cci1"].opt()], outs=[S["cco1"].opt()]))
                        g.wait_ge(s_cx.h, s_cx.n)
                        s_cc.dma(g.dma_start(grow2[:, 128 * NIT:128 * NIT + 8], S["cco1"]))

                    @blk.vector
                    def _(v):
                        v.wait_ge(s_cc.h, s_cc.n)
                        v.wait_ge(s_ve.h, s_ve.n)
                        s_ve.inc(v.tensor_tensor(grow2[:, 128 * NIT + 1:128 * NIT + 2],
                                                 grow2[:, 128 * NIT:128 * NIT + 1],
                                                 scm[:, 3:4], ALU.mult))
                        v.wait_ge(s_ve.h, s_ve.n)
                        s_ve.inc(v.tensor_scalar(grow2[:, 128 * NIT + 1:128 * NIT + 2],
                                                 grow2[:, 128 * NIT + 1:128 * NIT + 2],
                                                 1.0 / QMAX, None, ALU.mult))
                        v.wait_ge(s_ve.h, s_ve.n)
                        s_ve.inc(v.tensor_copy(scm[:, 7:8],
                                               grow2[:, 128 * NIT + 1:128 * NIT + 2]))
                        s_ve.inc(v.reciprocal(scm[:, 6:7],
                                              grow2[:, 128 * NIT + 1:128 * NIT + 2]))

                    @blk.gpsimd
                    def _(g):
                        g.wait_ge(s_ve.h, s_ve.n)
                        s_bc.dma(g.dma_start(scmc[:], bass.AP(scm, 0, [[1, 1], [0, 128], [1, 16]])))

                    @blk.vector
                    def _(v):
                        v.wait_ge(s_bc.h, s_bc.n)
                        s_ve.inc(v.tensor_tensor(stat[:, 48:48 + FT], wds[:, 0:FT],
                                                 scmc[:, 7:8].broadcast_to([128, FT]), ALU.mult))

                with bk() as blk:
                    lw0, ld0 = s_lw.n, s_ld.n
                    ve1, pe1 = s_ve.n, s_pe.n
                    KQ = 3 * NIT + FT

                    @blk.gpsimd
                    def _(g):
                        gate(g)
                        for ch in range(N5):
                            if ch >= 1:
                                g.wait_ge(s_ve.h, ve1 + ch * KQ - FT)
                            dma_mark(("hld", ch), s_gd,
                                     g.dma_start(
                                hch[:, 0:NIT * C5],
                                _dap(S["hpre"], ch * C5,
                                     [[T, 128], [128 * T, NIT], [1, C5]])))

                    @blk.sync
                    def _(sy):
                        gate(sy)
                        for dm in range(FT):
                            dma_mark(("wds", dm), s_lw if dm % 2 == 0 else s_w2,
                                     sy.dma_start(
                                wdst[dm][:, 0:NIT * 128],
                                _dap(S["wdT"], dm * 128,
                                     [[D, 128], [128 * D, NIT], [1, 128]])))

                    @blk.vector
                    def _(v):
                        gate(v)
                        for ch in range(N5):
                            dma_wait(v, ("hld", ch))
                            if ch >= 1:
                                v.wait_ge(s_pe.h, pe1 + ch * FT)
                            for it in range(NIT):
                                v.wait_ge(s_ve.h, s_ve.n)
                                s_ve.inc(v.tensor_scalar(q32[:, 0:C5],
                                                         hch[:, it * C5:(it + 1) * C5],
                                                         scmc[:, 6:7], MAGIC, ALU.mult, ALU.add))
                                v.wait_ge(s_ve.h, s_ve.n)
                                s_ve.inc(v.tensor_scalar(q32[:, 0:C5], q32[:, 0:C5], MAGIC, QMAX,
                                                         ALU.subtract, ALU.min))
                                v.wait_ge(s_ve.h, s_ve.n)
                                s_ve.inc(v.tensor_scalar(xq2[:, it * C5:(it + 1) * C5],
                                                         q32[:, 0:C5],
                                                         -QMAX - 1.0, None, ALU.max))
                            s_ve.inc(v.nop()) if False else None
                            for dm in range(FT):
                                v.wait_ge(s_pe.h, pe1 + ch * FT + dm + 1)
                                s_ve.inc(v.scalar_tensor_tensor(
                                    res[dm][:, ch * C5:(ch + 1) * C5],
                                    psum[dm % 4][:, 0:C5], stat[:, 48 + dm:49 + dm],
                                    res[dm][:, ch * C5:(ch + 1) * C5],
                                    ALU.mult, ALU.add))

                    @blk.tensor
                    def _(t):
                        for ch in range(N5):
                            t.wait_ge(s_ve.h, ve1 + ch * KQ + 3 * NIT)
                            for dm in range(FT):
                                q = ch * FT + dm
                                t.wait_ge(s_lw.h, dmk[("wds", FT - 2)][1])
                                t.wait_ge(s_w2.h, dmk[("wds", FT - 1)][1])
                                if q >= 4:
                                    qa, qb = divmod(q - 4, FT)
                                    t.wait_ge(s_ve.h, ve1 + qa * KQ + 3 * NIT + qb + 1)
                                for k in range(NIT):
                                    i = t.matmul(psum[dm % 4][:, 0:C5],
                                                 wdst[dm][:, k * 128:(k + 1) * 128],
                                                 xq2[:, k * C5:(k + 1) * C5],
                                                 start=(k == 0), stop=(k == NIT - 1))
                                s_pe.inc(i)

            dbg_dump(1)
            pbc.close()

            with ExitStack() as ph:
                HW = TC + 3
                wxT = sb("wxT", [128, NDT * 96 * 2], BF16, ph)
                wdtT = sb("wdtT", [64, NDT * 128 * 2], BF16, ph)
                with bk() as blk:
                    @blk.gpsimd
                    def _(g):
                        gate(g)
                        s_gd.dma(g.dma_start(wxT[:], S["wxs"]))
                        s_gd.dma(g.dma_start(wdtT[0:64, :], S["wdts"]))
                xzin = sb("xzin", [128, NDT * HW], BF16, ph)
                xsch = sb("xsch", [128, NDT * TC], BF16, ph)
                bbuf = sb("bbuf", [128, DS * TC], BF16, ph)
                cbuf = sb("cbuf", [128, DS * TC], BF16, ph)
                hall = sb("hall", [128, DS * TC], BF16, ph)
                dab = [sb(f"dab{i}", [128, TC], BF16, ph) for i in range(4)]
                dbb = [sb(f"dbb{i}", [128, TC], BF16, ph) for i in range(2)]
                pbf = [sb(f"pbf{i}", [128, TC], BF16, ph) for i in range(4)]
                pxd = [sb(f"pxd{i}", [128, TC], BF16, ph) for i in range(2)]
                ctp = [sb(f"ctp{i}", [128, TC], BF16, ph) for i in range(2)]
                dtch = sb("dtch", [128, TC], F32, ph)
                uch = sb("uch", [128, TC], BF16, ph)
                statev = sb("statev", [128, NDT * DS], F32, ph)
                conv32 = [sb(f"cv{i}", [128, TC], F32, ph) for i in range(1)]
                szb = [sb(f"szb{i}", [128, TC], BF16, ph) for i in range(NDT)]
                otb = [sb(f"otb{i}", [128, TC], BF16, ph) for i in range(2)]
                cmb = [sb(f"cmb{i}", [128, TC], BF16, ph) for i in range(2)]
                ofl = [sb(f"ofl{i}", [128, TC], BF16, ph) for i in range(2)]
                xdbl = sb("xdbl", [96, TC], BF16, ph)

                prev_tts_end = [None]
                prev_gp_end = [None]
                prev_pe_end = [None]

                def emit_xload(g, dirn_, ch_):
                    edge_ = (ch_ == 0)
                    if dirn_ == 0:
                        t0_ = ch_ * TC
                        lo_ = max(0, t0_ - 3)
                        nc_ = (t0_ + TC) - lo_
                    else:
                        thi_ = T - 1 - ch_ * TC
                        tlo_ = thi_ - (TC - 1)
                        lo_ = tlo_
                        nc_ = min(T, thi_ + 4) - tlo_
                    xzv = xzin[:].rearrange("p (t c) -> p t c", c=HW)
                    if dirn_ == 0:
                        off_ = 3 if edge_ else 0
                        if edge_:
                            g.memset(xzv[:, :, 0:3], 0.0)
                        i = g.dma_start(
                            xzv[:, :, off_:off_ + nc_],
                            _dap(S["xz"], lo_, [[T, 128], [128 * T, NDT], [1, nc_]]))
                    else:
                        if edge_:
                            g.memset(xzv[:, :, TC:TC + 3], 0.0)
                        i = g.dma_start(
                            xzv[:, :, 0:nc_],
                            _dap(S["xz"], lo_, [[T, 128], [128 * T, NDT], [1, nc_]]))
                    dma_mark(("xld", dirn_, ch_), s_gd, i)

                for dirn in range(2):
                    AO = dirn * NDT
                    with bk() as blk:
                        @blk.gpsimd
                        def _(g):
                            gate(g)
                            s_gp.inc(g.memset(statev[:], 0.0))
                            if dirn == 0:
                                emit_xload(g, 0, 0)

                    for ch in range(NCH):
                        if dirn == 0:
                            zlo = ch * TC
                        else:
                            zlo = T - 1 - ch * TC - (TC - 1)
                        edge = (ch == 0)

                        with bk() as blk:
                            fwd_marks(("osp", dirn, ch), [s_sp, s_p2],
                                      [s_sp.n, s_p2.n], NDT)
                            if dirn == 1:
                                fwd_marks(("ofl", dirn, ch), [s_o1, s_o2],
                                          [s_o1.n, s_o2.n], NDT)

                            ve_t0 = s_ve.n
                            ac_s0 = s_ac.n
                            pe_c0 = s_pe.n
                            gp_c0 = s_gp.n
                            zld_mark = [s_gd.n + 16]
                            cde = {}
                            _c = 0
                            for _d in range(0, NDT, 2):
                                _c += 4
                                cde[_d] = _c
                            codd = list(range(1, NDT, 2))
                            cgp = {d: 4 * (j + 1) for j, d in enumerate(codd)}
                            cpe = {d: 4 * (j + 1) for j, d in enumerate(codd)}

                            def xsl(d, k):
                                base = d * HW
                                if dirn == 0:
                                    return xzin[:, base + k:base + k + TC]
                                return xzin[:, base + 3 - k:base + 3 - k + TC][:, ::-1]

                            @blk.vector
                            def _(v):
                                gate(v)
                                dma_wait(v, ("xld", dirn, ch))
                                for d in range(0, NDT, 2):
                                    wof = (AO + d) * DC
                                    if d >= 2:
                                        v.wait_ge(s_ac.h, ac_s0 + d - 1)
                                    for k in range(DC):
                                        v.wait_ge(s_ve.h, s_ve.n)
                                        if k == 0:
                                            s_ve.inc(v.tensor_scalar(conv32[0][:], xsl(d, 0),
                                                                     convw[:, wof:wof + 1],
                                                                     convb[:, AO + d:AO + d + 1],
                                                                     ALU.mult, ALU.add))
                                        else:
                                            s_ve.inc(v.scalar_tensor_tensor(
                                                conv32[0][:], xsl(d, k),
                                                convw[:, wof + k:wof + k + 1],
                                                conv32[0][:], ALU.mult, ALU.add))

                            @blk.gpsimd
                            def _(g):
                                dma_wait(g, ("xld", dirn, ch))
                                for j, d in enumerate(codd):
                                    wof = (AO + d) * DC
                                    for k in range(DC):
                                        gi = 4 * j + k
                                        if gi >= 2:
                                            g.wait_ge(s_pe.h, pe_c0 + gi - 1)
                                        s_gp.inc(g.tensor_tensor(
                                            ctp[k % 2][:], xsl(d, k),
                                            convw[:, wof + k:wof + k + 1]
                                            .broadcast_to([128, TC]), ALU.mult))

                            @blk.scalar
                            def _(sc):
                                gate(sc)
                                for d in range(NDT):
                                    if d % 2 == 0:
                                        sc.wait_ge(s_ve.h, ve_t0 + cde[d])
                                        s_ac.inc(sc.activation(xsch[:, d * TC:(d + 1) * TC],
                                                               conv32[0][:], AF.Silu))
                                    else:
                                        sc.wait_ge(s_pe.h, pe_c0 + cpe[d])
                                        s_ac.inc(sc.activation(
                                            xsch[:, d * TC:(d + 1) * TC],
                                            psum[1 + (d // 2) % 2][:, 0:TC], AF.Silu,
                                            bias=convb[:, AO + d:AO + d + 1]))
                                sc.wait_ge(s_gd.h, zld_mark[0])
                                for d in range(NDT):
                                    zsl = xzin[:, d * HW:d * HW + TC]
                                    if dirn == 1:
                                        zsl = zsl[:, ::-1]
                                    s_ac.inc(sc.activation(szb[d][:], zsl, AF.Silu))

                            pe_wx0 = s_pe.n

                            @blk.tensor
                            def _(t):
                                gate(t)
                                for d in range(NDT):
                                    if d % 2 == 1:
                                        if d >= 5:
                                            t.wait_ge(s_ac.h, ac_s0 + d - 3)
                                        for k in range(DC):
                                            t.wait_ge(s_gp.h, gp_c0 + cgp[d] - 3 + k)
                                            s_pe.inc(t.matmul(
                                                psum[1 + (d // 2) % 2][:, 0:TC],
                                                ident_b[:], ctp[k % 2][:],
                                                start=(k == 0), stop=(k == DC - 1)))
                                    t.wait_ge(s_ac.h, ac_s0 + d + 1)
                                    i = t.matmul(psum[0][0:96, 0:TC],
                                                 wxT[:, (AO + d) * 96:(AO + d + 1) * 96],
                                                 xsch[:, d * TC:(d + 1) * TC],
                                                 start=(d == 0), stop=(d == NDT - 1))
                                s_pe.inc(i)

                            pe_wx_end = s_pe.n
                            ac_xdbl0 = s_ac.n

                            @blk.scalar
                            def _(sc):
                                sc.wait_ge(s_pe.h, pe_wx_end)
                                s_ac.inc(sc.copy(xdbl[:], psum[0][0:96, 0:TC]))

                            ld_z0 = s_gd.n
                            bc_0 = s_bc.n

                            @blk.gpsimd
                            def _(g):
                                g.wait_ge(s_ve.h, ve_t0 + cde[NDT - 2])
                                g.wait_ge(s_gp.h, gp_c0 + 4 * len(codd))
                                i = g.dma_start(
                                    xzin[:].rearrange("p (t c) -> p t c", c=HW)[:, :, 0:TC],
                                    _dap(S["xz"], NDT * 128 * T + zlo,
                                         [[T, 128], [128 * T, NDT], [1, TC]]))
                                s_gd.dma(i)
                                g.wait_ge(s_ac.h, ac_xdbl0 + 1)
                                for n in range(DS):
                                    s_bc.dma(g.dma_start(
                                        bbuf[:, n * TC:(n + 1) * TC],
                                        xdbl[64 + n:65 + n, :].unsqueeze(1)
                                        .broadcast_to([1, 128, TC])))
                                for n in range(DS):
                                    s_bc.dma(g.dma_start(
                                        cbuf[:, n * TC:(n + 1) * TC],
                                        xdbl[80 + n:81 + n, :].unsqueeze(1)
                                        .broadcast_to([1, 128, TC])))
                                if dirn == 1:
                                    for d in range(2):
                                        (s_o1 if d % 2 == 0 else s_o2).dma(
                                            g.dma_start(
                                            ofl[d % 2][:],
                                            S["outf"][d][:, zlo:zlo + TC]))

                            ac3 = s_ac.n
                            pe2 = s_pe.n
                            ve2b = s_ve.n
                            gp2b = s_gp.n
                            KA3 = 19

                            def ac_dtln(d):
                                return ac3 + d * KA3 + 2

                            def ac_p17(d):
                                return ac3 + d * KA3 + 3

                            def ac_dA(d, n):
                                return ac3 + d * KA3 + 4 + n

                            def ac_sz(d):
                                return ac_s0 + NDT + d + 1

                            def gp_p(d, n):
                                return gp2b + d * DS + n + 1

                            ppos = {}
                            _c = 0
                            for _d in range(2):
                                _c += 1
                                ppos[("dt", _d)] = _c
                            for _d in range(NDT):
                                if _d + 2 < NDT:
                                    _c += 1
                                    ppos[("dt", _d + 2)] = _c
                                _c += 1
                                ppos[("y17", _d)] = _c
                                for _n in range(DS):
                                    _c += 1
                                    ppos[("y", _d, _n)] = _c

                            def pe_dt(d):
                                return pe2 + ppos[("dt", d)]

                            def pe_y17(d):
                                return pe2 + ppos[("y17", d)]

                            def pe_y(d, n):
                                return pe2 + ppos[("y", d, n)]

                            vpos = {}
                            _c = 0
                            for _d in range(NDT):
                                _c += 1
                                vpos[("u", _d)] = _c
                                for _n in range(DS):
                                    _c += 1
                                    vpos[("dbu", _d, _n)] = _c
                                    _c += 1
                                    vpos[("tts", _d, _n)] = _c
                                    if _n == 0 and _d >= 1:
                                        _c += 1
                                        vpos[("ot", _d - 1)] = _c
                                        if dirn == 1:
                                            _c += 1
                                            vpos[("cmb", _d - 1)] = _c
                                _c += 1
                                vpos[("state", _d)] = _c
                            _c += 1
                            vpos[("ot", NDT - 1)] = _c
                            if dirn == 1:
                                _c += 1
                                vpos[("cmb", NDT - 1)] = _c

                            def ve_u(d):
                                return ve2b + vpos[("u", d)]

                            def ve_tts(d, n):
                                return ve2b + vpos[("tts", d, n)]

                            def ve_ot(d):
                                return ve2b + vpos[("ot", d)]

                            def ve_cmb(d):
                                return ve2b + vpos[("cmb", d)]

                            @blk.scalar
                            def _(sc):
                                for d in range(NDT):
                                    sc.wait_ge(s_pe.h, pe_dt(d))
                                    if d >= 1:
                                        sc.wait_ge(s_ve.h, ve_u(d - 1))
                                    s_ac.inc(sc.activation(conv32[0][:], psum[4 + d % 2][:, 0:TC],
                                                           AF.Exp,
                                                           bias=bdtc[:, AO + d:AO + d + 1]))
                                    sc.wait_ge(s_ac.h, s_ac.n)
                                    s_ac.inc(sc.activation(dtch[:], conv32[0][:], AF.Ln, bias=1.0))
                                    sc.wait_ge(s_ac.h, s_ac.n)
                                    if d >= 2:
                                        sc.wait_ge(s_pe.h, pe_y17(d - 2))
                                    s_ac.inc(sc.activation(
                                        pxd[d % 2][:], xsch[:, d * TC:(d + 1) * TC],
                                        AF.Copy, scale=dcol[:, AO + d:AO + d + 1]))
                                    if prev_tts_end[0] and d == 0:
                                        sc.wait_ge(s_ve.h, prev_tts_end[0])
                                    for n in range(DS):
                                        nn = n - 4
                                        if nn >= 0:
                                            sc.wait_ge(s_ve.h, ve_tts(d, nn))
                                        elif d >= 1:
                                            sc.wait_ge(s_ve.h, ve_tts(d - 1, DS + nn))
                                        s_ac.inc(sc.activation(
                                            dab[n % 4][:], dtch[:], AF.Exp,
                                            scale=a_cols[:, (AO + d) * DS + n:(AO + d) * DS + n + 1]))

                            @blk.tensor
                            def _(t):
                                def emit_dt(d):
                                    t.wait_ge(s_ac.h, ac_xdbl0 + 1)
                                    if d >= 2:
                                        t.wait_ge(s_ac.h, ac_dtln(d - 2))
                                    s_pe.inc(t.matmul(
                                        psum[4 + d % 2][:, 0:TC],
                                        wdtT[:, (AO + d) * 128:(AO + d + 1) * 128],
                                        xdbl[0:64, :], start=True, stop=True))

                                emit_dt(0)
                                emit_dt(1)
                                for d in range(NDT):
                                    if d + 2 < NDT:
                                        emit_dt(d + 2)
                                    if d >= 2:
                                        t.wait_ge(s_ve.h, ve_ot(d - 2))
                                    t.wait_ge(s_ac.h, ac_p17(d))
                                    s_pe.inc(t.matmul(psum[6 + d % 2][:, 0:TC],
                                                      ident_b[:], pxd[d % 2][:],
                                                      start=True, stop=False))
                                    for n in range(DS):
                                        t.wait_ge(s_gp.h, gp_p(d, n))
                                        s_pe.inc(t.matmul(psum[6 + d % 2][:, 0:TC],
                                                          ident_b[:], pbf[n % 4][:],
                                                          start=False, stop=(n == DS - 1)))

                            @blk.vector
                            def _(v):
                                v.wait_ge(s_bc.h, bc_0 + 32 * 16)

                                def emit_ot(d):
                                    v.wait_ge(s_pe.h, pe_y(d, DS - 1))
                                    v.wait_ge(s_ac.h, ac_sz(d))
                                    if dirn == 0:
                                        if d >= 2:
                                            dma_wait(v, ("osp", dirn, ch, d - 2))
                                        s_ve.inc(v.tensor_tensor(
                                            otb[d % 2][:], psum[6 + d % 2][:, 0:TC],
                                            szb[d][:], ALU.mult))
                                    else:
                                        s_ve.inc(v.tensor_tensor(
                                            otb[d % 2][:], psum[6 + d % 2][:, 0:TC][:, ::-1],
                                            szb[d][:, ::-1], ALU.mult))
                                        v.wait_ge(s_ve.h, s_ve.n)
                                        dma_wait(v, ("ofl", dirn, ch, d))
                                        if d >= 2:
                                            dma_wait(v, ("osp", dirn, ch, d - 2))
                                        s_ve.inc(v.tensor_tensor(cmb[d % 2][:], otb[d % 2][:],
                                                                 ofl[d % 2][:], ALU.add))

                                for d in range(NDT):
                                    v.wait_ge(s_ac.h, ac_dtln(d))
                                    s_ve.inc(v.tensor_tensor(uch[:], dtch[:],
                                                             xsch[:, d * TC:(d + 1) * TC],
                                                             ALU.mult))
                                    v.wait_ge(s_ve.h, s_ve.n)
                                    for n in range(DS):
                                        s_ve.inc(v.tensor_tensor(dbb[n % 2][:], uch[:],
                                                                 bbuf[:, n * TC:(n + 1) * TC],
                                                                 ALU.mult))
                                        if d >= 1:
                                            v.wait_ge(s_gp.h, gp_p(d - 1, n))
                                        elif prev_gp_end[0] is not None:
                                            v.wait_ge(s_gp.h, prev_gp_end[0][n])
                                        v.wait_ge(s_ac.h, ac_dA(d, n))
                                        v.wait_ge(s_ve.h, s_ve.n)
                                        init = (statev[:, (d * DS + n):(d * DS + n + 1)]
                                                if ch > 0 else 0.0)
                                        s_ve.inc(v.tensor_tensor_scan(
                                            hall[:, n * TC:(n + 1) * TC],
                                            dab[n % 4][:],
                                            dbb[n % 2][:], init, ALU.mult, ALU.add))
                                        if n == 0 and d >= 1:
                                            emit_ot(d - 1)
                                    v.wait_ge(s_ve.h, s_ve.n)
                                    s_ve.inc(v.tensor_copy(statev[:, d * DS:(d + 1) * DS],
                                                           hall[:, TC - 1::TC]))
                                emit_ot(NDT - 1)

                            @blk.gpsimd
                            def _(g):
                                def emit_io(e):
                                    if dirn == 0:
                                        g.wait_ge(s_ve.h, ve_ot(e))
                                        (s_sp if e % 2 == 0 else s_p2).dma(g.dma_start(
                                            S["outf"][e][:, zlo:zlo + TC], otb[e % 2][:]))
                                    else:
                                        if e + 2 < NDT:
                                            g.wait_ge(s_ve.h, ve_cmb(e))
                                            (s_o1 if e % 2 == 0 else s_o2).dma(g.dma_start(
                                                ofl[e % 2][:],
                                                S["outf"][e + 2][:, zlo:zlo + TC]))
                                        g.wait_ge(s_ve.h, ve_cmb(e))
                                        (s_sp if e % 2 == 0 else s_p2).dma(g.dma_start(
                                            S["outc"][e][:, zlo:zlo + TC], cmb[e % 2][:]))

                                nxt = ((dirn, ch + 1) if ch + 1 < NCH else
                                       ((1, 0) if dirn == 0 else None))
                                for d in range(NDT):
                                    for n in range(DS):
                                        g.wait_ge(s_ve.h, ve_tts(d, n))
                                        gi = d * DS + n
                                        if gi >= 4:
                                            pn = divmod(gi - 4, DS)
                                            g.wait_ge(s_pe.h, pe_y(*pn))
                                        elif prev_pe_end[0] is not None:
                                            g.wait_ge(s_pe.h, prev_pe_end[0])
                                        s_gp.inc(g.tensor_tensor(
                                            pbf[n % 4][:],
                                            hall[:, n * TC:(n + 1) * TC],
                                            cbuf[:, n * TC:(n + 1) * TC], ALU.mult))
                                    if d == 0 and nxt is not None:
                                        g.wait_ge(s_ac.h, ac_sz(NDT - 1))
                                        emit_xload(g, *nxt)
                                    if d >= 1:
                                        emit_io(d - 1)
                                emit_io(NDT - 1)

                            prev_tts_end[0] = ve_tts(NDT - 1, DS - 1)
                            prev_gp_end[0] = [gp_p(NDT - 1, n) for n in range(DS)]
                            prev_pe_end[0] = pe_y(NDT - 1, DS - 1)

            with ExitStack() as ph:
                ocst = sb("ocst", [128, NDT * 512], BF16, ph)
                wost = [sb(f"wost{i}", [128, NDT * 128], BF16, ph) for i in range(2)]
                with bk() as blk:
                    lw0, ld0, pe1, ve1 = s_lw.n, s_ld.n, s_pe.n, s_ve.n

                    @blk.gpsimd
                    def _(g):
                        gate(g)
                        for ch in range(N5):
                            if ch >= 1:
                                g.wait_ge(s_pe.h, pe1 + ch * FT)
                            dma_mark(("oc", ch), s_gd, g.dma_start(
                                ocst[:, 0:NDT * C5],
                                _dap(S["outc"], ch * C5,
                                     [[T, 128], [128 * T, NDT], [1, C5]])))

                    @blk.sync
                    def _(sy):
                        gate(sy)
                        for dm in range(FT):
                            if dm >= 2:
                                sy.wait_ge(s_pe.h, pe1 + dm - 1)
                            dma_mark(("wo", dm), s_lw if dm % 2 == 0 else s_w2,
                                     sy.dma_start(
                                wost[dm % 2][:, 0:NDT * 128],
                                _dap(S["woutT"], dm * 128,
                                     [[D, 128], [128 * D, NDT], [1, 128]])))

                    @blk.tensor
                    def _(t):
                        for ch in range(N5):
                            dma_wait(t, ("oc", ch))
                            for dm in range(FT):
                                q = ch * FT + dm
                                dma_wait(t, ("wo", dm))
                                if q >= 4:
                                    t.wait_ge(s_ve.h, ve1 + q - 3)
                                for k in range(NDT):
                                    i = t.matmul(psum[dm % 4][:, 0:C5],
                                                 wost[dm % 2][:, k * 128:(k + 1) * 128],
                                                 ocst[:, k * C5:(k + 1) * C5],
                                                 start=(k == 0), stop=(k == NDT - 1))
                                s_pe.inc(i)

                    @blk.vector
                    def _(v):
                        for ch in range(N5):
                            for dm in range(FT):
                                q = ch * FT + dm
                                v.wait_ge(s_pe.h, pe1 + q + 1)
                                s_ve.inc(v.tensor_tensor(
                                    res[dm][:, ch * C5:(ch + 1) * C5],
                                    res[dm][:, ch * C5:(ch + 1) * C5],
                                    psum[dm % 4][:, 0:C5], ALU.add))

            dbg_dump(2)

        with ExitStack() as pl1:
            rowm1 = sb("rowm1", [1, T], F32, pl1)
            rowr1 = sb("rowr1", [1, T], F32, pl1)
            rowt1 = sb("rowt1", [1, 2048], F32, pl1)
            xn1 = [sb(f"xn1_{i}", [128, T], BF16, pl1) for i in range(FT)]
            bc_m1 = sb("bc_m1", [128, T], F32, pl1)
            bc_r1 = sb("bc_r1", [128, T], F32, pl1)
            with ExitStack() as ph:
                sqt = sb("sqt1", [128, T], BF16, ph)
                big32 = [sb(f"bgb{i}", [128, T], F32, ph) for i in range(1)]
                grow = sb("grow1", [1, 128 * FT], F32, ph)
                marks1 = ln_stats_scale(xn1, sqt, big32, bc_m1, bc_r1, grow,
                                        rowm1, rowr1, rowt1, 2, None, 8, 9, finish=False)
                ln_cc_start(marks1, 2)

            with ExitStack() as ph:
                qkv = [sb(f"qkv{i}", [128, T], BF16, ph) for i in range(9)]
                kvT = sb("kvT", [128, NKT * 65], BF16, ph)
                pwt = ExitStack()
                watt = [sb(f"watt{k}", [128, D + 2 * HD], BF16, pwt) for k in range(FT)]

                with bk() as blk:
                    lw0, pe1, ac1 = s_lw.n, s_pe.n, s_ac.n
                    NJ = 9 * N5

                    @blk.sync
                    def _(sy):
                        gate(sy)
                        for k in range(FT):
                            s_lw.dma(sy.dma_start(watt[k][:], S["wattnT"][k]))

                    @blk.tensor
                    def _(t):
                        gate(t)
                        t.wait_ge(s_lw.h, s_lw.n)
                        for jt in range(9):
                            for ch in range(N5):
                                q = jt * N5 + ch
                                if q >= 4:
                                    t.wait_ge(s_ac.h, ac1 + q - 3)
                                for k in range(FT):
                                    i = t.matmul(psum[q % 4][:, 0:C5],
                                                 watt[k][:, jt * 128:(jt + 1) * 128],
                                                 xn1[k][:, ch * C5:(ch + 1) * C5],
                                                 start=(k == 0), stop=(k == FT - 1))
                                s_pe.inc(i)

                    @blk.scalar
                    def _(sc):
                        for jt in range(9):
                            for ch in range(N5):
                                q = jt * N5 + ch
                                sc.wait_ge(s_pe.h, pe1 + q + 1)
                                s_ac.inc(sc.copy(qkv[jt][:, ch * C5:(ch + 1) * C5],
                                                 psum[q % 4][:, 0:C5]))

                    @blk.gpsimd
                    def _(g):
                        gate(g)
                        s_gp.inc(g.memset(kvT[:], 1.0))

                    pe2 = s_pe.n
                    ve2 = s_ve.n
                    @blk.tensor
                    def _(t):
                        t.wait_ge(s_ac.h, s_ac.n)
                        t.wait_ge(s_gp.h, s_gp.n)
                        for kt in range(NKT):
                            if kt >= 2:
                                t.wait_ge(s_ve.h, ve2 + kt - 1)
                            s_pe.inc(t.transpose(psb(4 + kt % 2, 128, 64),
                                                 qkv[8][64:128, kt * 128:(kt + 1) * 128],
                                                 ident_b[64:128, 64:128]))

                    @blk.vector
                    def _(v):
                        for kt in range(NKT):
                            v.wait_ge(s_pe.h, pe2 + kt + 1)
                            s_ve.inc(v.tensor_copy(kvT[:, kt * 65:kt * 65 + 64],
                                                   psb(4 + kt % 2, 128, 64)))

                pwt.close()
                with ExitStack() as ph2:
                    ptile = [sb(f"pt{i}", [128, 512], BF16, ph2) for i in range(4)]
                    arow = sb("arow", [64, 512], F32, ph2)
                    kk = sb("kk", [128, T], BF16, ph2)
                    atto = [sb(f"atto{i}", [64, 512], BF16, ph2) for i in range(2)]
                    dmask = sb("dmask", [128, 512 * 4], BF16, ph2)
                    growf = [sb(f"growf{i}", [1, 512], F32, ph2) for i in range(2)]
                    onesf = sb("onesf", [1, 64], F32, ph2)

                    seq = []
                    hq = []
                    for h in range(NH):
                        for qc in range(T // 512):
                            nkt = (qc + 1) * 512 // 128
                            hq.append((h, qc, nkt))
                            for kt in range(nkt):
                                seq.append((h, qc, kt, kt * 128 >= qc * 512))

                    with bk() as blk:
                        @blk.gpsimd
                        def _(g):
                            gate(g)
                            s_ax.dma(g.dma_start(kk[64:128, :], qkv[8][0:64, :]))
                            g.memset(onesf[:], 1.0)
                            last = None
                            for j, off in enumerate((0, 128, 256, 384)):
                                last = g.affine_select(
                                    dmask[:, j * 512:j * 512 + 512],
                                    onesq_b[:, 0:1].broadcast_to([128, 512]),
                                    [[1, 512]], ALU.is_ge, 0.0, base=-off,
                                    channel_multiplier=-1)
                            s_gp.inc(last)

                    with bk() as blk:
                        pe3, ac3, ve3, bc3, gp3 = s_pe.n, s_ac.n, s_ve.n, s_bc.n, s_gp.n
                        axc = [s_ax.n, s_g2.n]
                        odd_idx = {}
                        _c = 0
                        for _ih in range(len(hq)):
                            if hq[_ih][0] % 2 == 1:
                                odd_idx[_ih] = _c
                                p = _c % 2
                                axc[p] += 16
                                dmk[("amv", _ih)] = ((s_ax if p == 0 else s_g2).h, axc[p])
                                _c += 1
                        odd_list = sorted(odd_idx, key=lambda k: odd_idx[k])
                        n_diag_before = []
                        nd = 0
                        for (h, qc, kt, diag) in seq:
                            n_diag_before.append(nd)
                            if diag:
                                nd += 1

                        hq_of_tile = []
                        for i_hq, (h, qc, nkt) in enumerate(hq):
                            for kt in range(nkt):
                                hq_of_tile.append(i_hq)
                        last_tile_of = {}
                        for i, ihq in enumerate(hq_of_tile):
                            last_tile_of[ihq] = i

                        acpos = {}
                        _c = 0
                        for i in range(len(seq)):
                            _c += 1
                            acpos[("exp", i)] = _c
                            if i == last_tile_of[hq_of_tile[i]]:
                                _c += 1
                                acpos[("den", hq_of_tile[i])] = _c
                        pe_ops = []
                        for i in range(len(seq)):
                            pe_ops.append(("sc", i))
                            j = i - 2
                            if j >= 0:
                                pe_ops.append(("pv", j))
                                if j == last_tile_of[hq_of_tile[j]]:
                                    pe_ops.append(("bc", hq_of_tile[j]))
                        for j in (len(seq) - 2, len(seq) - 1):
                            pe_ops.append(("pv", j))
                            if j == last_tile_of[hq_of_tile[j]]:
                                pe_ops.append(("bc", hq_of_tile[j]))
                        pepos = {}
                        for _c, op in enumerate(pe_ops):
                            pepos[op] = _c + 1
                        vepos = {}
                        _c = 0
                        for i, (h, qc, kt, diag) in enumerate(seq):
                            if diag:
                                _c += 1
                                vepos[("mask", i)] = _c
                            if i == last_tile_of[hq_of_tile[i]]:
                                _c += 1
                                vepos[("rc", hq_of_tile[i])] = _c
                                _c += 1
                                vepos[("mul", hq_of_tile[i])] = _c

                        def ve_evac_cnt(i_hq):
                            return ve3 + vepos[("mul", i_hq)]

                        @blk.tensor
                        def _(t):
                            gate(t)
                            for op in pe_ops:
                                kind, i = op
                                if kind == "bc":
                                    t.wait_ge(s_ac.h, ac3 + acpos[("den", i)])
                                    if i >= 2:
                                        t.wait_ge(s_ve.h, ve3 + vepos[("rc", i - 2)])
                                    s_pe.inc(t.matmul(
                                        psum[4 + i % 2][0:64, 0:512],
                                        onesf[:], growf[i % 2][:],
                                        start=True, stop=True))
                                    continue
                                h, qc, kt, diag = seq[i]
                                jq, rq = h // 2, (h % 2) * 64
                                ihq = hq_of_tile[i]
                                if kind == "sc":
                                    if i >= 4:
                                        t.wait_ge(s_ac.h, ac3 + acpos[("exp", i - 4)])
                                    ksrc = (qkv[8][0:64, kt * 128:(kt + 1) * 128]
                                            if rq == 0
                                            else kk[64:128, kt * 128:(kt + 1) * 128])
                                    t.wait_ge(s_ax.h, s_ax.n)
                                    s_pe.inc(t.matmul(
                                        psum[i % 4][:, 0:512],
                                        ksrc,
                                        qkv[jq][rq:rq + 64, qc * 512:(qc + 1) * 512],
                                        start=True, stop=True))
                                else:
                                    if diag:
                                        t.wait_ge(s_ve.h, ve3 + vepos[("mask", i)])
                                    else:
                                        t.wait_ge(s_ac.h, ac3 + acpos[("exp", i)])
                                    if kt == 0 and i >= 1:
                                        prev = ihq - 2
                                        if prev >= 0:
                                            t.wait_ge(s_ve.h, ve_evac_cnt(prev))
                                    s_pe.inc(t.matmul(
                                        psum[6 + ihq % 2][0:65, 0:512],
                                        kvT[:, kt * 65:(kt + 1) * 65],
                                        ptile[i % 4][:],
                                        start=(kt == 0), stop=(kt == hq[ihq][2] - 1)))

                        @blk.scalar
                        def _(sc):
                            for i, (h, qc, kt, diag) in enumerate(seq):
                                ihq = hq_of_tile[i]
                                sc.wait_ge(s_pe.h, pe3 + pepos[("sc", i)])
                                if i >= 4:
                                    sc.wait_ge(s_pe.h, pe3 + pepos[("pv", i - 4)])
                                s_ac.inc(sc.activation(ptile[i % 4][:],
                                                       psum[i % 4][:, 0:512],
                                                       AF.Exp, scale=0.125))
                                if i == last_tile_of[ihq]:
                                    sc.wait_ge(s_pe.h, pe3 + pepos[("pv", i)])
                                    if ihq >= 2:
                                        sc.wait_ge(s_pe.h, pe3 + pepos[("bc", ihq - 2)])
                                    s_ac.inc(sc.activation(growf[ihq % 2][:],
                                                           psum[6 + ihq % 2][64:65, 0:512],
                                                           AF.Copy))

                        @blk.vector
                        def _(v):
                            for i, (h, qc, kt, diag) in enumerate(seq):
                                ihq = hq_of_tile[i]
                                if diag:
                                    v.wait_ge(s_ac.h, ac3 + acpos[("exp", i)])
                                    moff = (kt * 128 - qc * 512) // 128
                                    s_ve.inc(v.tensor_tensor(
                                        ptile[i % 4][:], ptile[i % 4][:],
                                        dmask[:, moff * 512:(moff + 1) * 512], ALU.mult))
                                if i == last_tile_of[ihq]:
                                    h_, qc_ = hq[ihq][0], hq[ihq][1]
                                    jq, rq = h_ // 2, (h_ % 2) * 64
                                    v.wait_ge(s_pe.h, pe3 + pepos[("bc", ihq)])
                                    s_ve.inc(v.reciprocal(arow[:],
                                                          psum[4 + ihq % 2][0:64, 0:512]))
                                    v.wait_ge(s_ve.h, s_ve.n)
                                    if rq == 0:
                                        s_ve.inc(v.tensor_tensor(
                                            xn1[jq][0:64, qc_ * 512:(qc_ + 1) * 512],
                                            psum[6 + ihq % 2][0:64, 0:512], arow[:],
                                            ALU.mult))
                                    else:
                                        oi = odd_idx[ihq]
                                        if oi >= 2:
                                            dma_wait(v, ("amv", odd_list[oi - 2]))
                                        s_ve.inc(v.tensor_tensor(
                                            atto[oi % 2][:],
                                            psum[6 + ihq % 2][0:64, 0:512], arow[:],
                                            ALU.mult))

                        @blk.gpsimd
                        def _(g):
                            for i_hq in odd_list:
                                oi = odd_idx[i_hq]
                                h_, qc_ = hq[i_hq][0], hq[i_hq][1]
                                g.wait_ge(s_ve.h, ve_evac_cnt(i_hq))
                                ((s_ax if oi % 2 == 0 else s_g2)).dma(
                                    g.dma_start(
                                    xn1[h_ // 2][64:128, qc_ * 512:(qc_ + 1) * 512],
                                    atto[oi % 2][:]))

            ln_scale_calc(marks1, None, 8, 9)

            xq1 = [sb(f"xq1_{i}", [128, T], BF16, pl1) for i in range(FT)]
            with ExitStack() as ph:
                big32 = [sb(f"bgq1{i}", [128, T], F32, ph) for i in range(1)]
                ln_quant_sweep(xq1, big32, bc_m1, bc_r1, sinv_col=scmc[:, 8:9])

            with ExitStack() as ph2:
                wprj = [sb(f"wprj{k}", [128, D], BF16, ph2) for k in range(FT)]
                with bk() as blk:
                    pe1, ve1 = s_pe.n, s_ve.n

                    @blk.sync
                    def _(sy):
                        gate(sy)
                        for k in range(FT):
                            s_lw.dma(sy.dma_start(wprj[k][:], S["wprojT"][k]))

                    @blk.tensor
                    def _(t):
                        gate(t)
                        t.wait_ge(s_lw.h, s_lw.n)
                        for dm in range(FT):
                            for ch in range(N5):
                                q = dm * N5 + ch
                                if q >= 4:
                                    t.wait_ge(s_ve.h, ve1 + q - 3)
                                for k in range(FT):
                                    i = t.matmul(psum[q % 4][:, 0:C5],
                                                 wprj[k][:, dm * 128:(dm + 1) * 128],
                                                 xn1[k][:, ch * C5:(ch + 1) * C5],
                                                 start=(k == 0), stop=(k == FT - 1))
                                s_pe.inc(i)

                    @blk.vector
                    def _(v):
                        for dm in range(FT):
                            for ch in range(N5):
                                q = dm * N5 + ch
                                v.wait_ge(s_pe.h, pe1 + q + 1)
                                s_ve.inc(v.tensor_tensor(
                                    res[dm][:, ch * C5:(ch + 1) * C5],
                                    res[dm][:, ch * C5:(ch + 1) * C5],
                                    psum[q % 4][:, 0:C5], ALU.add))


            dbg_dump(3)

            with ExitStack() as ph:
                fwt = [sb(f"fwt{k}", [128, D], BF16, ph) for k in range(FT)]
                with bk() as blk:
                    pe1, ve1 = s_pe.n, s_ve.n

                    @blk.sync
                    def _(sy):
                        gate(sy)
                        for k in range(FT):
                            s_lw.dma(sy.dma_start(fwt[k][:], S["fwT"][k]))

                    @blk.vector
                    def _(v):
                        gate(v)
                        s_ve.inc(v.tensor_tensor(stat[:, 56:56 + FT], fsc[:, 0:FT],
                                                 scmc[:, 9:10].broadcast_to([128, FT]),
                                                 ALU.mult))
                        for dm in range(FT):
                            s_ve.inc(v.tensor_scalar(res[dm][:], res[dm][:],
                                                     fbc[:, dm:dm + 1], None, ALU.add))

                    ve2b = s_ve.n
                    @blk.tensor
                    def _(t):
                        gate(t)
                        t.wait_ge(s_lw.h, s_lw.n)
                        t.wait_ge(s_ve.h, s_ve.n)
                        for dm in range(FT):
                            for ch in range(N5):
                                q = dm * N5 + ch
                                if q >= 4:
                                    t.wait_ge(s_ve.h, ve2b + q - 3)
                                for k in range(FT):
                                    i = t.matmul(psum[q % 4][:, 0:C5],
                                                 fwt[k][:, dm * 128:(dm + 1) * 128],
                                                 xq1[k][:, ch * C5:(ch + 1) * C5],
                                                 start=(k == 0), stop=(k == FT - 1))
                                s_pe.inc(i)

                    @blk.vector
                    def _(v):
                        for dm in range(FT):
                            for ch in range(N5):
                                q = dm * N5 + ch
                                v.wait_ge(s_pe.h, pe1 + q + 1)
                                s_ve.inc(v.scalar_tensor_tensor(
                                    res[dm][:, ch * C5:(ch + 1) * C5],
                                    psum[q % 4][:, 0:C5], stat[:, 56 + dm:57 + dm],
                                    res[dm][:, ch * C5:(ch + 1) * C5],
                                    ALU.mult, ALU.add))

        dbg_dump(4)

        with ExitStack() as ph:
            ostg = [sb(f"ostg{i}", [128, D], F32, ph) for i in range(2)]
            with bk() as blk:
                pe1, ac1, st1 = s_pe.n, s_ac.n, s_st.n
                fwd_marks(("ost",), [s_st, s_s2], [s_st.n, s_s2.n], NTT)

                @blk.tensor
                def _(t):
                    gate(t)
                    for i in range(NTT):
                        for j in range(FT):
                            q = i * FT + j
                            if q >= 4:
                                t.wait_ge(s_ac.h, ac1 + q - 3)
                            s_pe.inc(t.transpose(psum[q % 4][:, 0:128],
                                                 res[j][:, i * 128:(i + 1) * 128],
                                                 ident_f[:]))

                @blk.scalar
                def _(sc):
                    for i in range(NTT):
                        if i >= 2:
                            dma_wait(sc, ("ost", i - 2))
                        for j in range(FT):
                            q = i * FT + j
                            sc.wait_ge(s_pe.h, pe1 + q + 1)
                            s_ac.inc(sc.copy(ostg[i % 2][:, j * 128:(j + 1) * 128],
                                             psum[q % 4][:, 0:128]))

                @blk.sync
                def _(sy):
                    for i in range(NTT):
                        sy.wait_ge(s_ac.h, ac1 + (i + 1) * FT)
                        (s_st if i % 2 == 0 else s_s2).dma(
                            sy.dma_start(out_h[i * 128:(i + 1) * 128, :],
                                         ostg[i % 2][:]))
                    sy.wait_ge(s_st.h, s_st.n)
                    sy.wait_ge(s_s2.h, s_s2.n)

        nc.all_engine_barrier()
        for s in sems.values():
            nc.gpsimd.sem_clear(s.h)
        nc.all_engine_barrier()

    return nc


_NAME_MAP = {
    "L0_attn_W_in": "i_W_in", "L0_attn_conv_w": "i_conv_w", "L0_attn_conv_b": "i_conv_b",
    "L0_attn_Wx": "i_Wx", "L0_attn_Wdt": "i_Wdt", "L0_attn_bdt": "i_bdt",
    "L0_attn_A_log": "i_A_log", "L0_attn_D": "i_Dd",
    "L0_attn_conv_w_b": "i_conv_w_b", "L0_attn_conv_b_b": "i_conv_b_b",
    "L0_attn_Wx_b": "i_Wx_b", "L0_attn_Wdt_b": "i_Wdt_b", "L0_attn_bdt_b": "i_bdt_b",
    "L0_attn_A_b_log": "i_A_b_log", "L0_attn_D_b": "i_Dd_b",
    "L0_attn_W_out": "i_W_out",
    "L0_ffn_clip_x": "i_clip_x", "L0_ffn_clip_down": "i_clip_down",
    "L0_ffn_w_gate": "i_w_gate", "L0_ffn_w_up": "i_w_up", "L0_ffn_w_down": "i_w_down",
    "L1_attn_w_attn": "i_w_attn", "L1_attn_w_proj": "i_w_proj",
    "L1_ffn_weight_int8": "i_fw_int8", "L1_ffn_weight_scale": "i_fw_scale",
    "L1_ffn_bias": "i_fbias",
}

_NC_CACHE = {}


def kernel(**inputs):
    B = inputs["L0_attn_hidden"].shape[0]
    T = inputs["L0_attn_hidden"].shape[1]
    if T not in _NC_CACHE:
        _NC_CACHE[T] = build_nc(T=T, n_cores=B)
    nc = _NC_CACHE[T]
    shared = {}
    for k, v in inputs.items():
        if k == "L0_attn_hidden":
            continue
        a = np.ascontiguousarray(v)
        if k == "L0_attn_conv_w" or k == "L0_attn_conv_w_b":
            pass
        shared[_NAME_MAP[k]] = a
    in_maps = []
    for b in range(B):
        m = dict(shared)
        m["x"] = np.ascontiguousarray(inputs["L0_attn_hidden"][b])
        in_maps.append(m)
    r = run_bass_kernel_spmd(nc, in_maps, core_ids=list(range(B)))
    return np.stack([r.results[b]["out"] for b in range(B)], axis=0)


# version 18
# speedup vs baseline: 1.1659x; 1.1659x over previous
import numpy as np
from contextlib import ExitStack

import concourse.bass as bass
import concourse.mybir as mybir
from concourse.bass_utils import run_bass_kernel_spmd

F32 = mybir.dt.float32
BF16 = mybir.dt.bfloat16
I8 = mybir.dt.int8
ALU = mybir.AluOpType
AF = mybir.ActivationFunctionType
AX = mybir.AxisListType.X

MAGIC = 12582912.0
QMAX = 127.0

D, DI, DS, DTR, DC, IF, NH, HD = 1024, 2048, 16, 64, 4, 2816, 16, 64
FT, NDT, NIT = D // 128, DI // 128, IF // 128



def _dap(src, offset, dims):
    return bass.AP(src.tensor, src.offset + offset, [list(d) for d in dims])


def build_nc(T=2048, n_cores=8, dbg=False):
    TC = min(512, T)
    NCH = T // TC
    NTT = T // 128
    NKT = T // 128
    N5 = T // 512 if T >= 512 else 1
    C5 = min(512, T)

    nc = bass.Bass("TRN2", target_bir_lowering=False, debug=False)

    x_h = nc.dram_tensor("x", [T, D], F32, kind="ExternalInput").ap()
    out_h = nc.dram_tensor("out", [T, D], F32, kind="ExternalOutput").ap()
    W = {}
    for name, shape, dt in [
        ("W_in", [2 * DI, D], F32), ("conv_w", [DI, 1, DC], F32), ("conv_b", [DI], F32),
        ("Wx", [DTR + 2 * DS, DI], F32), ("Wdt", [DI, DTR], F32), ("bdt", [DI], F32),
        ("A_log", [DI, DS], F32), ("Dd", [DI], F32),
        ("conv_w_b", [DI, 1, DC], F32), ("conv_b_b", [DI], F32),
        ("Wx_b", [DTR + 2 * DS, DI], F32), ("Wdt_b", [DI, DTR], F32), ("bdt_b", [DI], F32),
        ("A_b_log", [DI, DS], F32), ("Dd_b", [DI], F32),
        ("W_out", [D, DI], F32),
        ("clip_x", [1], F32), ("clip_down", [1], F32),
        ("w_gate", [IF, D], F32), ("w_up", [IF, D], F32), ("w_down", [D, IF], F32),
        ("w_attn", [D, D + 2 * HD], F32), ("w_proj", [D, D], F32),
        ("fw_int8", [D, D], I8), ("fw_scale", [D, 1], F32), ("fbias", [D], F32),
    ]:
        W[name] = nc.dram_tensor("i_" + name, shape, dt, kind="ExternalInput").ap()

    S = {
        "WinT": nc.dram_tensor("s_WinT", [FT, 128, 2 * DI], BF16).ap(),
        "wgT": nc.dram_tensor("s_wgT", [FT, 128, IF], BF16).ap(),
        "wuT": nc.dram_tensor("s_wuT", [FT, 128, IF], BF16).ap(),
        "wdT": nc.dram_tensor("s_wdT", [NIT, 128, D], BF16).ap(),
        "woutT": nc.dram_tensor("s_woutT", [NDT, 128, D], BF16).ap(),
        "wattnT": nc.dram_tensor("s_wattnT", [FT, 128, D + 2 * HD], BF16).ap(),
        "wprojT": nc.dram_tensor("s_wprojT", [FT, 128, D], BF16).ap(),
        "fwT": nc.dram_tensor("s_fwT", [FT, 128, D], BF16).ap(),
        "wxs": nc.dram_tensor("s_wxs", [128, NDT * 96 * 2], BF16).ap(),
        "wdts": nc.dram_tensor("s_wdts", [64, NDT * 128 * 2], BF16).ap(),
        "xz": nc.dram_tensor("s_xz", [2 * NDT, 128, T], BF16).ap(),
        "hpre": nc.dram_tensor("s_hpre", [NIT, 128, T], BF16).ap(),
        "outf": nc.dram_tensor("s_outf", [NDT, 128, T], BF16).ap(),
        "outc": nc.dram_tensor("s_outc", [NDT, 128, T], BF16).ap(),
    }
    for i in range(3):
        S[f"cci{i}"] = nc.dram_tensor(f"s_cci{i}", [1, 8], F32).ap()
        S[f"cco{i}"] = nc.dram_tensor(f"s_cco{i}", [1, 8], F32).ap()
    if dbg:
        for i in range(8):
            S[f"dbg{i}"] = nc.dram_tensor(f"s_dbg{i}", [FT, 128, T], F32).ap()

    ctx = ExitStack()
    sems = {}

    class Sem:
        def __init__(self, name):
            self.h = ctx.enter_context(nc.semaphore(name))
            self.n = 0

        def inc(self, inst, k=1):
            inst.then_inc(self.h, k)
            self.n += k
            return inst

        def dma(self, inst):
            return self.inc(inst, 16)

    def sem(name):
        if name not in sems:
            sems[name] = Sem(name)
        return sems[name]

    s_ld = sem("ld"); s_lw = sem("lw"); s_pe = sem("pe"); s_ve = sem("ve")
    s_ac = sem("ac"); s_gp = sem("gp"); s_sp = sem("sp"); s_wp = sem("wp")
    s_bc = sem("bc"); s_cc = sem("cc"); s_st = sem("st"); s_ax = sem("ax")
    s_l2 = sem("l2"); s_w2 = sem("w2"); s_p2 = sem("p2"); s_s2 = sem("s2")
    s_cx = sem("cx"); s_gd = sem("gd"); s_g2 = sem("g2")
    s_o1 = sem("o1"); s_o2 = sem("o2")

    dmk = {}

    def dma_mark(key, semx, inst):
        semx.dma(inst)
        dmk[key] = (semx.h, semx.n)

    def dma_wait(eng, key):
        h, n = dmk[key]
        eng.wait_ge(h, n)

    def fwd_marks(key, sems2, base, count, k=16):
        cnt = list(base)
        for i in range(count):
            p = i % 2
            cnt[p] += k
            dmk[key + (i,)] = (sems2[p].h, cnt[p])

    _snap = [[]]

    def bk():
        _snap[0] = [(s.h, s.n) for s in sems.values()
                    if s.n > 0 and s is not s_cx]
        return nc.Block()

    def gate(*engines):
        for eng in engines:
            for h, n in _snap[0]:
                eng.wait_ge(h, n)

    def sb(name, shape, dt, st=None):
        return (st or ctx).enter_context(nc.sbuf_tensor(name, list(shape), dt))

    with ctx:
        res = [sb(f"res{i}", [128, T], F32) for i in range(FT)]
        stat = sb("stat", [128, 64], F32)
        ident_b = sb("ident_b", [128, 128], BF16)
        ident_f = sb("ident_f", [128, 128], F32)
        onesq_b = sb("onesq_b", [128, 128], BF16)
        onecol_b = sb("onecol_b", [128, 1], BF16)
        scm = sb("scm", [1, 16], F32)
        colA = sb("colA", [128, 8], F32)
        a_cols = sb("a_cols", [128, NDT * DS * 2], F32)
        convw = sb("convw", [128, NDT * DC * 2], F32)
        convb = sb("convb", [128, NDT * 2], F32)
        dcol = sb("dcol", [128, NDT * 2], F32)
        bdtc = sb("bdtc", [128, NDT * 2], F32)
        wgs = sb("wgs", [128, 2 * NIT], F32)
        wds = sb("wds", [128, FT], F32)
        fsc = sb("fsc", [128, FT], F32)
        fbc = sb("fbc", [128, FT], F32)
        scmc = sb("scmc", [128, 16], F32)
        psum = [ctx.enter_context(nc.psum_tensor(f"ps{i}", [128, 512], F32))
                for i in range(8)]

        def psb(i, r, w):
            return psum[i][:].bitcast(BF16)[0:r, 0:w]

        def dbg_dump(idx):
            if not dbg:
                return
            with bk() as blk:
                @blk.gpsimd
                def _(g):
                    gate(g)
                    for j in range(FT):
                        s_gd.dma(g.dma_start(S[f"dbg{idx}"][j], res[j][:]))
                    g.wait_ge(s_gd.h, s_gd.n)

        for s in sems.values():
            nc.gpsimd.sem_clear(s.h)
        nc.all_engine_barrier()

        with bk() as blk:
            @blk.gpsimd
            def _(g):
                s_gp.inc(g.memset(scm[:], 0.0))
                s_gp.inc(g.memset(stat[:], 0.0))
                s_gp.inc(g.memset(onesq_b[:], 1.0))
                s_gp.inc(g.memset(onecol_b[:], 1.0))
                s_gp.inc(g.memset(ident_b[:], 0.0))
                s_gp.inc(g.memset(ident_f[:], 0.0))
                g.wait_ge(s_gp.h, s_gp.n)
                s_gp.inc(g.affine_select(ident_b[:], onesq_b[:], [[1, 128]], ALU.is_equal,
                                         0.0, base=0, channel_multiplier=-1))
                s_gp.inc(g.affine_select(ident_f[:], onesq_b[:], [[1, 128]], ALU.is_equal,
                                         0.0, base=0, channel_multiplier=-1))

            @blk.sync
            def _(sy):
                sy.wait_ge(s_gp.h, 2)
                for dirn in range(2):
                    cw = W["conv_w"] if dirn == 0 else W["conv_w_b"]
                    cb = W["conv_b"] if dirn == 0 else W["conv_b_b"]
                    al = W["A_log"] if dirn == 0 else W["A_b_log"]
                    dd = W["Dd"] if dirn == 0 else W["Dd_b"]
                    bd = W["bdt"] if dirn == 0 else W["bdt_b"]
                    o = dirn * NDT
                    s_lw.dma(sy.dma_start(convw[:, o * DC:(o + NDT) * DC],
                                          _dap(cw, 0, [[4, 128], [512, NDT], [1, 4]])))
                    s_lw.dma(sy.dma_start(convb[:, o:o + NDT],
                                          _dap(cb, 0, [[1, 128], [128, NDT]]),
                                          allow_slow_non_contiguous=True))
                    s_lw.dma(sy.dma_start(a_cols[:, o * DS:(o + NDT) * DS],
                                          _dap(al, 0, [[DS, 128], [128 * DS, NDT], [1, DS]])))
                    s_lw.dma(sy.dma_start(dcol[:, o:o + NDT],
                                          _dap(dd, 0, [[1, 128], [128, NDT]]),
                                          allow_slow_non_contiguous=True))
                    s_lw.dma(sy.dma_start(bdtc[:, o:o + NDT],
                                          _dap(bd, 0, [[1, 128], [128, NDT]]),
                                          allow_slow_non_contiguous=True))
                s_lw.dma(sy.dma_start(scm[:, 0:1], W["clip_x"].unsqueeze(0)))
                s_lw.dma(sy.dma_start(scm[:, 1:2], W["clip_down"].unsqueeze(0)))
                s_lw.dma(sy.dma_start(fsc[:], _dap(W["fw_scale"], 0, [[1, 128], [128, FT]]),
                                      allow_slow_non_contiguous=True))
                s_lw.dma(sy.dma_start(fbc[:], _dap(W["fbias"], 0, [[1, 128], [128, FT]]),
                                      allow_slow_non_contiguous=True))

        with bk() as blk:
            @blk.scalar
            def _(sc):
                gate(sc)
                s_ac.inc(sc.activation(a_cols[:], a_cols[:], AF.Exp))
                s_ac.inc(sc.activation(scm[:, 2:4], scm[:, 0:2], AF.Exp, scale=-1.0))

            @blk.vector
            def _(v):
                gate(v)
                v.wait_ge(s_ac.h, s_ac.n)
                s_ve.inc(v.tensor_scalar(a_cols[:], a_cols[:], -1.0, None, ALU.mult))
                s_ve.inc(v.tensor_scalar(scm[:, 2:4], scm[:, 2:4], 1.0, None, ALU.add))
                v.wait_ge(s_ve.h, s_ve.n)
                s_ve.inc(v.reciprocal(scm[:, 2:4], scm[:, 2:4]))

        with ExitStack() as ph:
            xstg = [sb(f"xstg{i}", [128, D], F32, ph) for i in range(2)]
            with bk() as blk:
                ld0, ve0, pe0 = s_ld.n, s_ve.n, s_pe.n

                @blk.sync
                def _(sy):
                    for i in range(NTT):
                        if i >= 2:
                            sy.wait_ge(s_pe.h, pe0 + (i - 1) * FT)
                        dma_mark(("x", i), s_ld if i % 2 == 0 else s_l2,
                                 sy.dma_start(xstg[i % 2][:], x_h[i * 128:(i + 1) * 128, :]))

                @blk.tensor
                def _(t):
                    t.wait_ge(s_gp.h, 8)
                    for i in range(NTT):
                        dma_wait(t, ("x", i))
                        for j in range(FT):
                            q = i * FT + j
                            if q >= 4:
                                t.wait_ge(s_ve.h, ve0 + q - 3)
                            s_pe.inc(t.transpose(psum[q % 4][:, 0:128],
                                                 xstg[i % 2][:, j * 128:(j + 1) * 128],
                                                 ident_f[:]))

                @blk.vector
                def _(v):
                    for i in range(NTT):
                        for j in range(FT):
                            q = i * FT + j
                            v.wait_ge(s_pe.h, pe0 + q + 1)
                            s_ve.inc(v.tensor_copy(res[j][:, i * 128:(i + 1) * 128],
                                                   psum[q % 4][:, 0:128]))

        with ExitStack() as ph:
            wld = [sb(f"wld{i}", [128, 3072], F32, ph) for i in range(2)]
            wstg = [sb(f"wstg{i}", [128, 3072], BF16, ph) for i in range(2)]
            wstr = [sb(f"wstr{i}", [128, 3072], BF16, ph) for i in range(2)]

            def prep_weight(src_ap, n_rt, n_cols, dst, quant=False, rs_store=None,
                            scale_const=None, src_i8=False):
                NC = n_cols // 128
                with bk() as blk:
                    lw0, ve1, pe1, ac1, wp1 = s_lw.n, s_ve.n, s_pe.n, s_ac.n, s_wp.n
                    p21 = s_p2.n
                    fwd_marks(("pws",), [s_wp, s_p2], [s_wp.n, s_p2.n], n_rt)

                    @blk.sync
                    def _(sy):
                        for r in range(n_rt):
                            sy.wait_ge(s_ve.h, ve1 + ((r - 1) * (5 if quant else 1)
                                                      if r >= 2 else 0))
                            dstb = (wld[r % 2][:, 0:n_cols] if not src_i8 else
                                    wld[r % 2].bitcast(I8)[:, 0:n_cols])
                            dma_mark(("pw", r), s_lw if r % 2 == 0 else s_w2,
                                     sy.dma_start(dstb, src_ap[r * 128:(r + 1) * 128, :]))

                    @blk.vector
                    def _(v):
                        for r in range(n_rt):
                            dma_wait(v, ("pw", r))
                            if r >= 2:
                                v.wait_ge(s_pe.h, pe1 + (r - 1) * NC)
                            src = (wld[r % 2][:, 0:n_cols] if not src_i8 else
                                   wld[r % 2].bitcast(I8)[:, 0:n_cols])
                            if quant:
                                s_ve.inc(v.tensor_reduce(colA[:, 0:1], src, AX, ALU.max,
                                                         apply_absolute_value=True))
                                v.wait_ge(s_ve.h, s_ve.n)
                                s_ve.inc(v.reciprocal(colA[:, 1:2], colA[:, 0:1]))
                                v.wait_ge(s_ve.h, s_ve.n)
                                s_ve.inc(v.tensor_scalar(colA[:, 1:2], colA[:, 1:2],
                                                         QMAX, None, ALU.mult))
                                if rs_store is not None:
                                    buf, c0 = rs_store
                                    v.tensor_scalar(buf[:, c0 + r:c0 + r + 1], colA[:, 0:1],
                                                    1.0 / QMAX, None, ALU.mult)
                                v.wait_ge(s_ve.h, s_ve.n)
                                s_ve.inc(v.tensor_scalar(wld[r % 2][:, 0:n_cols], src,
                                                         colA[:, 1:2], MAGIC, ALU.mult, ALU.add))
                                v.wait_ge(s_ve.h, s_ve.n)
                                i = v.tensor_scalar(wstg[r % 2][:, 0:n_cols],
                                                    wld[r % 2][:, 0:n_cols],
                                                    MAGIC, None, ALU.subtract)
                                s_ve.inc(i)
                            elif scale_const is not None:
                                s_ve.inc(v.tensor_scalar(wstg[r % 2][:, 0:n_cols], src,
                                                         scale_const, None, ALU.mult))
                            else:
                                s_ve.inc(v.tensor_copy(wstg[r % 2][:, 0:n_cols], src))

                    k_ve = 5 if quant else 1
                    @blk.tensor
                    def _(t):
                        for r in range(n_rt):
                            t.wait_ge(s_ve.h, ve1 + (r + 1) * k_ve)
                            if r >= 2:
                                dma_wait(t, ("pws", r - 2))
                            else:
                                t.wait_ge(s_wp.h, wp1)
                                t.wait_ge(s_p2.h, p21)
                            for c in range(NC):
                                q = r * NC + c
                                if q >= 2:
                                    t.wait_ge(s_ac.h, ac1 + q - 1)
                                s_pe.inc(t.transpose(psb(4 + q % 2, 128, 128),
                                                     wstg[r % 2][:, c * 128:(c + 1) * 128],
                                                     ident_b[:]))

                    @blk.scalar
                    def _(sc):
                        for r in range(n_rt):
                            for c in range(NC):
                                q = r * NC + c
                                sc.wait_ge(s_pe.h, pe1 + q + 1)
                                s_ac.inc(sc.copy(wstr[r % 2][:, c * 128:(c + 1) * 128],
                                                 psb(4 + q % 2, 128, 128)))

                    @blk.gpsimd
                    def _(g):
                        for r in range(n_rt):
                            g.wait_ge(s_ac.h, ac1 + (r + 1) * NC)
                            WT = dst.shape[2]
                            (s_wp if r % 2 == 0 else s_p2).dma(g.dma_start(
                                _dap(dst, r * 128, [[WT, 128], [128 * WT, NC], [1, 128]]),
                                wstr[r % 2][:, 0:NC * 128]))

            def prep_cast(src_ap, n_rt, n_cols, dst):
                with bk() as blk:
                    lw0, ve1, wp1 = s_lw.n, s_ve.n, s_wp.n
                    p21 = s_p2.n
                    fwd_marks(("pcs",), [s_wp, s_p2], [s_wp.n, s_p2.n], n_rt)

                    @blk.sync
                    def _(sy):
                        for r in range(n_rt):
                            sy.wait_ge(s_ve.h, ve1 + (r - 1 if r >= 2 else 0))
                            dma_mark(("pc", r), s_lw if r % 2 == 0 else s_w2,
                                     sy.dma_start(wld[r % 2][:, 0:n_cols],
                                                  src_ap[r * 128:(r + 1) * 128, :]))

                    @blk.vector
                    def _(v):
                        for r in range(n_rt):
                            dma_wait(v, ("pc", r))
                            if r >= 2:
                                dma_wait(v, ("pcs", r - 2))
                            else:
                                v.wait_ge(s_wp.h, wp1)
                                v.wait_ge(s_p2.h, p21)
                            s_ve.inc(v.tensor_copy(wstg[r % 2][:, 0:n_cols],
                                                   wld[r % 2][:, 0:n_cols]))

                    @blk.gpsimd
                    def _(g):
                        for r in range(n_rt):
                            g.wait_ge(s_ve.h, ve1 + r + 1)
                            (s_wp if r % 2 == 0 else s_p2).dma(
                                g.dma_start(dst[r], wstg[r % 2][:, 0:n_cols]))

            prep_weight(W["W_in"], 2 * DI // 128, D, S["WinT"])
            prep_weight(W["w_gate"], NIT, D, S["wgT"], quant=True, rs_store=(wgs, 0))
            prep_weight(W["w_up"], NIT, D, S["wuT"], quant=True, rs_store=(wgs, NIT))
            prep_weight(W["w_down"], FT, IF, S["wdT"], quant=True, rs_store=(wds, 0))
            prep_weight(W["W_out"], FT, DI, S["woutT"], scale_const=0.5)
            prep_weight(W["fw_int8"], FT, D, S["fwT"], src_i8=True)
            prep_cast(W["w_attn"], FT, D + 2 * HD, S["wattnT"])
            prep_cast(W["w_proj"], FT, D, S["wprojT"])

            wxTp = sb("wxTp", [128, NDT * 96 * 2], BF16, ph)
            wdtTp = sb("wdtTp", [64, NDT * 128 * 2], BF16, ph)
            for dirn in range(2):
                wx_src = W["Wx"] if dirn == 0 else W["Wx_b"]
                wd_src = W["Wdt"] if dirn == 0 else W["Wdt_b"]
                with bk() as blk:
                    ve1, pe1, ac1w = s_ve.n, s_pe.n, s_ac.n

                    @blk.sync
                    def _(sy):
                        gate(sy)
                        s_lw.dma(sy.dma_start(wld[0][0:96, 0:DI], wx_src))
                        s_lw.dma(sy.dma_start(
                            wld[1][:, 0:NDT * DTR],
                            _dap(wd_src, 0, [[DTR, 128], [128 * DTR, NDT], [1, DTR]])))

                    @blk.vector
                    def _(v):
                        v.wait_ge(s_lw.h, s_lw.n)
                        s_ve.inc(v.tensor_copy(wstg[0][0:96, 0:DI], wld[0][0:96, 0:DI]))
                        s_ve.inc(v.tensor_copy(wstg[1][:, 0:NDT * DTR],
                                               wld[1][:, 0:NDT * DTR]))

                    @blk.tensor
                    def _(t):
                        t.wait_ge(s_ve.h, ve1 + 2)
                        for u in range(NDT):
                            if u >= 2:
                                t.wait_ge(s_ac.h, ac1w + u - 1)
                            s_pe.inc(t.transpose(psb(4 + u % 2, 128, 96),
                                                 wstg[0][0:96, u * 128:(u + 1) * 128],
                                                 ident_b[0:96, 0:96]))
                        for u in range(NDT):
                            if u >= 2:
                                t.wait_ge(s_ac.h, ac1w + NDT + u - 1)
                            s_pe.inc(t.transpose(psb(6 + u % 2, 64, 128),
                                                 wstg[1][:, u * DTR:(u + 1) * DTR],
                                                 ident_b[:]))

                    @blk.scalar
                    def _(sc):
                        k = pe1
                        for u in range(NDT):
                            k += 1
                            sc.wait_ge(s_pe.h, k)
                            s_ac.inc(sc.copy(wxTp[:, (dirn * NDT + u) * 96:
                                                   (dirn * NDT + u + 1) * 96],
                                             psb(4 + u % 2, 128, 96)))
                        for u in range(NDT):
                            k += 1
                            sc.wait_ge(s_pe.h, k)
                            s_ac.inc(sc.copy(wdtTp[0:64, (dirn * NDT + u) * 128:
                                                    (dirn * NDT + u + 1) * 128],
                                             psb(6 + u % 2, 64, 128)))

            with bk() as blk:
                @blk.gpsimd
                def _(g):
                    gate(g)
                    s_wp.dma(g.dma_start(S["wxs"], wxTp[:]))
                    s_wp.dma(g.dma_start(S["wdts"], wdtTp[0:64, :]))

        def ln_stats_scale(xn_t, sqt, big32, bc_m, bc_r, grow, rowm, rowr, rowt,
                           cc_idx, sig_col, sinv_slot, s_slot, finish=True):
            with bk() as blk:
                ve1, ac1, pe1 = s_ve.n, s_ac.n, s_pe.n

                @blk.vector
                def _(v):
                    gate(v)
                    for j in range(FT):
                        s_ve.inc(v.tensor_copy(xn_t[j][:], res[j][:]))

                @blk.scalar
                def _(sc):
                    gate(sc)
                    for j in range(FT):
                        sc.wait_ge(s_ve.h, ve1 + j + 1)
                        if j >= 1:
                            sc.wait_ge(s_pe.h, pe1 + j * (2 * N5))
                        s_ac.inc(sc.activation(sqt[:], xn_t[j][:], AF.Square))
                        sc.wait_ge(s_ac.h, s_ac.n)

                @blk.tensor
                def _(t):
                    gate(t)
                    for j in range(FT):
                        t.wait_ge(s_ve.h, ve1 + j + 1)
                        t.wait_ge(s_ac.h, ac1 + j + 1)
                        for b in range(N5):
                            t.matmul(psum[b][0:1, 0:C5], onecol_b[:],
                                     xn_t[j][:, b * C5:(b + 1) * C5],
                                     start=(j == 0), stop=(j == FT - 1))
                            i = t.matmul(psum[4 + b][0:1, 0:C5], onecol_b[:],
                                         sqt[:, b * C5:(b + 1) * C5],
                                         start=(j == 0), stop=(j == FT - 1))
                            s_pe.inc(i, 2)

                @blk.vector
                def _(v):
                    v.wait_ge(s_pe.h, pe1 + FT * 2 * N5)
                    for b in range(N5):
                        s_ve.inc(v.tensor_scalar(rowm[:, b * C5:(b + 1) * C5],
                                                 psum[b][0:1, 0:C5], 1.0 / D, None, ALU.mult))
                        s_ve.inc(v.tensor_scalar(rowt[:, b * C5:(b + 1) * C5],
                                                 psum[4 + b][0:1, 0:C5], 1.0 / D, None, ALU.mult))
                    v.wait_ge(s_ve.h, s_ve.n)
                    s_ve.inc(v.tensor_tensor(rowr[:], rowm[:], rowm[:], ALU.mult))
                    v.wait_ge(s_ve.h, s_ve.n)
                    s_ve.inc(v.tensor_tensor(rowr[:], rowt[:, 0:T], rowr[:], ALU.subtract))
                    v.wait_ge(s_ve.h, s_ve.n)
                    s_ve.inc(v.tensor_scalar(rowr[:], rowr[:], 1e-5, None, ALU.add))

                @blk.scalar
                def _(sc):
                    sc.wait_ge(s_ve.h, s_ve.n)
                    s_ac.inc(sc.activation(rowr[:], rowr[:], AF.Ln))
                    sc.wait_ge(s_ac.h, s_ac.n)
                    s_ac.inc(sc.activation(rowr[:], rowr[:], AF.Exp, scale=-0.5))

                @blk.gpsimd
                def _(g):
                    g.wait_ge(s_ac.h, s_ac.n)
                    s_bc.dma(g.dma_start(bc_m[:], bass.AP(rowm, 0, [[1, 1], [0, 128], [1, T]])))
                    s_bc.dma(g.dma_start(bc_r[:], bass.AP(rowr, 0, [[1, 1], [0, 128], [1, T]])))

                @blk.vector
                def _(v):
                    v.wait_ge(s_bc.h, s_bc.n)
                    for j in range(FT):
                        v.wait_ge(s_ve.h, s_ve.n)
                        s_ve.inc(v.tensor_tensor(big32[0][:], res[j][:], bc_m[:], ALU.subtract))
                        v.wait_ge(s_ve.h, s_ve.n)
                        s_ve.inc(v.tensor_tensor(big32[0][:], big32[0][:], bc_r[:], ALU.mult))
                        v.wait_ge(s_ve.h, s_ve.n)
                        s_ve.inc(v.tensor_reduce(stat[:, j:j + 1], big32[0][:], AX, ALU.max,
                                                 apply_absolute_value=True))
                        v.wait_ge(s_ve.h, s_ve.n)
                        s_ve.inc(v.tensor_copy(xn_t[j][:], big32[0][:]))

                @blk.gpsimd
                def _(g):
                    g.wait_ge(s_ve.h, s_ve.n)
                    s_ax.dma(g.dma_start(grow[:, 0:128 * FT], stat[:, 0:FT]))

                @blk.vector
                def _(v):
                    v.wait_ge(s_ax.h, s_ax.n)
                    s_ve.inc(v.tensor_reduce(rowt[:, 0:1], grow[:, 0:128 * FT],
                                             AX, ALU.max))
                marks = {"ve": s_ve.n, "rowr": rowr, "rowt": rowt}
            if finish:
                ln_cc_start(marks, cc_idx)
                ln_scale_calc(marks, sig_col, sinv_slot, s_slot)
            return marks

        def ln_cc_start(marks, cc_idx):
            rowt = marks["rowt"]
            with bk() as blk:
                @blk.gpsimd
                def _(g):
                    g.wait_ge(s_ve.h, marks["ve"])
                    s_ax.dma(g.dma_start(rowt[:, 8:16],
                                         bass.AP(rowt, 0, [[1, 1], [0, 8], [1, 1]])))
                    g.wait_ge(s_ax.h, s_ax.n)
                    s_cc.dma(g.dma_start(S[f"cci{cc_idx}"], rowt[:, 8:16]))
                    g.wait_ge(s_cc.h, s_cc.n)
                    s_cx.inc(g.collective_compute(
                        "AllReduce", ALU.max, replica_groups=[list(range(n_cores))],
                        ins=[S[f"cci{cc_idx}"].opt()], outs=[S[f"cco{cc_idx}"].opt()]))
                marks["cx"] = s_cx.n
                marks["cc_idx"] = cc_idx

        def ln_scale_calc(marks, sig_col, sinv_slot, s_slot):
            rowr = marks["rowr"]
            rowt = marks["rowt"]
            with bk() as blk:
                @blk.gpsimd
                def _(g):
                    g.wait_ge(s_cx.h, marks["cx"])
                    s_cc.dma(g.dma_start(rowt[:, 0:8], S[f"cco{marks['cc_idx']}"]))
                cc2 = s_cc.n

                @blk.vector
                def _(v):
                    v.wait_ge(s_cc.h, cc2)
                    v.wait_ge(s_ve.h, marks["ve"])
                    if sig_col is not None:
                        s_ve.inc(v.tensor_tensor(rowt[:, 1:2], rowt[:, 0:1],
                                                 scm[:, sig_col:sig_col + 1], ALU.mult))
                        v.wait_ge(s_ve.h, s_ve.n)
                        s_ve.inc(v.tensor_scalar(rowt[:, 1:2], rowt[:, 1:2],
                                                 1.0 / QMAX, None, ALU.mult))
                    else:
                        s_ve.inc(v.tensor_scalar(rowt[:, 1:2], rowt[:, 0:1], 1.0 / QMAX, 1e-5,
                                                 ALU.mult, ALU.max))
                    v.wait_ge(s_ve.h, s_ve.n)
                    s_ve.inc(v.tensor_copy(scm[:, s_slot:s_slot + 1], rowt[:, 1:2]))
                    s_ve.inc(v.reciprocal(scm[:, sinv_slot:sinv_slot + 1], rowt[:, 1:2]))
                    v.wait_ge(s_ve.h, s_ve.n)
                    s_ve.inc(v.tensor_scalar(rowt[:, 0:T], rowr[:],
                                             scm[0:1, sinv_slot:sinv_slot + 1], None, ALU.mult))

                @blk.gpsimd
                def _(g):
                    g.wait_ge(s_ve.h, s_ve.n)
                    s_bc.dma(g.dma_start(scmc[:], bass.AP(scm, 0, [[1, 1], [0, 128], [1, 16]])))

        def ln_quant_sweep(xq_t, big32, bc_m, bc_r, sinv_col=None, rowt_src=None):
            with bk() as blk:
                if rowt_src is not None:
                    @blk.gpsimd
                    def _(g):
                        gate(g)
                        s_bc.dma(g.dma_start(
                            bc_r[:], bass.AP(rowt_src, 0, [[1, 1], [0, 128], [1, T]])))

                @blk.vector
                def _(v):
                    gate(v)
                    if rowt_src is not None:
                        v.wait_ge(s_bc.h, s_bc.n)
                    for j in range(FT):
                        v.wait_ge(s_ve.h, s_ve.n)
                        s_ve.inc(v.tensor_tensor(big32[0][:], res[j][:], bc_m[:], ALU.subtract))
                        v.wait_ge(s_ve.h, s_ve.n)
                        s_ve.inc(v.tensor_tensor(big32[0][:], big32[0][:], bc_r[:], ALU.mult))
                        v.wait_ge(s_ve.h, s_ve.n)
                        if sinv_col is not None:
                            s_ve.inc(v.tensor_scalar(big32[0][:], big32[0][:], sinv_col,
                                                     MAGIC, ALU.mult, ALU.add))
                        else:
                            s_ve.inc(v.tensor_scalar(big32[0][:], big32[0][:], MAGIC,
                                                     MAGIC, ALU.add, ALU.subtract))
                        v.wait_ge(s_ve.h, s_ve.n)
                        if sinv_col is not None:
                            s_ve.inc(v.tensor_scalar(big32[0][:], big32[0][:], MAGIC, QMAX,
                                                     ALU.subtract, ALU.min))
                            v.wait_ge(s_ve.h, s_ve.n)
                            s_ve.inc(v.tensor_scalar(xq_t[j][:], big32[0][:],
                                                     -QMAX - 1.0, None, ALU.max))
                        else:
                            s_ve.inc(v.tensor_scalar(big32[0][:], big32[0][:], QMAX,
                                                     -QMAX - 1.0, ALU.min, ALU.max))
                            v.wait_ge(s_ve.h, s_ve.n)
                            s_ve.inc(v.tensor_copy(xq_t[j][:], big32[0][:]))

        with ExitStack() as pl0:
            pbc = ExitStack()
            bc_m0 = sb("bc_m", [128, T], F32, pbc)
            prow = ExitStack()
            rowm = sb("rowm", [1, T], F32, prow)
            rowr = sb("rowr", [1, T], F32, prow)
            rowt = sb("rowt", [1, 2048], F32, prow)
            with ExitStack() as pxz:
                xn_bf = [sb(f"xnbf{i}", [128, T], BF16, pxz) for i in range(FT)]
                with ExitStack() as ph:
                    sqt = sb("sqt0", [128, T], BF16, ph)
                    big32 = [sb(f"bg{i}", [128, T], F32, ph) for i in range(1)]
                    bc_r0 = sb("bc_r", [128, T], F32, ph)
                    grow = sb("grow0", [1, 128 * FT], F32, ph)
                    marks0 = ln_stats_scale(xn_bf, sqt, big32, bc_m0, bc_r0, grow,
                                            rowm, rowr, rowt, 0, 2, 4, 5, finish=False)
                    ln_cc_start(marks0, 0)

                with ExitStack() as p2:
                    winT = [sb(f"winT{k}", [128, 2 * DI], BF16, p2) for k in range(FT)]
                    xzst = [sb(f"xzst{i}", [128, T], BF16, p2) for i in range(2)]
                    with bk() as blk:
                        lw0, pe1, ac1, sp1 = s_lw.n, s_pe.n, s_ac.n, s_sp.n
                        fwd_marks(("xzs",), [s_sp, s_p2], [s_sp.n, s_p2.n], 2 * NDT)

                        @blk.sync
                        def _(sy):
                            gate(sy)
                            for k in range(FT):
                                s_lw.dma(sy.dma_start(winT[k][:], S["WinT"][k]))

                        @blk.tensor
                        def _(t):
                            gate(t)
                            t.wait_ge(s_lw.h, s_lw.n)
                            for ct in range(2 * NDT):
                                for ch in range(N5):
                                    q = ct * N5 + ch
                                    if q >= 4:
                                        t.wait_ge(s_ac.h, ac1 + q - 3)
                                    for k in range(FT):
                                        i = t.matmul(psum[q % 4][:, 0:C5],
                                                     winT[k][:, ct * 128:(ct + 1) * 128],
                                                     xn_bf[k][:, ch * C5:(ch + 1) * C5],
                                                     start=(k == 0), stop=(k == FT - 1))
                                    s_pe.inc(i)

                        @blk.scalar
                        def _(sc):
                            for ct in range(2 * NDT):
                                if ct >= 2:
                                    dma_wait(sc, ("xzs", ct - 2))
                                for ch in range(N5):
                                    q = ct * N5 + ch
                                    sc.wait_ge(s_pe.h, pe1 + q + 1)
                                    s_ac.inc(sc.copy(xzst[ct % 2][:, ch * C5:(ch + 1) * C5],
                                                     psum[q % 4][:, 0:C5]))

                        @blk.gpsimd
                        def _(g):
                            for ct in range(2 * NDT):
                                g.wait_ge(s_ac.h, ac1 + (ct + 1) * N5)
                                (s_sp if ct % 2 == 0 else s_p2).dma(
                                    g.dma_start(S["xz"][ct], xzst[ct % 2][:]))

            ln_scale_calc(marks0, 2, 4, 5)

            with ExitStack() as pffn:
                xq0 = [sb(f"xq{i}", [128, T], BF16, pffn) for i in range(FT)]
                with ExitStack() as ph:
                    big32 = [sb(f"bgq{i}", [128, T], F32, ph) for i in range(1)]
                    bc_rq = sb("bc_rq", [128, T], F32, ph)
                    ln_quant_sweep(xq0, big32, bc_m0, bc_rq, rowt_src=rowt)

                with ExitStack() as p2:
                    wgu = [sb(f"wgu{i}", [128, 2048], BF16, p2) for i in range(2)]
                    hst = [sb(f"hst{i}", [128, T], BF16, p2) for i in range(2)]
                    sgt = [sb(f"sgt{i}", [128, 512], BF16, p2) for i in range(2)]
                    with bk() as blk:
                        lw0, pe1, ve1, ac1, sp1 = s_lw.n, s_pe.n, s_ve.n, s_ac.n, s_sp.n
                        KVE = N5 + 3
                        fwd_marks(("hsp",), [s_sp, s_p2], [s_sp.n, s_p2.n], NIT)

                        @blk.sync
                        def _(sy):
                            gate(sy)
                            for it in range(NIT):
                                if it >= 2:
                                    sy.wait_ge(s_pe.h, pe1 + (it - 1) * N5)
                                sx = s_lw if it % 2 == 0 else s_w2
                                sx.dma(sy.dma_start(
                                    wgu[it % 2][:, 0:1024],
                                    _dap(S["wgT"], it * 128,
                                         [[IF, 128], [128 * IF, FT], [1, 128]])))
                                dma_mark(("wgu", it), sx, sy.dma_start(
                                    wgu[it % 2][:, 1024:2048],
                                    _dap(S["wuT"], it * 128,
                                         [[IF, 128], [128 * IF, FT], [1, 128]])))

                        @blk.tensor
                        def _(t):
                            gate(t)
                            for it in range(NIT):
                                dma_wait(t, ("wgu", it))
                                if it >= 1:
                                    t.wait_ge(s_ac.h, ac1 + it * N5)
                                    t.wait_ge(s_ve.h, ve1 + it * KVE)
                                for ch in range(N5):
                                    for k in range(FT):
                                        t.matmul(psum[ch % 4][:, 0:C5],
                                                 wgu[it % 2][:, k * 128:(k + 1) * 128],
                                                 xq0[k][:, ch * C5:(ch + 1) * C5],
                                                 start=(k == 0), stop=(k == FT - 1))
                                    for k in range(FT):
                                        i = t.matmul(psum[4 + ch % 4][:, 0:C5],
                                                     wgu[it % 2][:, 1024 + k * 128:1024 + (k + 1) * 128],
                                                     xq0[k][:, ch * C5:(ch + 1) * C5],
                                                     start=(k == 0), stop=(k == FT - 1))
                                    s_pe.inc(i)

                        @blk.vector
                        def _(v):
                            for it in range(NIT):
                                s_ve.inc(v.tensor_tensor(colA[:, 2:3], wgs[:, it:it + 1],
                                                         scmc[:, 5:6], ALU.mult))
                                s_ve.inc(v.tensor_tensor(colA[:, 3:4], wgs[:, NIT + it:NIT + it + 1],
                                                         scmc[:, 5:6], ALU.mult))
                                v.wait_ge(s_ve.h, s_ve.n)
                                for ch in range(N5):
                                    v.wait_ge(s_ac.h, ac1 + it * N5 + ch + 1)
                                    if it >= 2:
                                        dma_wait(v, ("hsp", it - 2))
                                    s_ve.inc(v.scalar_tensor_tensor(
                                        hst[it % 2][:, ch * C5:(ch + 1) * C5],
                                        psum[4 + ch % 4][:, 0:C5], colA[:, 3:4],
                                        sgt[ch % 2][:, 0:C5], ALU.mult, ALU.mult))
                                v.wait_ge(s_ve.h, s_ve.n)
                                s_ve.inc(v.tensor_reduce(stat[:, 8 + it:9 + it], hst[it % 2][:],
                                                         AX, ALU.max, apply_absolute_value=True))

                        @blk.scalar
                        def _(sc):
                            for it in range(NIT):
                                sc.wait_ge(s_ve.h, ve1 + it * KVE + 2)
                                for ch in range(N5):
                                    sc.wait_ge(s_pe.h, pe1 + it * N5 + ch + 1)
                                    if it * N5 + ch >= 2:
                                        sc.wait_ge(s_ve.h, ve1 + (it * N5 + ch - 1) // N5 * KVE
                                                   + 2 + (it * N5 + ch - 1) % N5 + 1)
                                    s_ac.inc(sc.activation(sgt[ch % 2][:, 0:C5],
                                                           psum[ch % 4][:, 0:C5],
                                                           AF.Silu, scale=colA[:, 2:3]))

                        @blk.gpsimd
                        def _(g):
                            for it in range(NIT):
                                g.wait_ge(s_ve.h, ve1 + (it + 1) * KVE)
                                (s_sp if it % 2 == 0 else s_p2).dma(
                                    g.dma_start(S["hpre"][it], hst[it % 2][:]))

            prow.close()
            dbg_dump(0)

            with ExitStack() as ph:
                hch = sb("hch", [128, NIT * 512], BF16, ph)
                grow2 = sb("grow2", [1, 128 * NIT + 16], F32, ph)
                xq2 = sb("xqh2", [128, NIT * 512], BF16, ph)
                q32 = sb("q32", [128, 512], F32, ph)
                wdst = [sb(f"wdst{i}", [128, NIT * 128], BF16, ph) for i in range(FT)]

                with bk() as blk:
                    @blk.gpsimd
                    def _(g):
                        gate(g)
                        s_ax.dma(g.dma_start(grow2[:, 0:128 * NIT], stat[:, 8:8 + NIT]))

                    @blk.vector
                    def _(v):
                        gate(v)
                        v.wait_ge(s_ax.h, s_ax.n)
                        s_ve.inc(v.tensor_reduce(grow2[:, 128 * NIT:128 * NIT + 1],
                                                 grow2[:, 0:128 * NIT], AX, ALU.max))

                    @blk.gpsimd
                    def _(g):
                        g.wait_ge(s_ve.h, s_ve.n)
                        s_ax.dma(g.dma_start(
                            grow2[:, 128 * NIT + 8:128 * NIT + 16],
                            grow2[:, 128 * NIT:128 * NIT + 1].unsqueeze(1)
                            .broadcast_to([1, 8, 1])))
                        g.wait_ge(s_ax.h, s_ax.n)
                        s_cc.dma(g.dma_start(S["cci1"], grow2[:, 128 * NIT + 8:128 * NIT + 16]))
                        g.wait_ge(s_cc.h, s_cc.n)
                        s_cx.inc(g.collective_compute(
                            "AllReduce", ALU.max, replica_groups=[list(range(n_cores))],
                            ins=[S["cci1"].opt()], outs=[S["cco1"].opt()]))
                        g.wait_ge(s_cx.h, s_cx.n)
                        s_cc.dma(g.dma_start(grow2[:, 128 * NIT:128 * NIT + 8], S["cco1"]))

                    @blk.vector
                    def _(v):
                        v.wait_ge(s_cc.h, s_cc.n)
                        v.wait_ge(s_ve.h, s_ve.n)
                        s_ve.inc(v.tensor_tensor(grow2[:, 128 * NIT + 1:128 * NIT + 2],
                                                 grow2[:, 128 * NIT:128 * NIT + 1],
                                                 scm[:, 3:4], ALU.mult))
                        v.wait_ge(s_ve.h, s_ve.n)
                        s_ve.inc(v.tensor_scalar(grow2[:, 128 * NIT + 1:128 * NIT + 2],
                                                 grow2[:, 128 * NIT + 1:128 * NIT + 2],
                                                 1.0 / QMAX, None, ALU.mult))
                        v.wait_ge(s_ve.h, s_ve.n)
                        s_ve.inc(v.tensor_copy(scm[:, 7:8],
                                               grow2[:, 128 * NIT + 1:128 * NIT + 2]))
                        s_ve.inc(v.reciprocal(scm[:, 6:7],
                                              grow2[:, 128 * NIT + 1:128 * NIT + 2]))

                    @blk.gpsimd
                    def _(g):
                        g.wait_ge(s_ve.h, s_ve.n)
                        s_bc.dma(g.dma_start(scmc[:], bass.AP(scm, 0, [[1, 1], [0, 128], [1, 16]])))

                    @blk.vector
                    def _(v):
                        v.wait_ge(s_bc.h, s_bc.n)
                        s_ve.inc(v.tensor_tensor(stat[:, 48:48 + FT], wds[:, 0:FT],
                                                 scmc[:, 7:8].broadcast_to([128, FT]), ALU.mult))

                with bk() as blk:
                    lw0, ld0 = s_lw.n, s_ld.n
                    ve1, pe1 = s_ve.n, s_pe.n
                    KQ = 3 * NIT + FT

                    @blk.gpsimd
                    def _(g):
                        gate(g)
                        for ch in range(N5):
                            if ch >= 1:
                                g.wait_ge(s_ve.h, ve1 + ch * KQ - FT)
                            dma_mark(("hld", ch), s_gd,
                                     g.dma_start(
                                hch[:, 0:NIT * C5],
                                _dap(S["hpre"], ch * C5,
                                     [[T, 128], [128 * T, NIT], [1, C5]])))

                    @blk.sync
                    def _(sy):
                        gate(sy)
                        for dm in range(FT):
                            dma_mark(("wds", dm), s_lw if dm % 2 == 0 else s_w2,
                                     sy.dma_start(
                                wdst[dm][:, 0:NIT * 128],
                                _dap(S["wdT"], dm * 128,
                                     [[D, 128], [128 * D, NIT], [1, 128]])))

                    @blk.vector
                    def _(v):
                        gate(v)
                        for ch in range(N5):
                            dma_wait(v, ("hld", ch))
                            if ch >= 1:
                                v.wait_ge(s_pe.h, pe1 + ch * FT)
                            for it in range(NIT):
                                v.wait_ge(s_ve.h, s_ve.n)
                                s_ve.inc(v.tensor_scalar(q32[:, 0:C5],
                                                         hch[:, it * C5:(it + 1) * C5],
                                                         scmc[:, 6:7], MAGIC, ALU.mult, ALU.add))
                                v.wait_ge(s_ve.h, s_ve.n)
                                s_ve.inc(v.tensor_scalar(q32[:, 0:C5], q32[:, 0:C5], MAGIC, QMAX,
                                                         ALU.subtract, ALU.min))
                                v.wait_ge(s_ve.h, s_ve.n)
                                s_ve.inc(v.tensor_scalar(xq2[:, it * C5:(it + 1) * C5],
                                                         q32[:, 0:C5],
                                                         -QMAX - 1.0, None, ALU.max))
                            s_ve.inc(v.nop()) if False else None
                            for dm in range(FT):
                                v.wait_ge(s_pe.h, pe1 + ch * FT + dm + 1)
                                s_ve.inc(v.scalar_tensor_tensor(
                                    res[dm][:, ch * C5:(ch + 1) * C5],
                                    psum[dm % 4][:, 0:C5], stat[:, 48 + dm:49 + dm],
                                    res[dm][:, ch * C5:(ch + 1) * C5],
                                    ALU.mult, ALU.add))

                    @blk.tensor
                    def _(t):
                        for ch in range(N5):
                            t.wait_ge(s_ve.h, ve1 + ch * KQ + 3 * NIT)
                            for dm in range(FT):
                                q = ch * FT + dm
                                t.wait_ge(s_lw.h, dmk[("wds", FT - 2)][1])
                                t.wait_ge(s_w2.h, dmk[("wds", FT - 1)][1])
                                if q >= 4:
                                    qa, qb = divmod(q - 4, FT)
                                    t.wait_ge(s_ve.h, ve1 + qa * KQ + 3 * NIT + qb + 1)
                                for k in range(NIT):
                                    i = t.matmul(psum[dm % 4][:, 0:C5],
                                                 wdst[dm][:, k * 128:(k + 1) * 128],
                                                 xq2[:, k * C5:(k + 1) * C5],
                                                 start=(k == 0), stop=(k == NIT - 1))
                                s_pe.inc(i)

            dbg_dump(1)
            pbc.close()

            with ExitStack() as ph:
                HW = TC + 3
                wxT = sb("wxT", [128, NDT * 96 * 2], BF16, ph)
                wdtT = sb("wdtT", [64, NDT * 128 * 2], BF16, ph)
                with bk() as blk:
                    @blk.gpsimd
                    def _(g):
                        gate(g)
                        s_gd.dma(g.dma_start(wxT[:], S["wxs"]))
                        s_gd.dma(g.dma_start(wdtT[0:64, :], S["wdts"]))
                xzin = sb("xzin", [128, NDT * HW], BF16, ph)
                xsch = sb("xsch", [128, NDT * TC], BF16, ph)
                bbuf = sb("bbuf", [128, DS * TC], BF16, ph)
                cbuf = sb("cbuf", [128, DS * TC], BF16, ph)
                hall = sb("hall", [128, DS * TC], BF16, ph)
                dab = [sb(f"dab{i}", [128, TC], BF16, ph) for i in range(4)]
                dbb = [sb(f"dbb{i}", [128, TC], BF16, ph) for i in range(2)]
                pbf = [sb(f"pbf{i}", [128, TC], BF16, ph) for i in range(4)]
                pxd = [sb(f"pxd{i}", [128, TC], BF16, ph) for i in range(2)]
                ctp = [sb(f"ctp{i}", [128, TC], BF16, ph) for i in range(2)]
                dtch = sb("dtch", [128, TC], F32, ph)
                uch = sb("uch", [128, TC], BF16, ph)
                statev = sb("statev", [128, NDT * DS], F32, ph)
                conv32 = [sb(f"cv{i}", [128, TC], F32, ph) for i in range(1)]
                szb = [sb(f"szb{i}", [128, TC], BF16, ph) for i in range(NDT)]
                otb = [sb(f"otb{i}", [128, TC], BF16, ph) for i in range(2)]
                cmb = [sb(f"cmb{i}", [128, TC], BF16, ph) for i in range(2)]
                ofl = [sb(f"ofl{i}", [128, TC], BF16, ph) for i in range(2)]
                xdbl = sb("xdbl", [96, TC], BF16, ph)

                prev_tts_end = [None]
                prev_gp_end = [None]
                prev_pe_end = [None]

                def emit_xload(g, dirn_, ch_):
                    edge_ = (ch_ == 0)
                    if dirn_ == 0:
                        t0_ = ch_ * TC
                        lo_ = max(0, t0_ - 3)
                        nc_ = (t0_ + TC) - lo_
                    else:
                        thi_ = T - 1 - ch_ * TC
                        tlo_ = thi_ - (TC - 1)
                        lo_ = tlo_
                        nc_ = min(T, thi_ + 4) - tlo_
                    xzv = xzin[:].rearrange("p (t c) -> p t c", c=HW)
                    if dirn_ == 0:
                        off_ = 3 if edge_ else 0
                        if edge_:
                            g.memset(xzv[:, :, 0:3], 0.0)
                        i = g.dma_start(
                            xzv[:, :, off_:off_ + nc_],
                            _dap(S["xz"], lo_, [[T, 128], [128 * T, NDT], [1, nc_]]))
                    else:
                        if edge_:
                            g.memset(xzv[:, :, TC:TC + 3], 0.0)
                        i = g.dma_start(
                            xzv[:, :, 0:nc_],
                            _dap(S["xz"], lo_, [[T, 128], [128 * T, NDT], [1, nc_]]))
                    dma_mark(("xld", dirn_, ch_), s_gd, i)

                for dirn in range(2):
                    AO = dirn * NDT
                    with bk() as blk:
                        @blk.gpsimd
                        def _(g):
                            gate(g)
                            s_gp.inc(g.memset(statev[:], 0.0))
                            if dirn == 0:
                                emit_xload(g, 0, 0)

                    for ch in range(NCH):
                        if dirn == 0:
                            zlo = ch * TC
                        else:
                            zlo = T - 1 - ch * TC - (TC - 1)
                        edge = (ch == 0)

                        with bk() as blk:
                            fwd_marks(("osp", dirn, ch), [s_sp, s_p2],
                                      [s_sp.n, s_p2.n], NDT)
                            if dirn == 1:
                                fwd_marks(("ofl", dirn, ch), [s_o1, s_o2],
                                          [s_o1.n, s_o2.n], NDT)

                            ve_t0 = s_ve.n
                            ac_s0 = s_ac.n
                            pe_c0 = s_pe.n
                            gp_c0 = s_gp.n
                            zld_mark = [s_gd.n + 16]
                            cde = {}
                            _c = 0
                            for _d in range(0, NDT, 2):
                                _c += 4
                                cde[_d] = _c
                            codd = list(range(1, NDT, 2))
                            cgp = {d: 4 * (j + 1) for j, d in enumerate(codd)}
                            cpe = {d: 4 * (j + 1) for j, d in enumerate(codd)}

                            def xsl(d, k):
                                base = d * HW
                                if dirn == 0:
                                    return xzin[:, base + k:base + k + TC]
                                return xzin[:, base + 3 - k:base + 3 - k + TC][:, ::-1]

                            @blk.vector
                            def _(v):
                                gate(v)
                                dma_wait(v, ("xld", dirn, ch))
                                for d in range(0, NDT, 2):
                                    wof = (AO + d) * DC
                                    if d >= 2:
                                        v.wait_ge(s_ac.h, ac_s0 + d - 1)
                                    for k in range(DC):
                                        v.wait_ge(s_ve.h, s_ve.n)
                                        if k == 0:
                                            s_ve.inc(v.tensor_scalar(conv32[0][:], xsl(d, 0),
                                                                     convw[:, wof:wof + 1],
                                                                     convb[:, AO + d:AO + d + 1],
                                                                     ALU.mult, ALU.add))
                                        else:
                                            s_ve.inc(v.scalar_tensor_tensor(
                                                conv32[0][:], xsl(d, k),
                                                convw[:, wof + k:wof + k + 1],
                                                conv32[0][:], ALU.mult, ALU.add))

                            @blk.gpsimd
                            def _(g):
                                dma_wait(g, ("xld", dirn, ch))
                                for j, d in enumerate(codd):
                                    wof = (AO + d) * DC
                                    for k in range(DC):
                                        gi = 4 * j + k
                                        if gi >= 2:
                                            g.wait_ge(s_pe.h, pe_c0 + gi - 1)
                                        s_gp.inc(g.tensor_tensor(
                                            ctp[k % 2][:], xsl(d, k),
                                            convw[:, wof + k:wof + k + 1]
                                            .broadcast_to([128, TC]), ALU.mult))

                            @blk.scalar
                            def _(sc):
                                gate(sc)
                                for d in range(NDT):
                                    if d % 2 == 0:
                                        sc.wait_ge(s_ve.h, ve_t0 + cde[d])
                                        s_ac.inc(sc.activation(xsch[:, d * TC:(d + 1) * TC],
                                                               conv32[0][:], AF.Silu))
                                    else:
                                        sc.wait_ge(s_pe.h, pe_c0 + cpe[d])
                                        s_ac.inc(sc.activation(
                                            xsch[:, d * TC:(d + 1) * TC],
                                            psum[1 + (d // 2) % 2][:, 0:TC], AF.Silu,
                                            bias=convb[:, AO + d:AO + d + 1]))
                                sc.wait_ge(s_gd.h, zld_mark[0])
                                for d in range(NDT):
                                    zsl = xzin[:, d * HW:d * HW + TC]
                                    if dirn == 1:
                                        zsl = zsl[:, ::-1]
                                    s_ac.inc(sc.activation(szb[d][:], zsl, AF.Silu))

                            pe_wx0 = s_pe.n

                            @blk.tensor
                            def _(t):
                                gate(t)
                                for d in range(NDT):
                                    if d % 2 == 1:
                                        if d >= 5:
                                            t.wait_ge(s_ac.h, ac_s0 + d - 3)
                                        for k in range(DC):
                                            t.wait_ge(s_gp.h, gp_c0 + cgp[d] - 3 + k)
                                            s_pe.inc(t.matmul(
                                                psum[1 + (d // 2) % 2][:, 0:TC],
                                                ident_b[:], ctp[k % 2][:],
                                                start=(k == 0), stop=(k == DC - 1)))
                                    t.wait_ge(s_ac.h, ac_s0 + d + 1)
                                    i = t.matmul(psum[0][0:96, 0:TC],
                                                 wxT[:, (AO + d) * 96:(AO + d + 1) * 96],
                                                 xsch[:, d * TC:(d + 1) * TC],
                                                 start=(d == 0), stop=(d == NDT - 1))
                                s_pe.inc(i)

                            pe_wx_end = s_pe.n
                            ac_xdbl0 = s_ac.n

                            @blk.scalar
                            def _(sc):
                                sc.wait_ge(s_pe.h, pe_wx_end)
                                s_ac.inc(sc.copy(xdbl[:], psum[0][0:96, 0:TC]))

                            ld_z0 = s_gd.n
                            bc_0 = s_bc.n

                            @blk.gpsimd
                            def _(g):
                                g.wait_ge(s_ve.h, ve_t0 + cde[NDT - 2])
                                g.wait_ge(s_gp.h, gp_c0 + 4 * len(codd))
                                i = g.dma_start(
                                    xzin[:].rearrange("p (t c) -> p t c", c=HW)[:, :, 0:TC],
                                    _dap(S["xz"], NDT * 128 * T + zlo,
                                         [[T, 128], [128 * T, NDT], [1, TC]]))
                                s_gd.dma(i)
                                g.wait_ge(s_ac.h, ac_xdbl0 + 1)
                                for n in range(DS):
                                    s_bc.dma(g.dma_start(
                                        bbuf[:, n * TC:(n + 1) * TC],
                                        xdbl[64 + n:65 + n, :].unsqueeze(1)
                                        .broadcast_to([1, 128, TC])))
                                for n in range(DS):
                                    s_bc.dma(g.dma_start(
                                        cbuf[:, n * TC:(n + 1) * TC],
                                        xdbl[80 + n:81 + n, :].unsqueeze(1)
                                        .broadcast_to([1, 128, TC])))
                                if dirn == 1:
                                    for d in range(2):
                                        (s_o1 if d % 2 == 0 else s_o2).dma(
                                            g.dma_start(
                                            ofl[d % 2][:],
                                            S["outf"][d][:, zlo:zlo + TC]))

                            ac3 = s_ac.n
                            pe2 = s_pe.n
                            ve2b = s_ve.n
                            gp2b = s_gp.n
                            KA3 = 19

                            def ac_dtln(d):
                                return ac3 + d * KA3 + 2

                            def ac_p17(d):
                                return ac3 + d * KA3 + 3

                            def ac_dA(d, n):
                                return ac3 + d * KA3 + 4 + n

                            def ac_sz(d):
                                return ac_s0 + NDT + d + 1

                            def gp_p(d, n):
                                return gp2b + d * DS + n + 1

                            ppos = {}
                            _c = 0
                            for _d in range(2):
                                _c += 1
                                ppos[("dt", _d)] = _c
                            for _d in range(NDT):
                                if _d + 2 < NDT:
                                    _c += 1
                                    ppos[("dt", _d + 2)] = _c
                                _c += 1
                                ppos[("y17", _d)] = _c
                                for _n in range(DS):
                                    _c += 1
                                    ppos[("y", _d, _n)] = _c

                            def pe_dt(d):
                                return pe2 + ppos[("dt", d)]

                            def pe_y17(d):
                                return pe2 + ppos[("y17", d)]

                            def pe_y(d, n):
                                return pe2 + ppos[("y", d, n)]

                            vpos = {}
                            _c = 0
                            for _d in range(NDT):
                                _c += 1
                                vpos[("u", _d)] = _c
                                for _n in range(DS):
                                    _c += 1
                                    vpos[("dbu", _d, _n)] = _c
                                    _c += 1
                                    vpos[("tts", _d, _n)] = _c
                                    if _n == 0 and _d >= 1:
                                        _c += 1
                                        vpos[("ot", _d - 1)] = _c
                                        if dirn == 1:
                                            _c += 1
                                            vpos[("cmb", _d - 1)] = _c
                                _c += 1
                                vpos[("state", _d)] = _c
                            _c += 1
                            vpos[("ot", NDT - 1)] = _c
                            if dirn == 1:
                                _c += 1
                                vpos[("cmb", NDT - 1)] = _c

                            def ve_u(d):
                                return ve2b + vpos[("u", d)]

                            def ve_tts(d, n):
                                return ve2b + vpos[("tts", d, n)]

                            def ve_ot(d):
                                return ve2b + vpos[("ot", d)]

                            def ve_cmb(d):
                                return ve2b + vpos[("cmb", d)]

                            @blk.scalar
                            def _(sc):
                                for d in range(NDT):
                                    sc.wait_ge(s_pe.h, pe_dt(d))
                                    if d >= 1:
                                        sc.wait_ge(s_ve.h, ve_u(d - 1))
                                    s_ac.inc(sc.activation(conv32[0][:], psum[4 + d % 2][:, 0:TC],
                                                           AF.Exp,
                                                           bias=bdtc[:, AO + d:AO + d + 1]))
                                    sc.wait_ge(s_ac.h, s_ac.n)
                                    s_ac.inc(sc.activation(dtch[:], conv32[0][:], AF.Ln, bias=1.0))
                                    sc.wait_ge(s_ac.h, s_ac.n)
                                    if d >= 2:
                                        sc.wait_ge(s_pe.h, pe_y17(d - 2))
                                    s_ac.inc(sc.activation(
                                        pxd[d % 2][:], xsch[:, d * TC:(d + 1) * TC],
                                        AF.Copy, scale=dcol[:, AO + d:AO + d + 1]))
                                    if prev_tts_end[0] and d == 0:
                                        sc.wait_ge(s_ve.h, prev_tts_end[0])
                                    for n in range(DS):
                                        nn = n - 4
                                        if nn >= 0:
                                            sc.wait_ge(s_ve.h, ve_tts(d, nn))
                                        elif d >= 1:
                                            sc.wait_ge(s_ve.h, ve_tts(d - 1, DS + nn))
                                        s_ac.inc(sc.activation(
                                            dab[n % 4][:], dtch[:], AF.Exp,
                                            scale=a_cols[:, (AO + d) * DS + n:(AO + d) * DS + n + 1]))

                            @blk.tensor
                            def _(t):
                                def emit_dt(d):
                                    t.wait_ge(s_ac.h, ac_xdbl0 + 1)
                                    if d >= 2:
                                        t.wait_ge(s_ac.h, ac_dtln(d - 2))
                                    s_pe.inc(t.matmul(
                                        psum[4 + d % 2][:, 0:TC],
                                        wdtT[:, (AO + d) * 128:(AO + d + 1) * 128],
                                        xdbl[0:64, :], start=True, stop=True))

                                emit_dt(0)
                                emit_dt(1)
                                for d in range(NDT):
                                    if d + 2 < NDT:
                                        emit_dt(d + 2)
                                    if d >= 2:
                                        t.wait_ge(s_ve.h, ve_ot(d - 2))
                                    t.wait_ge(s_ac.h, ac_p17(d))
                                    s_pe.inc(t.matmul(psum[6 + d % 2][:, 0:TC],
                                                      ident_b[:], pxd[d % 2][:],
                                                      start=True, stop=False))
                                    for n in range(DS):
                                        t.wait_ge(s_gp.h, gp_p(d, n))
                                        s_pe.inc(t.matmul(psum[6 + d % 2][:, 0:TC],
                                                          ident_b[:], pbf[n % 4][:],
                                                          start=False, stop=(n == DS - 1)))

                            @blk.vector
                            def _(v):
                                v.wait_ge(s_bc.h, bc_0 + 32 * 16)

                                def emit_ot(d):
                                    v.wait_ge(s_pe.h, pe_y(d, DS - 1))
                                    v.wait_ge(s_ac.h, ac_sz(d))
                                    if dirn == 0:
                                        if d >= 2:
                                            dma_wait(v, ("osp", dirn, ch, d - 2))
                                        s_ve.inc(v.tensor_tensor(
                                            otb[d % 2][:], psum[6 + d % 2][:, 0:TC],
                                            szb[d][:], ALU.mult))
                                    else:
                                        s_ve.inc(v.tensor_tensor(
                                            otb[d % 2][:], psum[6 + d % 2][:, 0:TC][:, ::-1],
                                            szb[d][:, ::-1], ALU.mult))
                                        v.wait_ge(s_ve.h, s_ve.n)
                                        dma_wait(v, ("ofl", dirn, ch, d))
                                        if d >= 2:
                                            dma_wait(v, ("osp", dirn, ch, d - 2))
                                        s_ve.inc(v.tensor_tensor(cmb[d % 2][:], otb[d % 2][:],
                                                                 ofl[d % 2][:], ALU.add))

                                for d in range(NDT):
                                    v.wait_ge(s_ac.h, ac_dtln(d))
                                    s_ve.inc(v.tensor_tensor(uch[:], dtch[:],
                                                             xsch[:, d * TC:(d + 1) * TC],
                                                             ALU.mult))
                                    v.wait_ge(s_ve.h, s_ve.n)
                                    for n in range(DS):
                                        s_ve.inc(v.tensor_tensor(dbb[n % 2][:], uch[:],
                                                                 bbuf[:, n * TC:(n + 1) * TC],
                                                                 ALU.mult))
                                        if d >= 1:
                                            v.wait_ge(s_gp.h, gp_p(d - 1, n))
                                        elif prev_gp_end[0] is not None:
                                            v.wait_ge(s_gp.h, prev_gp_end[0][n])
                                        v.wait_ge(s_ac.h, ac_dA(d, n))
                                        v.wait_ge(s_ve.h, s_ve.n)
                                        init = (statev[:, (d * DS + n):(d * DS + n + 1)]
                                                if ch > 0 else 0.0)
                                        s_ve.inc(v.tensor_tensor_scan(
                                            hall[:, n * TC:(n + 1) * TC],
                                            dab[n % 4][:],
                                            dbb[n % 2][:], init, ALU.mult, ALU.add))
                                        if n == 0 and d >= 1:
                                            emit_ot(d - 1)
                                    v.wait_ge(s_ve.h, s_ve.n)
                                    s_ve.inc(v.tensor_copy(statev[:, d * DS:(d + 1) * DS],
                                                           hall[:, TC - 1::TC]))
                                emit_ot(NDT - 1)

                            @blk.gpsimd
                            def _(g):
                                def emit_io(e):
                                    if dirn == 0:
                                        g.wait_ge(s_ve.h, ve_ot(e))
                                        (s_sp if e % 2 == 0 else s_p2).dma(g.dma_start(
                                            S["outf"][e][:, zlo:zlo + TC], otb[e % 2][:]))
                                    else:
                                        if e + 2 < NDT:
                                            g.wait_ge(s_ve.h, ve_cmb(e))
                                            (s_o1 if e % 2 == 0 else s_o2).dma(g.dma_start(
                                                ofl[e % 2][:],
                                                S["outf"][e + 2][:, zlo:zlo + TC]))
                                        g.wait_ge(s_ve.h, ve_cmb(e))
                                        (s_sp if e % 2 == 0 else s_p2).dma(g.dma_start(
                                            S["outc"][e][:, zlo:zlo + TC], cmb[e % 2][:]))

                                nxt = ((dirn, ch + 1) if ch + 1 < NCH else
                                       ((1, 0) if dirn == 0 else None))
                                for d in range(NDT):
                                    for n in range(DS):
                                        g.wait_ge(s_ve.h, ve_tts(d, n))
                                        gi = d * DS + n
                                        if gi >= 4:
                                            pn = divmod(gi - 4, DS)
                                            g.wait_ge(s_pe.h, pe_y(*pn))
                                        elif prev_pe_end[0] is not None:
                                            g.wait_ge(s_pe.h, prev_pe_end[0])
                                        s_gp.inc(g.tensor_tensor(
                                            pbf[n % 4][:],
                                            hall[:, n * TC:(n + 1) * TC],
                                            cbuf[:, n * TC:(n + 1) * TC], ALU.mult))
                                    if d == 0 and nxt is not None:
                                        g.wait_ge(s_ac.h, ac_sz(NDT - 1))
                                        emit_xload(g, *nxt)
                                    if d >= 1:
                                        emit_io(d - 1)
                                emit_io(NDT - 1)

                            prev_tts_end[0] = ve_tts(NDT - 1, DS - 1)
                            prev_gp_end[0] = [gp_p(NDT - 1, n) for n in range(DS)]
                            prev_pe_end[0] = pe_y(NDT - 1, DS - 1)

            with ExitStack() as ph:
                ocst = [sb(f"ocst{i}", [128, NDT * 512], BF16, ph) for i in range(2)]
                wost = [sb(f"wost{i}", [128, NDT * 128], BF16, ph) for i in range(FT)]
                with bk() as blk:
                    lw0, ld0, pe1, ve1 = s_lw.n, s_ld.n, s_pe.n, s_ve.n

                    @blk.gpsimd
                    def _(g):
                        gate(g)
                        for ch in range(N5):
                            if ch >= 2:
                                g.wait_ge(s_pe.h, pe1 + (ch - 1) * FT)
                            dma_mark(("oc", ch), s_gd, g.dma_start(
                                ocst[ch % 2][:, 0:NDT * C5],
                                _dap(S["outc"], ch * C5,
                                     [[T, 128], [128 * T, NDT], [1, C5]])))

                    @blk.sync
                    def _(sy):
                        gate(sy)
                        for dm in range(FT):
                            if dm >= 2:
                                dma_wait(sy, ("wo", dm - 2))
                            dma_mark(("wo", dm), s_lw if dm % 2 == 0 else s_w2,
                                     sy.dma_start(
                                wost[dm][:, 0:NDT * 128],
                                _dap(S["woutT"], dm * 128,
                                     [[D, 128], [128 * D, NDT], [1, 128]])))

                    @blk.tensor
                    def _(t):
                        for ch in range(N5):
                            dma_wait(t, ("oc", ch))
                            for dm in range(FT):
                                q = ch * FT + dm
                                dma_wait(t, ("wo", dm))
                                if q >= 4:
                                    t.wait_ge(s_ve.h, ve1 + q - 3)
                                for k in range(NDT):
                                    i = t.matmul(psum[dm % 4][:, 0:C5],
                                                 wost[dm][:, k * 128:(k + 1) * 128],
                                                 ocst[ch % 2][:, k * C5:(k + 1) * C5],
                                                 start=(k == 0), stop=(k == NDT - 1))
                                s_pe.inc(i)

                    @blk.vector
                    def _(v):
                        for ch in range(N5):
                            for dm in range(FT):
                                q = ch * FT + dm
                                v.wait_ge(s_pe.h, pe1 + q + 1)
                                s_ve.inc(v.tensor_tensor(
                                    res[dm][:, ch * C5:(ch + 1) * C5],
                                    res[dm][:, ch * C5:(ch + 1) * C5],
                                    psum[dm % 4][:, 0:C5], ALU.add))

            dbg_dump(2)

        with ExitStack() as pl1:
            rowm1 = sb("rowm1", [1, T], F32, pl1)
            rowr1 = sb("rowr1", [1, T], F32, pl1)
            rowt1 = sb("rowt1", [1, 2048], F32, pl1)
            xn1 = [sb(f"xn1_{i}", [128, T], BF16, pl1) for i in range(FT)]
            bc_m1 = sb("bc_m1", [128, T], F32, pl1)
            bc_r1 = sb("bc_r1", [128, T], F32, pl1)
            with ExitStack() as ph:
                sqt = sb("sqt1", [128, T], BF16, ph)
                big32 = [sb(f"bgb{i}", [128, T], F32, ph) for i in range(1)]
                grow = sb("grow1", [1, 128 * FT], F32, ph)
                marks1 = ln_stats_scale(xn1, sqt, big32, bc_m1, bc_r1, grow,
                                        rowm1, rowr1, rowt1, 2, None, 8, 9, finish=False)
                ln_cc_start(marks1, 2)

            with ExitStack() as ph:
                qkv = [sb(f"qkv{i}", [128, T], BF16, ph) for i in range(9)]
                kvT = sb("kvT", [128, NKT * 65], BF16, ph)
                pwt = ExitStack()
                watt = [sb(f"watt{k}", [128, D + 2 * HD], BF16, pwt) for k in range(FT)]

                with bk() as blk:
                    lw0, pe1, ac1 = s_lw.n, s_pe.n, s_ac.n
                    NJ = 9 * N5

                    @blk.sync
                    def _(sy):
                        gate(sy)
                        for k in range(FT):
                            s_lw.dma(sy.dma_start(watt[k][:], S["wattnT"][k]))

                    @blk.tensor
                    def _(t):
                        gate(t)
                        t.wait_ge(s_lw.h, s_lw.n)
                        for jt in range(9):
                            for ch in range(N5):
                                q = jt * N5 + ch
                                if q >= 4:
                                    t.wait_ge(s_ac.h, ac1 + q - 3)
                                for k in range(FT):
                                    i = t.matmul(psum[q % 4][:, 0:C5],
                                                 watt[k][:, jt * 128:(jt + 1) * 128],
                                                 xn1[k][:, ch * C5:(ch + 1) * C5],
                                                 start=(k == 0), stop=(k == FT - 1))
                                s_pe.inc(i)

                    @blk.scalar
                    def _(sc):
                        for jt in range(9):
                            for ch in range(N5):
                                q = jt * N5 + ch
                                sc.wait_ge(s_pe.h, pe1 + q + 1)
                                s_ac.inc(sc.copy(qkv[jt][:, ch * C5:(ch + 1) * C5],
                                                 psum[q % 4][:, 0:C5]))

                    @blk.gpsimd
                    def _(g):
                        gate(g)
                        s_gp.inc(g.memset(kvT[:], 1.0))

                    pe2 = s_pe.n
                    ve2 = s_ve.n
                    @blk.tensor
                    def _(t):
                        t.wait_ge(s_ac.h, s_ac.n)
                        t.wait_ge(s_gp.h, s_gp.n)
                        for kt in range(NKT):
                            if kt >= 2:
                                t.wait_ge(s_ve.h, ve2 + kt - 1)
                            s_pe.inc(t.transpose(psb(4 + kt % 2, 128, 64),
                                                 qkv[8][64:128, kt * 128:(kt + 1) * 128],
                                                 ident_b[64:128, 64:128]))

                    @blk.vector
                    def _(v):
                        for kt in range(NKT):
                            v.wait_ge(s_pe.h, pe2 + kt + 1)
                            s_ve.inc(v.tensor_copy(kvT[:, kt * 65:kt * 65 + 64],
                                                   psb(4 + kt % 2, 128, 64)))

                pwt.close()
                with ExitStack() as ph2:
                    ptile = [sb(f"pt{i}", [128, 512], BF16, ph2) for i in range(4)]
                    arow = sb("arow", [64, 512], F32, ph2)
                    kk = sb("kk", [128, T], BF16, ph2)
                    atto = [sb(f"atto{i}", [64, 512], BF16, ph2) for i in range(2)]
                    dmask = sb("dmask", [128, 512 * 4], BF16, ph2)
                    growf = [sb(f"growf{i}", [1, 512], F32, ph2) for i in range(2)]
                    onesf = sb("onesf", [1, 64], F32, ph2)

                    seq = []
                    hq = []
                    for h in range(NH):
                        for qc in range(T // 512):
                            nkt = (qc + 1) * 512 // 128
                            hq.append((h, qc, nkt))
                            for kt in range(nkt):
                                seq.append((h, qc, kt, kt * 128 >= qc * 512))

                    with bk() as blk:
                        @blk.gpsimd
                        def _(g):
                            gate(g)
                            s_ax.dma(g.dma_start(kk[64:128, :], qkv[8][0:64, :]))
                            g.memset(onesf[:], 1.0)
                            last = None
                            for j, off in enumerate((0, 128, 256, 384)):
                                last = g.affine_select(
                                    dmask[:, j * 512:j * 512 + 512],
                                    onesq_b[:, 0:1].broadcast_to([128, 512]),
                                    [[1, 512]], ALU.is_ge, 0.0, base=-off,
                                    channel_multiplier=-1)
                            s_gp.inc(last)

                    with bk() as blk:
                        pe3, ac3, ve3, bc3, gp3 = s_pe.n, s_ac.n, s_ve.n, s_bc.n, s_gp.n
                        axc = [s_ax.n, s_g2.n]
                        odd_idx = {}
                        _c = 0
                        for _ih in range(len(hq)):
                            if hq[_ih][0] % 2 == 1:
                                odd_idx[_ih] = _c
                                p = _c % 2
                                axc[p] += 16
                                dmk[("amv", _ih)] = ((s_ax if p == 0 else s_g2).h, axc[p])
                                _c += 1
                        odd_list = sorted(odd_idx, key=lambda k: odd_idx[k])
                        n_diag_before = []
                        nd = 0
                        for (h, qc, kt, diag) in seq:
                            n_diag_before.append(nd)
                            if diag:
                                nd += 1

                        hq_of_tile = []
                        for i_hq, (h, qc, nkt) in enumerate(hq):
                            for kt in range(nkt):
                                hq_of_tile.append(i_hq)
                        last_tile_of = {}
                        for i, ihq in enumerate(hq_of_tile):
                            last_tile_of[ihq] = i

                        acpos = {}
                        _c = 0
                        for i in range(len(seq)):
                            _c += 1
                            acpos[("exp", i)] = _c
                            if i == last_tile_of[hq_of_tile[i]]:
                                _c += 1
                                acpos[("den", hq_of_tile[i])] = _c
                        pe_ops = []
                        for i in range(len(seq)):
                            pe_ops.append(("sc", i))
                            j = i - 2
                            if j >= 0:
                                pe_ops.append(("pv", j))
                                if j == last_tile_of[hq_of_tile[j]]:
                                    pe_ops.append(("bc", hq_of_tile[j]))
                        for j in (len(seq) - 2, len(seq) - 1):
                            pe_ops.append(("pv", j))
                            if j == last_tile_of[hq_of_tile[j]]:
                                pe_ops.append(("bc", hq_of_tile[j]))
                        pepos = {}
                        for _c, op in enumerate(pe_ops):
                            pepos[op] = _c + 1
                        vepos = {}
                        _c = 0
                        for i, (h, qc, kt, diag) in enumerate(seq):
                            if diag:
                                _c += 1
                                vepos[("mask", i)] = _c
                            if i == last_tile_of[hq_of_tile[i]]:
                                _c += 1
                                vepos[("rc", hq_of_tile[i])] = _c
                                _c += 1
                                vepos[("mul", hq_of_tile[i])] = _c

                        def ve_evac_cnt(i_hq):
                            return ve3 + vepos[("mul", i_hq)]

                        @blk.tensor
                        def _(t):
                            gate(t)
                            for op in pe_ops:
                                kind, i = op
                                if kind == "bc":
                                    t.wait_ge(s_ac.h, ac3 + acpos[("den", i)])
                                    if i >= 2:
                                        t.wait_ge(s_ve.h, ve3 + vepos[("rc", i - 2)])
                                    s_pe.inc(t.matmul(
                                        psum[4 + i % 2][0:64, 0:512],
                                        onesf[:], growf[i % 2][:],
                                        start=True, stop=True))
                                    continue
                                h, qc, kt, diag = seq[i]
                                jq, rq = h // 2, (h % 2) * 64
                                ihq = hq_of_tile[i]
                                if kind == "sc":
                                    if i >= 4:
                                        t.wait_ge(s_ac.h, ac3 + acpos[("exp", i - 4)])
                                    ksrc = (qkv[8][0:64, kt * 128:(kt + 1) * 128]
                                            if rq == 0
                                            else kk[64:128, kt * 128:(kt + 1) * 128])
                                    t.wait_ge(s_ax.h, s_ax.n)
                                    s_pe.inc(t.matmul(
                                        psum[i % 4][:, 0:512],
                                        ksrc,
                                        qkv[jq][rq:rq + 64, qc * 512:(qc + 1) * 512],
                                        start=True, stop=True))
                                else:
                                    if diag:
                                        t.wait_ge(s_ve.h, ve3 + vepos[("mask", i)])
                                    else:
                                        t.wait_ge(s_ac.h, ac3 + acpos[("exp", i)])
                                    if kt == 0 and i >= 1:
                                        prev = ihq - 2
                                        if prev >= 0:
                                            t.wait_ge(s_ve.h, ve_evac_cnt(prev))
                                    s_pe.inc(t.matmul(
                                        psum[6 + ihq % 2][0:65, 0:512],
                                        kvT[:, kt * 65:(kt + 1) * 65],
                                        ptile[i % 4][:],
                                        start=(kt == 0), stop=(kt == hq[ihq][2] - 1)))

                        @blk.scalar
                        def _(sc):
                            for i, (h, qc, kt, diag) in enumerate(seq):
                                ihq = hq_of_tile[i]
                                sc.wait_ge(s_pe.h, pe3 + pepos[("sc", i)])
                                if i >= 4:
                                    sc.wait_ge(s_pe.h, pe3 + pepos[("pv", i - 4)])
                                s_ac.inc(sc.activation(ptile[i % 4][:],
                                                       psum[i % 4][:, 0:512],
                                                       AF.Exp, scale=0.125))
                                if i == last_tile_of[ihq]:
                                    sc.wait_ge(s_pe.h, pe3 + pepos[("pv", i)])
                                    if ihq >= 2:
                                        sc.wait_ge(s_pe.h, pe3 + pepos[("bc", ihq - 2)])
                                    s_ac.inc(sc.activation(growf[ihq % 2][:],
                                                           psum[6 + ihq % 2][64:65, 0:512],
                                                           AF.Copy))

                        @blk.vector
                        def _(v):
                            for i, (h, qc, kt, diag) in enumerate(seq):
                                ihq = hq_of_tile[i]
                                if diag:
                                    v.wait_ge(s_ac.h, ac3 + acpos[("exp", i)])
                                    moff = (kt * 128 - qc * 512) // 128
                                    s_ve.inc(v.tensor_tensor(
                                        ptile[i % 4][:], ptile[i % 4][:],
                                        dmask[:, moff * 512:(moff + 1) * 512], ALU.mult))
                                if i == last_tile_of[ihq]:
                                    h_, qc_ = hq[ihq][0], hq[ihq][1]
                                    jq, rq = h_ // 2, (h_ % 2) * 64
                                    v.wait_ge(s_pe.h, pe3 + pepos[("bc", ihq)])
                                    s_ve.inc(v.reciprocal(arow[:],
                                                          psum[4 + ihq % 2][0:64, 0:512]))
                                    v.wait_ge(s_ve.h, s_ve.n)
                                    if rq == 0:
                                        s_ve.inc(v.tensor_tensor(
                                            xn1[jq][0:64, qc_ * 512:(qc_ + 1) * 512],
                                            psum[6 + ihq % 2][0:64, 0:512], arow[:],
                                            ALU.mult))
                                    else:
                                        oi = odd_idx[ihq]
                                        if oi >= 2:
                                            dma_wait(v, ("amv", odd_list[oi - 2]))
                                        s_ve.inc(v.tensor_tensor(
                                            atto[oi % 2][:],
                                            psum[6 + ihq % 2][0:64, 0:512], arow[:],
                                            ALU.mult))

                        @blk.gpsimd
                        def _(g):
                            for i_hq in odd_list:
                                oi = odd_idx[i_hq]
                                h_, qc_ = hq[i_hq][0], hq[i_hq][1]
                                g.wait_ge(s_ve.h, ve_evac_cnt(i_hq))
                                ((s_ax if oi % 2 == 0 else s_g2)).dma(
                                    g.dma_start(
                                    xn1[h_ // 2][64:128, qc_ * 512:(qc_ + 1) * 512],
                                    atto[oi % 2][:]))

            ln_scale_calc(marks1, None, 8, 9)

            xq1 = [sb(f"xq1_{i}", [128, T], BF16, pl1) for i in range(FT)]
            with ExitStack() as ph:
                big32 = [sb(f"bgq1{i}", [128, T], F32, ph) for i in range(1)]
                ln_quant_sweep(xq1, big32, bc_m1, bc_r1, sinv_col=scmc[:, 8:9])

            with ExitStack() as ph2:
                wprj = [sb(f"wprj{k}", [128, D], BF16, ph2) for k in range(FT)]
                with bk() as blk:
                    pe1, ve1 = s_pe.n, s_ve.n

                    @blk.sync
                    def _(sy):
                        gate(sy)
                        for k in range(FT):
                            s_lw.dma(sy.dma_start(wprj[k][:], S["wprojT"][k]))

                    @blk.tensor
                    def _(t):
                        gate(t)
                        t.wait_ge(s_lw.h, s_lw.n)
                        for dm in range(FT):
                            for ch in range(N5):
                                q = dm * N5 + ch
                                if q >= 4:
                                    t.wait_ge(s_ve.h, ve1 + q - 3)
                                for k in range(FT):
                                    i = t.matmul(psum[q % 4][:, 0:C5],
                                                 wprj[k][:, dm * 128:(dm + 1) * 128],
                                                 xn1[k][:, ch * C5:(ch + 1) * C5],
                                                 start=(k == 0), stop=(k == FT - 1))
                                s_pe.inc(i)

                    @blk.vector
                    def _(v):
                        for dm in range(FT):
                            for ch in range(N5):
                                q = dm * N5 + ch
                                v.wait_ge(s_pe.h, pe1 + q + 1)
                                s_ve.inc(v.tensor_tensor(
                                    res[dm][:, ch * C5:(ch + 1) * C5],
                                    res[dm][:, ch * C5:(ch + 1) * C5],
                                    psum[q % 4][:, 0:C5], ALU.add))


            dbg_dump(3)

            with ExitStack() as ph:
                fwt = [sb(f"fwt{k}", [128, D], BF16, ph) for k in range(FT)]
                with bk() as blk:
                    pe1, ve1 = s_pe.n, s_ve.n

                    @blk.sync
                    def _(sy):
                        gate(sy)
                        for k in range(FT):
                            s_lw.dma(sy.dma_start(fwt[k][:], S["fwT"][k]))

                    @blk.vector
                    def _(v):
                        gate(v)
                        s_ve.inc(v.tensor_tensor(stat[:, 56:56 + FT], fsc[:, 0:FT],
                                                 scmc[:, 9:10].broadcast_to([128, FT]),
                                                 ALU.mult))
                        for dm in range(FT):
                            s_ve.inc(v.tensor_scalar(res[dm][:], res[dm][:],
                                                     fbc[:, dm:dm + 1], None, ALU.add))

                    ve2b = s_ve.n
                    @blk.tensor
                    def _(t):
                        gate(t)
                        t.wait_ge(s_lw.h, s_lw.n)
                        t.wait_ge(s_ve.h, s_ve.n)
                        for dm in range(FT):
                            for ch in range(N5):
                                q = dm * N5 + ch
                                if q >= 4:
                                    t.wait_ge(s_ve.h, ve2b + q - 3)
                                for k in range(FT):
                                    i = t.matmul(psum[q % 4][:, 0:C5],
                                                 fwt[k][:, dm * 128:(dm + 1) * 128],
                                                 xq1[k][:, ch * C5:(ch + 1) * C5],
                                                 start=(k == 0), stop=(k == FT - 1))
                                s_pe.inc(i)

                    @blk.vector
                    def _(v):
                        for dm in range(FT):
                            for ch in range(N5):
                                q = dm * N5 + ch
                                v.wait_ge(s_pe.h, pe1 + q + 1)
                                s_ve.inc(v.scalar_tensor_tensor(
                                    res[dm][:, ch * C5:(ch + 1) * C5],
                                    psum[q % 4][:, 0:C5], stat[:, 56 + dm:57 + dm],
                                    res[dm][:, ch * C5:(ch + 1) * C5],
                                    ALU.mult, ALU.add))

        dbg_dump(4)

        with ExitStack() as ph:
            ostg = [sb(f"ostg{i}", [128, D], F32, ph) for i in range(2)]
            with bk() as blk:
                pe1, ac1, st1 = s_pe.n, s_ac.n, s_st.n
                fwd_marks(("ost",), [s_st, s_s2], [s_st.n, s_s2.n], NTT)

                @blk.tensor
                def _(t):
                    gate(t)
                    for i in range(NTT):
                        for j in range(FT):
                            q = i * FT + j
                            if q >= 4:
                                t.wait_ge(s_ac.h, ac1 + q - 3)
                            s_pe.inc(t.transpose(psum[q % 4][:, 0:128],
                                                 res[j][:, i * 128:(i + 1) * 128],
                                                 ident_f[:]))

                @blk.scalar
                def _(sc):
                    for i in range(NTT):
                        if i >= 2:
                            dma_wait(sc, ("ost", i - 2))
                        for j in range(FT):
                            q = i * FT + j
                            sc.wait_ge(s_pe.h, pe1 + q + 1)
                            s_ac.inc(sc.copy(ostg[i % 2][:, j * 128:(j + 1) * 128],
                                             psum[q % 4][:, 0:128]))

                @blk.sync
                def _(sy):
                    for i in range(NTT):
                        sy.wait_ge(s_ac.h, ac1 + (i + 1) * FT)
                        (s_st if i % 2 == 0 else s_s2).dma(
                            sy.dma_start(out_h[i * 128:(i + 1) * 128, :],
                                         ostg[i % 2][:]))
                    sy.wait_ge(s_st.h, s_st.n)
                    sy.wait_ge(s_s2.h, s_s2.n)

        nc.all_engine_barrier()
        for s in sems.values():
            nc.gpsimd.sem_clear(s.h)
        nc.all_engine_barrier()

    return nc


_NAME_MAP = {
    "L0_attn_W_in": "i_W_in", "L0_attn_conv_w": "i_conv_w", "L0_attn_conv_b": "i_conv_b",
    "L0_attn_Wx": "i_Wx", "L0_attn_Wdt": "i_Wdt", "L0_attn_bdt": "i_bdt",
    "L0_attn_A_log": "i_A_log", "L0_attn_D": "i_Dd",
    "L0_attn_conv_w_b": "i_conv_w_b", "L0_attn_conv_b_b": "i_conv_b_b",
    "L0_attn_Wx_b": "i_Wx_b", "L0_attn_Wdt_b": "i_Wdt_b", "L0_attn_bdt_b": "i_bdt_b",
    "L0_attn_A_b_log": "i_A_b_log", "L0_attn_D_b": "i_Dd_b",
    "L0_attn_W_out": "i_W_out",
    "L0_ffn_clip_x": "i_clip_x", "L0_ffn_clip_down": "i_clip_down",
    "L0_ffn_w_gate": "i_w_gate", "L0_ffn_w_up": "i_w_up", "L0_ffn_w_down": "i_w_down",
    "L1_attn_w_attn": "i_w_attn", "L1_attn_w_proj": "i_w_proj",
    "L1_ffn_weight_int8": "i_fw_int8", "L1_ffn_weight_scale": "i_fw_scale",
    "L1_ffn_bias": "i_fbias",
}

_NC_CACHE = {}


def kernel(**inputs):
    B = inputs["L0_attn_hidden"].shape[0]
    T = inputs["L0_attn_hidden"].shape[1]
    if T not in _NC_CACHE:
        _NC_CACHE[T] = build_nc(T=T, n_cores=B)
    nc = _NC_CACHE[T]
    shared = {}
    for k, v in inputs.items():
        if k == "L0_attn_hidden":
            continue
        a = np.ascontiguousarray(v)
        if k == "L0_attn_conv_w" or k == "L0_attn_conv_w_b":
            pass
        shared[_NAME_MAP[k]] = a
    in_maps = []
    for b in range(B):
        m = dict(shared)
        m["x"] = np.ascontiguousarray(inputs["L0_attn_hidden"][b])
        in_maps.append(m)
    r = run_bass_kernel_spmd(nc, in_maps, core_ids=list(range(B)))
    return np.stack([r.results[b]["out"] for b in range(B)], axis=0)


# version 19 (final)
# speedup vs baseline: 1.1779x; 1.0103x over previous
import numpy as np
from contextlib import ExitStack

import concourse.bass as bass
import concourse.mybir as mybir
from concourse.bass_utils import run_bass_kernel_spmd

F32 = mybir.dt.float32
BF16 = mybir.dt.bfloat16
I8 = mybir.dt.int8
ALU = mybir.AluOpType
AF = mybir.ActivationFunctionType
AX = mybir.AxisListType.X

MAGIC = 12582912.0
QMAX = 127.0

D, DI, DS, DTR, DC, IF, NH, HD = 1024, 2048, 16, 64, 4, 2816, 16, 64
FT, NDT, NIT = D // 128, DI // 128, IF // 128



def _dap(src, offset, dims):
    return bass.AP(src.tensor, src.offset + offset, [list(d) for d in dims])


def build_nc(T=2048, n_cores=8, dbg=False):
    TC = min(512, T)
    NCH = T // TC
    NTT = T // 128
    NKT = T // 128
    N5 = T // 512 if T >= 512 else 1
    C5 = min(512, T)

    nc = bass.Bass("TRN2", target_bir_lowering=False, debug=False)

    x_h = nc.dram_tensor("x", [T, D], F32, kind="ExternalInput").ap()
    out_h = nc.dram_tensor("out", [T, D], F32, kind="ExternalOutput").ap()
    W = {}
    for name, shape, dt in [
        ("W_in", [2 * DI, D], F32), ("conv_w", [DI, 1, DC], F32), ("conv_b", [DI], F32),
        ("Wx", [DTR + 2 * DS, DI], F32), ("Wdt", [DI, DTR], F32), ("bdt", [DI], F32),
        ("A_log", [DI, DS], F32), ("Dd", [DI], F32),
        ("conv_w_b", [DI, 1, DC], F32), ("conv_b_b", [DI], F32),
        ("Wx_b", [DTR + 2 * DS, DI], F32), ("Wdt_b", [DI, DTR], F32), ("bdt_b", [DI], F32),
        ("A_b_log", [DI, DS], F32), ("Dd_b", [DI], F32),
        ("W_out", [D, DI], F32),
        ("clip_x", [1], F32), ("clip_down", [1], F32),
        ("w_gate", [IF, D], F32), ("w_up", [IF, D], F32), ("w_down", [D, IF], F32),
        ("w_attn", [D, D + 2 * HD], F32), ("w_proj", [D, D], F32),
        ("fw_int8", [D, D], I8), ("fw_scale", [D, 1], F32), ("fbias", [D], F32),
    ]:
        W[name] = nc.dram_tensor("i_" + name, shape, dt, kind="ExternalInput").ap()

    S = {
        "WinT": nc.dram_tensor("s_WinT", [FT, 128, 2 * DI], BF16).ap(),
        "wgT": nc.dram_tensor("s_wgT", [FT, 128, IF], BF16).ap(),
        "wuT": nc.dram_tensor("s_wuT", [FT, 128, IF], BF16).ap(),
        "wdT": nc.dram_tensor("s_wdT", [NIT, 128, D], BF16).ap(),
        "woutT": nc.dram_tensor("s_woutT", [NDT, 128, D], BF16).ap(),
        "wattnT": nc.dram_tensor("s_wattnT", [FT, 128, D + 2 * HD], BF16).ap(),
        "wprojT": nc.dram_tensor("s_wprojT", [FT, 128, D], BF16).ap(),
        "fwT": nc.dram_tensor("s_fwT", [FT, 128, D], BF16).ap(),
        "wxs": nc.dram_tensor("s_wxs", [128, NDT * 96 * 2], BF16).ap(),
        "wdts": nc.dram_tensor("s_wdts", [64, NDT * 128 * 2], BF16).ap(),
        "xz": nc.dram_tensor("s_xz", [2 * NDT, 128, T], BF16).ap(),
        "hpre": nc.dram_tensor("s_hpre", [NIT, 128, T], BF16).ap(),
        "outf": nc.dram_tensor("s_outf", [NDT, 128, T], BF16).ap(),
        "outc": nc.dram_tensor("s_outc", [NDT, 128, T], BF16).ap(),
    }
    for i in range(3):
        S[f"cci{i}"] = nc.dram_tensor(f"s_cci{i}", [1, 8], F32).ap()
        S[f"cco{i}"] = nc.dram_tensor(f"s_cco{i}", [1, 8], F32).ap()
    if dbg:
        for i in range(8):
            S[f"dbg{i}"] = nc.dram_tensor(f"s_dbg{i}", [FT, 128, T], F32).ap()

    ctx = ExitStack()
    sems = {}

    class Sem:
        def __init__(self, name):
            self.h = ctx.enter_context(nc.semaphore(name))
            self.n = 0

        def inc(self, inst, k=1):
            inst.then_inc(self.h, k)
            self.n += k
            return inst

        def dma(self, inst):
            return self.inc(inst, 16)

    def sem(name):
        if name not in sems:
            sems[name] = Sem(name)
        return sems[name]

    s_ld = sem("ld"); s_lw = sem("lw"); s_pe = sem("pe"); s_ve = sem("ve")
    s_ac = sem("ac"); s_gp = sem("gp"); s_sp = sem("sp"); s_wp = sem("wp")
    s_bc = sem("bc"); s_cc = sem("cc"); s_st = sem("st"); s_ax = sem("ax")
    s_l2 = sem("l2"); s_w2 = sem("w2"); s_p2 = sem("p2"); s_s2 = sem("s2")
    s_cx = sem("cx"); s_gd = sem("gd"); s_g2 = sem("g2")
    s_o1 = sem("o1"); s_o2 = sem("o2")

    dmk = {}

    def dma_mark(key, semx, inst):
        semx.dma(inst)
        dmk[key] = (semx.h, semx.n)

    def dma_wait(eng, key):
        h, n = dmk[key]
        eng.wait_ge(h, n)

    def fwd_marks(key, sems2, base, count, k=16):
        cnt = list(base)
        for i in range(count):
            p = i % 2
            cnt[p] += k
            dmk[key + (i,)] = (sems2[p].h, cnt[p])

    _snap = [[]]

    def bk():
        _snap[0] = [(s.h, s.n) for s in sems.values()
                    if s.n > 0 and s is not s_cx]
        return nc.Block()

    def gate(*engines):
        for eng in engines:
            for h, n in _snap[0]:
                eng.wait_ge(h, n)

    def sb(name, shape, dt, st=None):
        return (st or ctx).enter_context(nc.sbuf_tensor(name, list(shape), dt))

    with ctx:
        res = [sb(f"res{i}", [128, T], F32) for i in range(FT)]
        stat = sb("stat", [128, 64], F32)
        ident_b = sb("ident_b", [128, 128], BF16)
        ident_f = sb("ident_f", [128, 128], F32)
        onesq_b = sb("onesq_b", [128, 128], BF16)
        onecol_b = sb("onecol_b", [128, 1], BF16)
        scm = sb("scm", [1, 16], F32)
        colA = sb("colA", [128, 8], F32)
        a_cols = sb("a_cols", [128, NDT * DS * 2], F32)
        convw = sb("convw", [128, NDT * DC * 2], F32)
        convb = sb("convb", [128, NDT * 2], F32)
        dcol = sb("dcol", [128, NDT * 2], F32)
        bdtc = sb("bdtc", [128, NDT * 2], F32)
        wgs = sb("wgs", [128, 2 * NIT], F32)
        wds = sb("wds", [128, FT], F32)
        fsc = sb("fsc", [128, FT], F32)
        fbc = sb("fbc", [128, FT], F32)
        scmc = sb("scmc", [128, 16], F32)
        psum = [ctx.enter_context(nc.psum_tensor(f"ps{i}", [128, 512], F32))
                for i in range(8)]

        def psb(i, r, w):
            return psum[i][:].bitcast(BF16)[0:r, 0:w]

        def dbg_dump(idx):
            if not dbg:
                return
            with bk() as blk:
                @blk.gpsimd
                def _(g):
                    gate(g)
                    for j in range(FT):
                        s_gd.dma(g.dma_start(S[f"dbg{idx}"][j], res[j][:]))
                    g.wait_ge(s_gd.h, s_gd.n)

        for s in sems.values():
            nc.gpsimd.sem_clear(s.h)
        nc.all_engine_barrier()

        with bk() as blk:
            @blk.gpsimd
            def _(g):
                s_gp.inc(g.memset(scm[:], 0.0))
                s_gp.inc(g.memset(stat[:], 0.0))
                s_gp.inc(g.memset(onesq_b[:], 1.0))
                s_gp.inc(g.memset(onecol_b[:], 1.0))
                s_gp.inc(g.memset(ident_b[:], 0.0))
                s_gp.inc(g.memset(ident_f[:], 0.0))
                g.wait_ge(s_gp.h, s_gp.n)
                s_gp.inc(g.affine_select(ident_b[:], onesq_b[:], [[1, 128]], ALU.is_equal,
                                         0.0, base=0, channel_multiplier=-1))
                s_gp.inc(g.affine_select(ident_f[:], onesq_b[:], [[1, 128]], ALU.is_equal,
                                         0.0, base=0, channel_multiplier=-1))

            @blk.sync
            def _(sy):
                sy.wait_ge(s_gp.h, 2)
                for dirn in range(2):
                    cw = W["conv_w"] if dirn == 0 else W["conv_w_b"]
                    cb = W["conv_b"] if dirn == 0 else W["conv_b_b"]
                    al = W["A_log"] if dirn == 0 else W["A_b_log"]
                    dd = W["Dd"] if dirn == 0 else W["Dd_b"]
                    bd = W["bdt"] if dirn == 0 else W["bdt_b"]
                    o = dirn * NDT
                    s_lw.dma(sy.dma_start(convw[:, o * DC:(o + NDT) * DC],
                                          _dap(cw, 0, [[4, 128], [512, NDT], [1, 4]])))
                    s_lw.dma(sy.dma_start(convb[:, o:o + NDT],
                                          _dap(cb, 0, [[1, 128], [128, NDT]]),
                                          allow_slow_non_contiguous=True))
                    s_lw.dma(sy.dma_start(a_cols[:, o * DS:(o + NDT) * DS],
                                          _dap(al, 0, [[DS, 128], [128 * DS, NDT], [1, DS]])))
                    s_lw.dma(sy.dma_start(dcol[:, o:o + NDT],
                                          _dap(dd, 0, [[1, 128], [128, NDT]]),
                                          allow_slow_non_contiguous=True))
                    s_lw.dma(sy.dma_start(bdtc[:, o:o + NDT],
                                          _dap(bd, 0, [[1, 128], [128, NDT]]),
                                          allow_slow_non_contiguous=True))
                s_lw.dma(sy.dma_start(scm[:, 0:1], W["clip_x"].unsqueeze(0)))
                s_lw.dma(sy.dma_start(scm[:, 1:2], W["clip_down"].unsqueeze(0)))
                s_lw.dma(sy.dma_start(fsc[:], _dap(W["fw_scale"], 0, [[1, 128], [128, FT]]),
                                      allow_slow_non_contiguous=True))
                s_lw.dma(sy.dma_start(fbc[:], _dap(W["fbias"], 0, [[1, 128], [128, FT]]),
                                      allow_slow_non_contiguous=True))

        with bk() as blk:
            @blk.scalar
            def _(sc):
                gate(sc)
                s_ac.inc(sc.activation(a_cols[:], a_cols[:], AF.Exp))
                s_ac.inc(sc.activation(scm[:, 2:4], scm[:, 0:2], AF.Exp, scale=-1.0))

            @blk.vector
            def _(v):
                gate(v)
                v.wait_ge(s_ac.h, s_ac.n)
                s_ve.inc(v.tensor_scalar(a_cols[:], a_cols[:], -1.0, None, ALU.mult))
                s_ve.inc(v.tensor_scalar(scm[:, 2:4], scm[:, 2:4], 1.0, None, ALU.add))
                v.wait_ge(s_ve.h, s_ve.n)
                s_ve.inc(v.reciprocal(scm[:, 2:4], scm[:, 2:4]))

        with ExitStack() as ph:
            xstg = [sb(f"xstg{i}", [128, D], F32, ph) for i in range(2)]
            with bk() as blk:
                ld0, ve0, pe0 = s_ld.n, s_ve.n, s_pe.n

                @blk.sync
                def _(sy):
                    for i in range(NTT):
                        if i >= 2:
                            sy.wait_ge(s_pe.h, pe0 + (i - 1) * FT)
                        dma_mark(("x", i), s_ld if i % 2 == 0 else s_l2,
                                 sy.dma_start(xstg[i % 2][:], x_h[i * 128:(i + 1) * 128, :]))

                @blk.tensor
                def _(t):
                    t.wait_ge(s_gp.h, 8)
                    for i in range(NTT):
                        dma_wait(t, ("x", i))
                        for j in range(FT):
                            q = i * FT + j
                            if q >= 4:
                                t.wait_ge(s_ve.h, ve0 + q - 3)
                            s_pe.inc(t.transpose(psum[q % 4][:, 0:128],
                                                 xstg[i % 2][:, j * 128:(j + 1) * 128],
                                                 ident_f[:]))

                @blk.vector
                def _(v):
                    for i in range(NTT):
                        for j in range(FT):
                            q = i * FT + j
                            v.wait_ge(s_pe.h, pe0 + q + 1)
                            s_ve.inc(v.tensor_copy(res[j][:, i * 128:(i + 1) * 128],
                                                   psum[q % 4][:, 0:128]))

        with ExitStack() as ph:
            wld = [sb(f"wld{i}", [128, 3072], F32, ph) for i in range(2)]
            wstg = [sb(f"wstg{i}", [128, 3072], BF16, ph) for i in range(2)]
            wstr = [sb(f"wstr{i}", [128, 3072], BF16, ph) for i in range(2)]

            def prep_weight(src_ap, n_rt, n_cols, dst, quant=False, rs_store=None,
                            scale_const=None, src_i8=False):
                NC = n_cols // 128
                with bk() as blk:
                    lw0, ve1, pe1, ac1, wp1 = s_lw.n, s_ve.n, s_pe.n, s_ac.n, s_wp.n
                    p21 = s_p2.n
                    fwd_marks(("pws",), [s_wp, s_p2], [s_wp.n, s_p2.n], n_rt)

                    @blk.sync
                    def _(sy):
                        for r in range(n_rt):
                            sy.wait_ge(s_ve.h, ve1 + ((r - 1) * (5 if quant else 1)
                                                      if r >= 2 else 0))
                            dstb = (wld[r % 2][:, 0:n_cols] if not src_i8 else
                                    wld[r % 2].bitcast(I8)[:, 0:n_cols])
                            dma_mark(("pw", r), s_lw if r % 2 == 0 else s_w2,
                                     sy.dma_start(dstb, src_ap[r * 128:(r + 1) * 128, :]))

                    @blk.vector
                    def _(v):
                        for r in range(n_rt):
                            dma_wait(v, ("pw", r))
                            if r >= 2:
                                v.wait_ge(s_pe.h, pe1 + (r - 1) * NC)
                            src = (wld[r % 2][:, 0:n_cols] if not src_i8 else
                                   wld[r % 2].bitcast(I8)[:, 0:n_cols])
                            if quant:
                                s_ve.inc(v.tensor_reduce(colA[:, 0:1], src, AX, ALU.max,
                                                         apply_absolute_value=True))
                                v.wait_ge(s_ve.h, s_ve.n)
                                s_ve.inc(v.reciprocal(colA[:, 1:2], colA[:, 0:1]))
                                v.wait_ge(s_ve.h, s_ve.n)
                                s_ve.inc(v.tensor_scalar(colA[:, 1:2], colA[:, 1:2],
                                                         QMAX, None, ALU.mult))
                                if rs_store is not None:
                                    buf, c0 = rs_store
                                    v.tensor_scalar(buf[:, c0 + r:c0 + r + 1], colA[:, 0:1],
                                                    1.0 / QMAX, None, ALU.mult)
                                v.wait_ge(s_ve.h, s_ve.n)
                                s_ve.inc(v.tensor_scalar(wld[r % 2][:, 0:n_cols], src,
                                                         colA[:, 1:2], MAGIC, ALU.mult, ALU.add))
                                v.wait_ge(s_ve.h, s_ve.n)
                                i = v.tensor_scalar(wstg[r % 2][:, 0:n_cols],
                                                    wld[r % 2][:, 0:n_cols],
                                                    MAGIC, None, ALU.subtract)
                                s_ve.inc(i)
                            elif scale_const is not None:
                                s_ve.inc(v.tensor_scalar(wstg[r % 2][:, 0:n_cols], src,
                                                         scale_const, None, ALU.mult))
                            else:
                                s_ve.inc(v.tensor_copy(wstg[r % 2][:, 0:n_cols], src))

                    k_ve = 5 if quant else 1
                    @blk.tensor
                    def _(t):
                        for r in range(n_rt):
                            t.wait_ge(s_ve.h, ve1 + (r + 1) * k_ve)
                            if r >= 2:
                                dma_wait(t, ("pws", r - 2))
                            else:
                                t.wait_ge(s_wp.h, wp1)
                                t.wait_ge(s_p2.h, p21)
                            for c in range(NC):
                                q = r * NC + c
                                if q >= 2:
                                    t.wait_ge(s_ac.h, ac1 + q - 1)
                                s_pe.inc(t.transpose(psb(4 + q % 2, 128, 128),
                                                     wstg[r % 2][:, c * 128:(c + 1) * 128],
                                                     ident_b[:]))

                    @blk.scalar
                    def _(sc):
                        for r in range(n_rt):
                            for c in range(NC):
                                q = r * NC + c
                                sc.wait_ge(s_pe.h, pe1 + q + 1)
                                s_ac.inc(sc.copy(wstr[r % 2][:, c * 128:(c + 1) * 128],
                                                 psb(4 + q % 2, 128, 128)))

                    @blk.gpsimd
                    def _(g):
                        for r in range(n_rt):
                            g.wait_ge(s_ac.h, ac1 + (r + 1) * NC)
                            WT = dst.shape[2]
                            (s_wp if r % 2 == 0 else s_p2).dma(g.dma_start(
                                _dap(dst, r * 128, [[WT, 128], [128 * WT, NC], [1, 128]]),
                                wstr[r % 2][:, 0:NC * 128]))

            def prep_cast(src_ap, n_rt, n_cols, dst):
                with bk() as blk:
                    lw0, ve1, wp1 = s_lw.n, s_ve.n, s_wp.n
                    p21 = s_p2.n
                    fwd_marks(("pcs",), [s_wp, s_p2], [s_wp.n, s_p2.n], n_rt)

                    @blk.sync
                    def _(sy):
                        for r in range(n_rt):
                            sy.wait_ge(s_ve.h, ve1 + (r - 1 if r >= 2 else 0))
                            dma_mark(("pc", r), s_lw if r % 2 == 0 else s_w2,
                                     sy.dma_start(wld[r % 2][:, 0:n_cols],
                                                  src_ap[r * 128:(r + 1) * 128, :]))

                    @blk.vector
                    def _(v):
                        for r in range(n_rt):
                            dma_wait(v, ("pc", r))
                            if r >= 2:
                                dma_wait(v, ("pcs", r - 2))
                            else:
                                v.wait_ge(s_wp.h, wp1)
                                v.wait_ge(s_p2.h, p21)
                            s_ve.inc(v.tensor_copy(wstg[r % 2][:, 0:n_cols],
                                                   wld[r % 2][:, 0:n_cols]))

                    @blk.gpsimd
                    def _(g):
                        for r in range(n_rt):
                            g.wait_ge(s_ve.h, ve1 + r + 1)
                            (s_wp if r % 2 == 0 else s_p2).dma(
                                g.dma_start(dst[r], wstg[r % 2][:, 0:n_cols]))

            prep_weight(W["W_in"], 2 * DI // 128, D, S["WinT"])
            prep_weight(W["w_gate"], NIT, D, S["wgT"], quant=True, rs_store=(wgs, 0))
            prep_weight(W["w_up"], NIT, D, S["wuT"], quant=True, rs_store=(wgs, NIT))
            prep_weight(W["w_down"], FT, IF, S["wdT"], quant=True, rs_store=(wds, 0))
            prep_weight(W["W_out"], FT, DI, S["woutT"], scale_const=0.5)
            prep_weight(W["fw_int8"], FT, D, S["fwT"], src_i8=True)
            prep_cast(W["w_attn"], FT, D + 2 * HD, S["wattnT"])
            prep_cast(W["w_proj"], FT, D, S["wprojT"])

            wxTp = sb("wxTp", [128, NDT * 96 * 2], BF16, ph)
            wdtTp = sb("wdtTp", [64, NDT * 128 * 2], BF16, ph)
            for dirn in range(2):
                wx_src = W["Wx"] if dirn == 0 else W["Wx_b"]
                wd_src = W["Wdt"] if dirn == 0 else W["Wdt_b"]
                with bk() as blk:
                    ve1, pe1, ac1w = s_ve.n, s_pe.n, s_ac.n

                    @blk.sync
                    def _(sy):
                        gate(sy)
                        s_lw.dma(sy.dma_start(wld[0][0:96, 0:DI], wx_src))
                        s_lw.dma(sy.dma_start(
                            wld[1][:, 0:NDT * DTR],
                            _dap(wd_src, 0, [[DTR, 128], [128 * DTR, NDT], [1, DTR]])))

                    @blk.vector
                    def _(v):
                        v.wait_ge(s_lw.h, s_lw.n)
                        s_ve.inc(v.tensor_copy(wstg[0][0:96, 0:DI], wld[0][0:96, 0:DI]))
                        s_ve.inc(v.tensor_copy(wstg[1][:, 0:NDT * DTR],
                                               wld[1][:, 0:NDT * DTR]))

                    @blk.tensor
                    def _(t):
                        t.wait_ge(s_ve.h, ve1 + 2)
                        for u in range(NDT):
                            if u >= 2:
                                t.wait_ge(s_ac.h, ac1w + u - 1)
                            s_pe.inc(t.transpose(psb(4 + u % 2, 128, 96),
                                                 wstg[0][0:96, u * 128:(u + 1) * 128],
                                                 ident_b[0:96, 0:96]))
                        for u in range(NDT):
                            if u >= 2:
                                t.wait_ge(s_ac.h, ac1w + NDT + u - 1)
                            s_pe.inc(t.transpose(psb(6 + u % 2, 64, 128),
                                                 wstg[1][:, u * DTR:(u + 1) * DTR],
                                                 ident_b[:]))

                    @blk.scalar
                    def _(sc):
                        k = pe1
                        for u in range(NDT):
                            k += 1
                            sc.wait_ge(s_pe.h, k)
                            s_ac.inc(sc.copy(wxTp[:, (dirn * NDT + u) * 96:
                                                   (dirn * NDT + u + 1) * 96],
                                             psb(4 + u % 2, 128, 96)))
                        for u in range(NDT):
                            k += 1
                            sc.wait_ge(s_pe.h, k)
                            s_ac.inc(sc.copy(wdtTp[0:64, (dirn * NDT + u) * 128:
                                                    (dirn * NDT + u + 1) * 128],
                                             psb(6 + u % 2, 64, 128)))

            with bk() as blk:
                @blk.gpsimd
                def _(g):
                    gate(g)
                    s_wp.dma(g.dma_start(S["wxs"], wxTp[:]))
                    s_wp.dma(g.dma_start(S["wdts"], wdtTp[0:64, :]))

        def ln_stats_scale(xn_t, sqt, big32, bc_m, bc_r, grow, rowm, rowr, rowt,
                           cc_idx, sig_col, sinv_slot, s_slot, finish=True):
            with bk() as blk:
                ve1, ac1, pe1 = s_ve.n, s_ac.n, s_pe.n

                @blk.vector
                def _(v):
                    gate(v)
                    for j in range(FT):
                        s_ve.inc(v.tensor_copy(xn_t[j][:], res[j][:]))

                @blk.scalar
                def _(sc):
                    gate(sc)
                    for j in range(FT):
                        sc.wait_ge(s_ve.h, ve1 + j + 1)
                        if j >= 1:
                            sc.wait_ge(s_pe.h, pe1 + j * (2 * N5))
                        s_ac.inc(sc.activation(sqt[:], xn_t[j][:], AF.Square))
                        sc.wait_ge(s_ac.h, s_ac.n)

                @blk.tensor
                def _(t):
                    gate(t)
                    for j in range(FT):
                        t.wait_ge(s_ve.h, ve1 + j + 1)
                        t.wait_ge(s_ac.h, ac1 + j + 1)
                        for b in range(N5):
                            t.matmul(psum[b][0:1, 0:C5], onecol_b[:],
                                     xn_t[j][:, b * C5:(b + 1) * C5],
                                     start=(j == 0), stop=(j == FT - 1))
                            i = t.matmul(psum[4 + b][0:1, 0:C5], onecol_b[:],
                                         sqt[:, b * C5:(b + 1) * C5],
                                         start=(j == 0), stop=(j == FT - 1))
                            s_pe.inc(i, 2)

                @blk.vector
                def _(v):
                    v.wait_ge(s_pe.h, pe1 + FT * 2 * N5)
                    for b in range(N5):
                        s_ve.inc(v.tensor_scalar(rowm[:, b * C5:(b + 1) * C5],
                                                 psum[b][0:1, 0:C5], 1.0 / D, None, ALU.mult))
                        s_ve.inc(v.tensor_scalar(rowt[:, b * C5:(b + 1) * C5],
                                                 psum[4 + b][0:1, 0:C5], 1.0 / D, None, ALU.mult))
                    v.wait_ge(s_ve.h, s_ve.n)
                    s_ve.inc(v.tensor_tensor(rowr[:], rowm[:], rowm[:], ALU.mult))
                    v.wait_ge(s_ve.h, s_ve.n)
                    s_ve.inc(v.tensor_tensor(rowr[:], rowt[:, 0:T], rowr[:], ALU.subtract))
                    v.wait_ge(s_ve.h, s_ve.n)
                    s_ve.inc(v.tensor_scalar(rowr[:], rowr[:], 1e-5, None, ALU.add))

                @blk.scalar
                def _(sc):
                    sc.wait_ge(s_ve.h, s_ve.n)
                    s_ac.inc(sc.activation(rowr[:], rowr[:], AF.Ln))
                    sc.wait_ge(s_ac.h, s_ac.n)
                    s_ac.inc(sc.activation(rowr[:], rowr[:], AF.Exp, scale=-0.5))

                @blk.gpsimd
                def _(g):
                    g.wait_ge(s_ac.h, s_ac.n)
                    s_bc.dma(g.dma_start(bc_m[:], bass.AP(rowm, 0, [[1, 1], [0, 128], [1, T]])))
                    s_bc.dma(g.dma_start(bc_r[:], bass.AP(rowr, 0, [[1, 1], [0, 128], [1, T]])))

                @blk.vector
                def _(v):
                    v.wait_ge(s_bc.h, s_bc.n)
                    for j in range(FT):
                        v.wait_ge(s_ve.h, s_ve.n)
                        s_ve.inc(v.tensor_tensor(big32[0][:], res[j][:], bc_m[:], ALU.subtract))
                        v.wait_ge(s_ve.h, s_ve.n)
                        s_ve.inc(v.tensor_tensor(big32[0][:], big32[0][:], bc_r[:], ALU.mult))
                        v.wait_ge(s_ve.h, s_ve.n)
                        s_ve.inc(v.tensor_reduce(stat[:, j:j + 1], big32[0][:], AX, ALU.max,
                                                 apply_absolute_value=True))
                        v.wait_ge(s_ve.h, s_ve.n)
                        s_ve.inc(v.tensor_copy(xn_t[j][:], big32[0][:]))

                @blk.gpsimd
                def _(g):
                    g.wait_ge(s_ve.h, s_ve.n)
                    s_ax.dma(g.dma_start(grow[:, 0:128 * FT], stat[:, 0:FT]))

                @blk.vector
                def _(v):
                    v.wait_ge(s_ax.h, s_ax.n)
                    s_ve.inc(v.tensor_reduce(rowt[:, 0:1], grow[:, 0:128 * FT],
                                             AX, ALU.max))
                marks = {"ve": s_ve.n, "rowr": rowr, "rowt": rowt}
            if finish:
                ln_cc_start(marks, cc_idx)
                ln_scale_calc(marks, sig_col, sinv_slot, s_slot)
            return marks

        def ln_cc_start(marks, cc_idx):
            rowt = marks["rowt"]
            with bk() as blk:
                @blk.gpsimd
                def _(g):
                    g.wait_ge(s_ve.h, marks["ve"])
                    s_ax.dma(g.dma_start(rowt[:, 8:16],
                                         bass.AP(rowt, 0, [[1, 1], [0, 8], [1, 1]])))
                    g.wait_ge(s_ax.h, s_ax.n)
                    s_cc.dma(g.dma_start(S[f"cci{cc_idx}"], rowt[:, 8:16]))
                    g.wait_ge(s_cc.h, s_cc.n)
                    s_cx.inc(g.collective_compute(
                        "AllReduce", ALU.max, replica_groups=[list(range(n_cores))],
                        ins=[S[f"cci{cc_idx}"].opt()], outs=[S[f"cco{cc_idx}"].opt()]))
                marks["cx"] = s_cx.n
                marks["cc_idx"] = cc_idx

        def ln_scale_calc(marks, sig_col, sinv_slot, s_slot):
            rowr = marks["rowr"]
            rowt = marks["rowt"]
            with bk() as blk:
                @blk.gpsimd
                def _(g):
                    g.wait_ge(s_cx.h, marks["cx"])
                    s_cc.dma(g.dma_start(rowt[:, 0:8], S[f"cco{marks['cc_idx']}"]))
                cc2 = s_cc.n

                @blk.vector
                def _(v):
                    v.wait_ge(s_cc.h, cc2)
                    v.wait_ge(s_ve.h, marks["ve"])
                    if sig_col is not None:
                        s_ve.inc(v.tensor_tensor(rowt[:, 1:2], rowt[:, 0:1],
                                                 scm[:, sig_col:sig_col + 1], ALU.mult))
                        v.wait_ge(s_ve.h, s_ve.n)
                        s_ve.inc(v.tensor_scalar(rowt[:, 1:2], rowt[:, 1:2],
                                                 1.0 / QMAX, None, ALU.mult))
                    else:
                        s_ve.inc(v.tensor_scalar(rowt[:, 1:2], rowt[:, 0:1], 1.0 / QMAX, 1e-5,
                                                 ALU.mult, ALU.max))
                    v.wait_ge(s_ve.h, s_ve.n)
                    s_ve.inc(v.tensor_copy(scm[:, s_slot:s_slot + 1], rowt[:, 1:2]))
                    s_ve.inc(v.reciprocal(scm[:, sinv_slot:sinv_slot + 1], rowt[:, 1:2]))
                    v.wait_ge(s_ve.h, s_ve.n)
                    s_ve.inc(v.tensor_scalar(rowt[:, 0:T], rowr[:],
                                             scm[0:1, sinv_slot:sinv_slot + 1], None, ALU.mult))

                @blk.gpsimd
                def _(g):
                    g.wait_ge(s_ve.h, s_ve.n)
                    s_bc.dma(g.dma_start(scmc[:], bass.AP(scm, 0, [[1, 1], [0, 128], [1, 16]])))

        def ln_quant_sweep(xq_t, big32, bc_m, bc_r, sinv_col=None, rowt_src=None):
            with bk() as blk:
                if rowt_src is not None:
                    @blk.gpsimd
                    def _(g):
                        gate(g)
                        s_bc.dma(g.dma_start(
                            bc_r[:], bass.AP(rowt_src, 0, [[1, 1], [0, 128], [1, T]])))

                @blk.vector
                def _(v):
                    gate(v)
                    if rowt_src is not None:
                        v.wait_ge(s_bc.h, s_bc.n)
                    for j in range(FT):
                        v.wait_ge(s_ve.h, s_ve.n)
                        s_ve.inc(v.tensor_tensor(big32[0][:], res[j][:], bc_m[:], ALU.subtract))
                        v.wait_ge(s_ve.h, s_ve.n)
                        s_ve.inc(v.tensor_tensor(big32[0][:], big32[0][:], bc_r[:], ALU.mult))
                        v.wait_ge(s_ve.h, s_ve.n)
                        if sinv_col is not None:
                            s_ve.inc(v.tensor_scalar(big32[0][:], big32[0][:], sinv_col,
                                                     MAGIC, ALU.mult, ALU.add))
                        else:
                            s_ve.inc(v.tensor_scalar(big32[0][:], big32[0][:], MAGIC,
                                                     MAGIC, ALU.add, ALU.subtract))
                        v.wait_ge(s_ve.h, s_ve.n)
                        if sinv_col is not None:
                            s_ve.inc(v.tensor_scalar(big32[0][:], big32[0][:], MAGIC, QMAX,
                                                     ALU.subtract, ALU.min))
                            v.wait_ge(s_ve.h, s_ve.n)
                            s_ve.inc(v.tensor_scalar(xq_t[j][:], big32[0][:],
                                                     -QMAX - 1.0, None, ALU.max))
                        else:
                            s_ve.inc(v.tensor_scalar(big32[0][:], big32[0][:], QMAX,
                                                     -QMAX - 1.0, ALU.min, ALU.max))
                            v.wait_ge(s_ve.h, s_ve.n)
                            s_ve.inc(v.tensor_copy(xq_t[j][:], big32[0][:]))

        with ExitStack() as pl0:
            pbc = ExitStack()
            bc_m0 = sb("bc_m", [128, T], F32, pbc)
            prow = ExitStack()
            rowm = sb("rowm", [1, T], F32, prow)
            rowr = sb("rowr", [1, T], F32, prow)
            rowt = sb("rowt", [1, 2048], F32, prow)
            with ExitStack() as pxz:
                xn_bf = [sb(f"xnbf{i}", [128, T], BF16, pxz) for i in range(FT)]
                with ExitStack() as ph:
                    sqt = sb("sqt0", [128, T], BF16, ph)
                    big32 = [sb(f"bg{i}", [128, T], F32, ph) for i in range(1)]
                    bc_r0 = sb("bc_r", [128, T], F32, ph)
                    grow = sb("grow0", [1, 128 * FT], F32, ph)
                    marks0 = ln_stats_scale(xn_bf, sqt, big32, bc_m0, bc_r0, grow,
                                            rowm, rowr, rowt, 0, 2, 4, 5, finish=False)
                    ln_cc_start(marks0, 0)

                with ExitStack() as p2:
                    winT = [sb(f"winT{k}", [128, 2 * DI], BF16, p2) for k in range(FT)]
                    xzst = [sb(f"xzst{i}", [128, T], BF16, p2) for i in range(2)]
                    with bk() as blk:
                        lw0, pe1, ac1, sp1 = s_lw.n, s_pe.n, s_ac.n, s_sp.n
                        fwd_marks(("xzs",), [s_sp, s_p2], [s_sp.n, s_p2.n], 2 * NDT)

                        @blk.sync
                        def _(sy):
                            gate(sy)
                            for k in range(FT):
                                s_lw.dma(sy.dma_start(winT[k][:], S["WinT"][k]))

                        @blk.tensor
                        def _(t):
                            gate(t)
                            t.wait_ge(s_lw.h, s_lw.n)
                            for ct in range(2 * NDT):
                                for ch in range(N5):
                                    q = ct * N5 + ch
                                    if q >= 4:
                                        t.wait_ge(s_ac.h, ac1 + q - 3)
                                    for k in range(FT):
                                        i = t.matmul(psum[q % 4][:, 0:C5],
                                                     winT[k][:, ct * 128:(ct + 1) * 128],
                                                     xn_bf[k][:, ch * C5:(ch + 1) * C5],
                                                     start=(k == 0), stop=(k == FT - 1))
                                    s_pe.inc(i)

                        @blk.scalar
                        def _(sc):
                            for ct in range(2 * NDT):
                                if ct >= 2:
                                    dma_wait(sc, ("xzs", ct - 2))
                                for ch in range(N5):
                                    q = ct * N5 + ch
                                    sc.wait_ge(s_pe.h, pe1 + q + 1)
                                    s_ac.inc(sc.copy(xzst[ct % 2][:, ch * C5:(ch + 1) * C5],
                                                     psum[q % 4][:, 0:C5]))

                        @blk.gpsimd
                        def _(g):
                            for ct in range(2 * NDT):
                                g.wait_ge(s_ac.h, ac1 + (ct + 1) * N5)
                                (s_sp if ct % 2 == 0 else s_p2).dma(
                                    g.dma_start(S["xz"][ct], xzst[ct % 2][:]))

            ln_scale_calc(marks0, 2, 4, 5)

            with ExitStack() as pffn:
                xq0 = [sb(f"xq{i}", [128, T], BF16, pffn) for i in range(FT)]
                with ExitStack() as ph:
                    big32 = [sb(f"bgq{i}", [128, T], F32, ph) for i in range(1)]
                    bc_rq = sb("bc_rq", [128, T], F32, ph)
                    ln_quant_sweep(xq0, big32, bc_m0, bc_rq, rowt_src=rowt)

                with ExitStack() as p2:
                    wgu = [sb(f"wgu{i}", [128, 2048], BF16, p2) for i in range(2)]
                    hst = [sb(f"hst{i}", [128, T], BF16, p2) for i in range(2)]
                    sgt = [sb(f"sgt{i}", [128, 512], BF16, p2) for i in range(2)]
                    with bk() as blk:
                        lw0, pe1, ve1, ac1, sp1 = s_lw.n, s_pe.n, s_ve.n, s_ac.n, s_sp.n
                        KVE = N5 + 3
                        fwd_marks(("hsp",), [s_sp, s_p2], [s_sp.n, s_p2.n], NIT)

                        @blk.sync
                        def _(sy):
                            gate(sy)
                            for it in range(NIT):
                                if it >= 2:
                                    sy.wait_ge(s_pe.h, pe1 + (it - 1) * N5)
                                sx = s_lw if it % 2 == 0 else s_w2
                                sx.dma(sy.dma_start(
                                    wgu[it % 2][:, 0:1024],
                                    _dap(S["wgT"], it * 128,
                                         [[IF, 128], [128 * IF, FT], [1, 128]])))
                                dma_mark(("wgu", it), sx, sy.dma_start(
                                    wgu[it % 2][:, 1024:2048],
                                    _dap(S["wuT"], it * 128,
                                         [[IF, 128], [128 * IF, FT], [1, 128]])))

                        @blk.tensor
                        def _(t):
                            gate(t)
                            for it in range(NIT):
                                dma_wait(t, ("wgu", it))
                                if it >= 1:
                                    t.wait_ge(s_ac.h, ac1 + it * N5)
                                    t.wait_ge(s_ve.h, ve1 + it * KVE)
                                for ch in range(N5):
                                    for k in range(FT):
                                        t.matmul(psum[ch % 4][:, 0:C5],
                                                 wgu[it % 2][:, k * 128:(k + 1) * 128],
                                                 xq0[k][:, ch * C5:(ch + 1) * C5],
                                                 start=(k == 0), stop=(k == FT - 1))
                                    for k in range(FT):
                                        i = t.matmul(psum[4 + ch % 4][:, 0:C5],
                                                     wgu[it % 2][:, 1024 + k * 128:1024 + (k + 1) * 128],
                                                     xq0[k][:, ch * C5:(ch + 1) * C5],
                                                     start=(k == 0), stop=(k == FT - 1))
                                    s_pe.inc(i)

                        @blk.vector
                        def _(v):
                            for it in range(NIT):
                                s_ve.inc(v.tensor_tensor(colA[:, 2:3], wgs[:, it:it + 1],
                                                         scmc[:, 5:6], ALU.mult))
                                s_ve.inc(v.tensor_tensor(colA[:, 3:4], wgs[:, NIT + it:NIT + it + 1],
                                                         scmc[:, 5:6], ALU.mult))
                                v.wait_ge(s_ve.h, s_ve.n)
                                for ch in range(N5):
                                    v.wait_ge(s_ac.h, ac1 + it * N5 + ch + 1)
                                    if it >= 2:
                                        dma_wait(v, ("hsp", it - 2))
                                    s_ve.inc(v.scalar_tensor_tensor(
                                        hst[it % 2][:, ch * C5:(ch + 1) * C5],
                                        psum[4 + ch % 4][:, 0:C5], colA[:, 3:4],
                                        sgt[ch % 2][:, 0:C5], ALU.mult, ALU.mult))
                                v.wait_ge(s_ve.h, s_ve.n)
                                s_ve.inc(v.tensor_reduce(stat[:, 8 + it:9 + it], hst[it % 2][:],
                                                         AX, ALU.max, apply_absolute_value=True))

                        @blk.scalar
                        def _(sc):
                            for it in range(NIT):
                                sc.wait_ge(s_ve.h, ve1 + it * KVE + 2)
                                for ch in range(N5):
                                    sc.wait_ge(s_pe.h, pe1 + it * N5 + ch + 1)
                                    if it * N5 + ch >= 2:
                                        sc.wait_ge(s_ve.h, ve1 + (it * N5 + ch - 1) // N5 * KVE
                                                   + 2 + (it * N5 + ch - 1) % N5 + 1)
                                    s_ac.inc(sc.activation(sgt[ch % 2][:, 0:C5],
                                                           psum[ch % 4][:, 0:C5],
                                                           AF.Silu, scale=colA[:, 2:3]))

                        @blk.gpsimd
                        def _(g):
                            for it in range(NIT):
                                g.wait_ge(s_ve.h, ve1 + (it + 1) * KVE)
                                (s_sp if it % 2 == 0 else s_p2).dma(
                                    g.dma_start(S["hpre"][it], hst[it % 2][:]))

            prow.close()
            dbg_dump(0)

            with ExitStack() as ph:
                hch = sb("hch", [128, NIT * 512], BF16, ph)
                grow2 = sb("grow2", [1, 128 * NIT + 16], F32, ph)
                xq2 = sb("xqh2", [128, NIT * 512], BF16, ph)
                q32 = sb("q32", [128, 512], F32, ph)
                wdst = [sb(f"wdst{i}", [128, NIT * 128], BF16, ph) for i in range(FT)]

                with bk() as blk:
                    @blk.gpsimd
                    def _(g):
                        gate(g)
                        s_ax.dma(g.dma_start(grow2[:, 0:128 * NIT], stat[:, 8:8 + NIT]))

                    @blk.vector
                    def _(v):
                        gate(v)
                        v.wait_ge(s_ax.h, s_ax.n)
                        s_ve.inc(v.tensor_reduce(grow2[:, 128 * NIT:128 * NIT + 1],
                                                 grow2[:, 0:128 * NIT], AX, ALU.max))

                    @blk.gpsimd
                    def _(g):
                        g.wait_ge(s_ve.h, s_ve.n)
                        s_ax.dma(g.dma_start(
                            grow2[:, 128 * NIT + 8:128 * NIT + 16],
                            grow2[:, 128 * NIT:128 * NIT + 1].unsqueeze(1)
                            .broadcast_to([1, 8, 1])))
                        g.wait_ge(s_ax.h, s_ax.n)
                        s_cc.dma(g.dma_start(S["cci1"], grow2[:, 128 * NIT + 8:128 * NIT + 16]))
                        g.wait_ge(s_cc.h, s_cc.n)
                        s_cx.inc(g.collective_compute(
                            "AllReduce", ALU.max, replica_groups=[list(range(n_cores))],
                            ins=[S["cci1"].opt()], outs=[S["cco1"].opt()]))
                        dma_mark(("hld", 0), s_gd, g.dma_start(
                            hch[:, 0:NIT * C5],
                            _dap(S["hpre"], 0, [[T, 128], [128 * T, NIT], [1, C5]])))
                        g.wait_ge(s_cx.h, s_cx.n)
                        s_cc.dma(g.dma_start(grow2[:, 128 * NIT:128 * NIT + 8], S["cco1"]))

                    @blk.vector
                    def _(v):
                        v.wait_ge(s_cc.h, s_cc.n)
                        v.wait_ge(s_ve.h, s_ve.n)
                        s_ve.inc(v.tensor_tensor(grow2[:, 128 * NIT + 1:128 * NIT + 2],
                                                 grow2[:, 128 * NIT:128 * NIT + 1],
                                                 scm[:, 3:4], ALU.mult))
                        v.wait_ge(s_ve.h, s_ve.n)
                        s_ve.inc(v.tensor_scalar(grow2[:, 128 * NIT + 1:128 * NIT + 2],
                                                 grow2[:, 128 * NIT + 1:128 * NIT + 2],
                                                 1.0 / QMAX, None, ALU.mult))
                        v.wait_ge(s_ve.h, s_ve.n)
                        s_ve.inc(v.tensor_copy(scm[:, 7:8],
                                               grow2[:, 128 * NIT + 1:128 * NIT + 2]))
                        s_ve.inc(v.reciprocal(scm[:, 6:7],
                                              grow2[:, 128 * NIT + 1:128 * NIT + 2]))

                    @blk.gpsimd
                    def _(g):
                        g.wait_ge(s_ve.h, s_ve.n)
                        s_bc.dma(g.dma_start(scmc[:], bass.AP(scm, 0, [[1, 1], [0, 128], [1, 16]])))

                    @blk.vector
                    def _(v):
                        v.wait_ge(s_bc.h, s_bc.n)
                        s_ve.inc(v.tensor_tensor(stat[:, 48:48 + FT], wds[:, 0:FT],
                                                 scmc[:, 7:8].broadcast_to([128, FT]), ALU.mult))

                with bk() as blk:
                    lw0, ld0 = s_lw.n, s_ld.n
                    ve1, pe1 = s_ve.n, s_pe.n
                    KQ = 3 * NIT + FT

                    @blk.gpsimd
                    def _(g):
                        gate(g)
                        for ch in range(1, N5):
                            g.wait_ge(s_ve.h, ve1 + ch * KQ - FT)
                            dma_mark(("hld", ch), s_gd,
                                     g.dma_start(
                                hch[:, 0:NIT * C5],
                                _dap(S["hpre"], ch * C5,
                                     [[T, 128], [128 * T, NIT], [1, C5]])))

                    @blk.sync
                    def _(sy):
                        gate(sy)
                        for dm in range(FT):
                            dma_mark(("wds", dm), s_lw if dm % 2 == 0 else s_w2,
                                     sy.dma_start(
                                wdst[dm][:, 0:NIT * 128],
                                _dap(S["wdT"], dm * 128,
                                     [[D, 128], [128 * D, NIT], [1, 128]])))

                    @blk.vector
                    def _(v):
                        gate(v)
                        for ch in range(N5):
                            dma_wait(v, ("hld", ch))
                            if ch >= 1:
                                v.wait_ge(s_pe.h, pe1 + ch * FT)
                            for it in range(NIT):
                                v.wait_ge(s_ve.h, s_ve.n)
                                s_ve.inc(v.tensor_scalar(q32[:, 0:C5],
                                                         hch[:, it * C5:(it + 1) * C5],
                                                         scmc[:, 6:7], MAGIC, ALU.mult, ALU.add))
                                v.wait_ge(s_ve.h, s_ve.n)
                                s_ve.inc(v.tensor_scalar(q32[:, 0:C5], q32[:, 0:C5], MAGIC, QMAX,
                                                         ALU.subtract, ALU.min))
                                v.wait_ge(s_ve.h, s_ve.n)
                                s_ve.inc(v.tensor_scalar(xq2[:, it * C5:(it + 1) * C5],
                                                         q32[:, 0:C5],
                                                         -QMAX - 1.0, None, ALU.max))
                            s_ve.inc(v.nop()) if False else None
                            for dm in range(FT):
                                v.wait_ge(s_pe.h, pe1 + ch * FT + dm + 1)
                                s_ve.inc(v.scalar_tensor_tensor(
                                    res[dm][:, ch * C5:(ch + 1) * C5],
                                    psum[dm % 4][:, 0:C5], stat[:, 48 + dm:49 + dm],
                                    res[dm][:, ch * C5:(ch + 1) * C5],
                                    ALU.mult, ALU.add))

                    @blk.tensor
                    def _(t):
                        for ch in range(N5):
                            t.wait_ge(s_ve.h, ve1 + ch * KQ + 3 * NIT)
                            for dm in range(FT):
                                q = ch * FT + dm
                                t.wait_ge(s_lw.h, dmk[("wds", FT - 2)][1])
                                t.wait_ge(s_w2.h, dmk[("wds", FT - 1)][1])
                                if q >= 4:
                                    qa, qb = divmod(q - 4, FT)
                                    t.wait_ge(s_ve.h, ve1 + qa * KQ + 3 * NIT + qb + 1)
                                for k in range(NIT):
                                    i = t.matmul(psum[dm % 4][:, 0:C5],
                                                 wdst[dm][:, k * 128:(k + 1) * 128],
                                                 xq2[:, k * C5:(k + 1) * C5],
                                                 start=(k == 0), stop=(k == NIT - 1))
                                s_pe.inc(i)

            dbg_dump(1)
            pbc.close()

            with ExitStack() as ph:
                HW = TC + 3
                wxT = sb("wxT", [128, NDT * 96 * 2], BF16, ph)
                wdtT = sb("wdtT", [64, NDT * 128 * 2], BF16, ph)
                with bk() as blk:
                    @blk.gpsimd
                    def _(g):
                        gate(g)
                        s_gd.dma(g.dma_start(wxT[:], S["wxs"]))
                        s_gd.dma(g.dma_start(wdtT[0:64, :], S["wdts"]))
                xzin = sb("xzin", [128, NDT * HW], BF16, ph)
                xsch = sb("xsch", [128, NDT * TC], BF16, ph)
                bbuf = sb("bbuf", [128, DS * TC], BF16, ph)
                cbuf = sb("cbuf", [128, DS * TC], BF16, ph)
                hall = sb("hall", [128, DS * TC], BF16, ph)
                dab = [sb(f"dab{i}", [128, TC], BF16, ph) for i in range(4)]
                dbb = [sb(f"dbb{i}", [128, TC], BF16, ph) for i in range(2)]
                pbf = [sb(f"pbf{i}", [128, TC], BF16, ph) for i in range(4)]
                pxd = [sb(f"pxd{i}", [128, TC], BF16, ph) for i in range(2)]
                ctp = [sb(f"ctp{i}", [128, TC], BF16, ph) for i in range(2)]
                dtch = sb("dtch", [128, TC], F32, ph)
                uch = sb("uch", [128, TC], BF16, ph)
                statev = sb("statev", [128, NDT * DS], F32, ph)
                conv32 = [sb(f"cv{i}", [128, TC], F32, ph) for i in range(1)]
                szb = [sb(f"szb{i}", [128, TC], BF16, ph) for i in range(NDT)]
                otb = [sb(f"otb{i}", [128, TC], BF16, ph) for i in range(2)]
                cmb = [sb(f"cmb{i}", [128, TC], BF16, ph) for i in range(2)]
                ofl = [sb(f"ofl{i}", [128, TC], BF16, ph) for i in range(2)]
                xdbl = sb("xdbl", [96, TC], BF16, ph)

                prev_tts_end = [None]
                prev_gp_end = [None]
                prev_pe_end = [None]

                def emit_xload(g, dirn_, ch_):
                    edge_ = (ch_ == 0)
                    if dirn_ == 0:
                        t0_ = ch_ * TC
                        lo_ = max(0, t0_ - 3)
                        nc_ = (t0_ + TC) - lo_
                    else:
                        thi_ = T - 1 - ch_ * TC
                        tlo_ = thi_ - (TC - 1)
                        lo_ = tlo_
                        nc_ = min(T, thi_ + 4) - tlo_
                    xzv = xzin[:].rearrange("p (t c) -> p t c", c=HW)
                    if dirn_ == 0:
                        off_ = 3 if edge_ else 0
                        if edge_:
                            g.memset(xzv[:, :, 0:3], 0.0)
                        i = g.dma_start(
                            xzv[:, :, off_:off_ + nc_],
                            _dap(S["xz"], lo_, [[T, 128], [128 * T, NDT], [1, nc_]]))
                    else:
                        if edge_:
                            g.memset(xzv[:, :, TC:TC + 3], 0.0)
                        i = g.dma_start(
                            xzv[:, :, 0:nc_],
                            _dap(S["xz"], lo_, [[T, 128], [128 * T, NDT], [1, nc_]]))
                    dma_mark(("xld", dirn_, ch_), s_gd, i)

                for dirn in range(2):
                    AO = dirn * NDT
                    with bk() as blk:
                        @blk.gpsimd
                        def _(g):
                            gate(g)
                            s_gp.inc(g.memset(statev[:], 0.0))
                            if dirn == 0:
                                emit_xload(g, 0, 0)

                    for ch in range(NCH):
                        if dirn == 0:
                            zlo = ch * TC
                        else:
                            zlo = T - 1 - ch * TC - (TC - 1)
                        edge = (ch == 0)

                        with bk() as blk:
                            fwd_marks(("osp", dirn, ch), [s_sp, s_p2],
                                      [s_sp.n, s_p2.n], NDT)
                            if dirn == 1:
                                fwd_marks(("ofl", dirn, ch), [s_o1, s_o2],
                                          [s_o1.n, s_o2.n], NDT)

                            ve_t0 = s_ve.n
                            ac_s0 = s_ac.n
                            pe_c0 = s_pe.n
                            gp_c0 = s_gp.n
                            zld_mark = [s_gd.n + 16]
                            cde = {}
                            _c = 0
                            for _d in range(0, NDT, 2):
                                _c += 4
                                cde[_d] = _c
                            codd = list(range(1, NDT, 2))
                            cgp = {d: 4 * (j + 1) for j, d in enumerate(codd)}
                            cpe = {d: 4 * (j + 1) for j, d in enumerate(codd)}

                            def xsl(d, k):
                                base = d * HW
                                if dirn == 0:
                                    return xzin[:, base + k:base + k + TC]
                                return xzin[:, base + 3 - k:base + 3 - k + TC][:, ::-1]

                            @blk.vector
                            def _(v):
                                gate(v)
                                dma_wait(v, ("xld", dirn, ch))
                                for d in range(0, NDT, 2):
                                    wof = (AO + d) * DC
                                    if d >= 2:
                                        v.wait_ge(s_ac.h, ac_s0 + d - 1)
                                    for k in range(DC):
                                        v.wait_ge(s_ve.h, s_ve.n)
                                        if k == 0:
                                            s_ve.inc(v.tensor_scalar(conv32[0][:], xsl(d, 0),
                                                                     convw[:, wof:wof + 1],
                                                                     convb[:, AO + d:AO + d + 1],
                                                                     ALU.mult, ALU.add))
                                        else:
                                            s_ve.inc(v.scalar_tensor_tensor(
                                                conv32[0][:], xsl(d, k),
                                                convw[:, wof + k:wof + k + 1],
                                                conv32[0][:], ALU.mult, ALU.add))

                            @blk.gpsimd
                            def _(g):
                                dma_wait(g, ("xld", dirn, ch))
                                for j, d in enumerate(codd):
                                    wof = (AO + d) * DC
                                    for k in range(DC):
                                        gi = 4 * j + k
                                        if gi >= 2:
                                            g.wait_ge(s_pe.h, pe_c0 + gi - 1)
                                        s_gp.inc(g.tensor_tensor(
                                            ctp[k % 2][:], xsl(d, k),
                                            convw[:, wof + k:wof + k + 1]
                                            .broadcast_to([128, TC]), ALU.mult))

                            @blk.scalar
                            def _(sc):
                                gate(sc)
                                for d in range(NDT):
                                    if d % 2 == 0:
                                        sc.wait_ge(s_ve.h, ve_t0 + cde[d])
                                        s_ac.inc(sc.activation(xsch[:, d * TC:(d + 1) * TC],
                                                               conv32[0][:], AF.Silu))
                                    else:
                                        sc.wait_ge(s_pe.h, pe_c0 + cpe[d])
                                        s_ac.inc(sc.activation(
                                            xsch[:, d * TC:(d + 1) * TC],
                                            psum[1 + (d // 2) % 2][:, 0:TC], AF.Silu,
                                            bias=convb[:, AO + d:AO + d + 1]))
                                sc.wait_ge(s_gd.h, zld_mark[0])
                                for d in range(NDT):
                                    zsl = xzin[:, d * HW:d * HW + TC]
                                    if dirn == 1:
                                        zsl = zsl[:, ::-1]
                                    s_ac.inc(sc.activation(szb[d][:], zsl, AF.Silu))

                            pe_wx0 = s_pe.n

                            @blk.tensor
                            def _(t):
                                gate(t)
                                for d in range(NDT):
                                    if d % 2 == 1:
                                        if d >= 5:
                                            t.wait_ge(s_ac.h, ac_s0 + d - 3)
                                        for k in range(DC):
                                            t.wait_ge(s_gp.h, gp_c0 + cgp[d] - 3 + k)
                                            s_pe.inc(t.matmul(
                                                psum[1 + (d // 2) % 2][:, 0:TC],
                                                ident_b[:], ctp[k % 2][:],
                                                start=(k == 0), stop=(k == DC - 1)))
                                    t.wait_ge(s_ac.h, ac_s0 + d + 1)
                                    i = t.matmul(psum[0][0:96, 0:TC],
                                                 wxT[:, (AO + d) * 96:(AO + d + 1) * 96],
                                                 xsch[:, d * TC:(d + 1) * TC],
                                                 start=(d == 0), stop=(d == NDT - 1))
                                s_pe.inc(i)

                            pe_wx_end = s_pe.n
                            ac_xdbl0 = s_ac.n

                            @blk.scalar
                            def _(sc):
                                sc.wait_ge(s_pe.h, pe_wx_end)
                                s_ac.inc(sc.copy(xdbl[:], psum[0][0:96, 0:TC]))

                            ld_z0 = s_gd.n
                            bc_0 = s_bc.n

                            @blk.gpsimd
                            def _(g):
                                g.wait_ge(s_ve.h, ve_t0 + cde[NDT - 2])
                                g.wait_ge(s_gp.h, gp_c0 + 4 * len(codd))
                                i = g.dma_start(
                                    xzin[:].rearrange("p (t c) -> p t c", c=HW)[:, :, 0:TC],
                                    _dap(S["xz"], NDT * 128 * T + zlo,
                                         [[T, 128], [128 * T, NDT], [1, TC]]))
                                s_gd.dma(i)
                                g.wait_ge(s_ac.h, ac_xdbl0 + 1)
                                for n in range(DS):
                                    s_bc.dma(g.dma_start(
                                        bbuf[:, n * TC:(n + 1) * TC],
                                        xdbl[64 + n:65 + n, :].unsqueeze(1)
                                        .broadcast_to([1, 128, TC])))
                                for n in range(DS):
                                    s_bc.dma(g.dma_start(
                                        cbuf[:, n * TC:(n + 1) * TC],
                                        xdbl[80 + n:81 + n, :].unsqueeze(1)
                                        .broadcast_to([1, 128, TC])))
                                if dirn == 1:
                                    for d in range(2):
                                        (s_o1 if d % 2 == 0 else s_o2).dma(
                                            g.dma_start(
                                            ofl[d % 2][:],
                                            S["outf"][d][:, zlo:zlo + TC]))

                            ac3 = s_ac.n
                            pe2 = s_pe.n
                            ve2b = s_ve.n
                            gp2b = s_gp.n
                            KA3 = 19

                            def ac_dtln(d):
                                return ac3 + d * KA3 + 2

                            def ac_p17(d):
                                return ac3 + d * KA3 + 3

                            def ac_dA(d, n):
                                return ac3 + d * KA3 + 4 + n

                            def ac_sz(d):
                                return ac_s0 + NDT + d + 1

                            def gp_p(d, n):
                                return gp2b + d * DS + n + 1

                            ppos = {}
                            _c = 0
                            for _d in range(2):
                                _c += 1
                                ppos[("dt", _d)] = _c
                            for _d in range(NDT):
                                if _d + 2 < NDT:
                                    _c += 1
                                    ppos[("dt", _d + 2)] = _c
                                _c += 1
                                ppos[("y17", _d)] = _c
                                for _n in range(DS):
                                    _c += 1
                                    ppos[("y", _d, _n)] = _c

                            def pe_dt(d):
                                return pe2 + ppos[("dt", d)]

                            def pe_y17(d):
                                return pe2 + ppos[("y17", d)]

                            def pe_y(d, n):
                                return pe2 + ppos[("y", d, n)]

                            vpos = {}
                            _c = 0
                            for _d in range(NDT):
                                _c += 1
                                vpos[("u", _d)] = _c
                                for _n in range(DS):
                                    _c += 1
                                    vpos[("dbu", _d, _n)] = _c
                                    _c += 1
                                    vpos[("tts", _d, _n)] = _c
                                    if _n == 0 and _d >= 1:
                                        _c += 1
                                        vpos[("ot", _d - 1)] = _c
                                        if dirn == 1:
                                            _c += 1
                                            vpos[("cmb", _d - 1)] = _c
                                _c += 1
                                vpos[("state", _d)] = _c
                            _c += 1
                            vpos[("ot", NDT - 1)] = _c
                            if dirn == 1:
                                _c += 1
                                vpos[("cmb", NDT - 1)] = _c

                            def ve_u(d):
                                return ve2b + vpos[("u", d)]

                            def ve_tts(d, n):
                                return ve2b + vpos[("tts", d, n)]

                            def ve_ot(d):
                                return ve2b + vpos[("ot", d)]

                            def ve_cmb(d):
                                return ve2b + vpos[("cmb", d)]

                            @blk.scalar
                            def _(sc):
                                for d in range(NDT):
                                    sc.wait_ge(s_pe.h, pe_dt(d))
                                    if d >= 1:
                                        sc.wait_ge(s_ve.h, ve_u(d - 1))
                                    s_ac.inc(sc.activation(conv32[0][:], psum[4 + d % 2][:, 0:TC],
                                                           AF.Exp,
                                                           bias=bdtc[:, AO + d:AO + d + 1]))
                                    sc.wait_ge(s_ac.h, s_ac.n)
                                    s_ac.inc(sc.activation(dtch[:], conv32[0][:], AF.Ln, bias=1.0))
                                    sc.wait_ge(s_ac.h, s_ac.n)
                                    if d >= 2:
                                        sc.wait_ge(s_pe.h, pe_y17(d - 2))
                                    s_ac.inc(sc.activation(
                                        pxd[d % 2][:], xsch[:, d * TC:(d + 1) * TC],
                                        AF.Copy, scale=dcol[:, AO + d:AO + d + 1]))
                                    if prev_tts_end[0] and d == 0:
                                        sc.wait_ge(s_ve.h, prev_tts_end[0])
                                    for n in range(DS):
                                        nn = n - 4
                                        if nn >= 0:
                                            sc.wait_ge(s_ve.h, ve_tts(d, nn))
                                        elif d >= 1:
                                            sc.wait_ge(s_ve.h, ve_tts(d - 1, DS + nn))
                                        s_ac.inc(sc.activation(
                                            dab[n % 4][:], dtch[:], AF.Exp,
                                            scale=a_cols[:, (AO + d) * DS + n:(AO + d) * DS + n + 1]))

                            @blk.tensor
                            def _(t):
                                def emit_dt(d):
                                    t.wait_ge(s_ac.h, ac_xdbl0 + 1)
                                    if d >= 2:
                                        t.wait_ge(s_ac.h, ac_dtln(d - 2))
                                    s_pe.inc(t.matmul(
                                        psum[4 + d % 2][:, 0:TC],
                                        wdtT[:, (AO + d) * 128:(AO + d + 1) * 128],
                                        xdbl[0:64, :], start=True, stop=True))

                                emit_dt(0)
                                emit_dt(1)
                                for d in range(NDT):
                                    if d + 2 < NDT:
                                        emit_dt(d + 2)
                                    if d >= 2:
                                        t.wait_ge(s_ve.h, ve_ot(d - 2))
                                    t.wait_ge(s_ac.h, ac_p17(d))
                                    s_pe.inc(t.matmul(psum[6 + d % 2][:, 0:TC],
                                                      ident_b[:], pxd[d % 2][:],
                                                      start=True, stop=False))
                                    for n in range(DS):
                                        t.wait_ge(s_gp.h, gp_p(d, n))
                                        s_pe.inc(t.matmul(psum[6 + d % 2][:, 0:TC],
                                                          ident_b[:], pbf[n % 4][:],
                                                          start=False, stop=(n == DS - 1)))

                            @blk.vector
                            def _(v):
                                v.wait_ge(s_bc.h, bc_0 + 32 * 16)

                                def emit_ot(d):
                                    v.wait_ge(s_pe.h, pe_y(d, DS - 1))
                                    v.wait_ge(s_ac.h, ac_sz(d))
                                    if dirn == 0:
                                        if d >= 2:
                                            dma_wait(v, ("osp", dirn, ch, d - 2))
                                        s_ve.inc(v.tensor_tensor(
                                            otb[d % 2][:], psum[6 + d % 2][:, 0:TC],
                                            szb[d][:], ALU.mult))
                                    else:
                                        s_ve.inc(v.tensor_tensor(
                                            otb[d % 2][:], psum[6 + d % 2][:, 0:TC][:, ::-1],
                                            szb[d][:, ::-1], ALU.mult))
                                        v.wait_ge(s_ve.h, s_ve.n)
                                        dma_wait(v, ("ofl", dirn, ch, d))
                                        if d >= 2:
                                            dma_wait(v, ("osp", dirn, ch, d - 2))
                                        s_ve.inc(v.tensor_tensor(cmb[d % 2][:], otb[d % 2][:],
                                                                 ofl[d % 2][:], ALU.add))

                                for d in range(NDT):
                                    v.wait_ge(s_ac.h, ac_dtln(d))
                                    s_ve.inc(v.tensor_tensor(uch[:], dtch[:],
                                                             xsch[:, d * TC:(d + 1) * TC],
                                                             ALU.mult))
                                    v.wait_ge(s_ve.h, s_ve.n)
                                    for n in range(DS):
                                        s_ve.inc(v.tensor_tensor(dbb[n % 2][:], uch[:],
                                                                 bbuf[:, n * TC:(n + 1) * TC],
                                                                 ALU.mult))
                                        if d >= 1:
                                            v.wait_ge(s_gp.h, gp_p(d - 1, n))
                                        elif prev_gp_end[0] is not None:
                                            v.wait_ge(s_gp.h, prev_gp_end[0][n])
                                        v.wait_ge(s_ac.h, ac_dA(d, n))
                                        v.wait_ge(s_ve.h, s_ve.n)
                                        init = (statev[:, (d * DS + n):(d * DS + n + 1)]
                                                if ch > 0 else 0.0)
                                        s_ve.inc(v.tensor_tensor_scan(
                                            hall[:, n * TC:(n + 1) * TC],
                                            dab[n % 4][:],
                                            dbb[n % 2][:], init, ALU.mult, ALU.add))
                                        if n == 0 and d >= 1:
                                            emit_ot(d - 1)
                                    v.wait_ge(s_ve.h, s_ve.n)
                                    s_ve.inc(v.tensor_copy(statev[:, d * DS:(d + 1) * DS],
                                                           hall[:, TC - 1::TC]))
                                emit_ot(NDT - 1)

                            @blk.gpsimd
                            def _(g):
                                def emit_io(e):
                                    if dirn == 0:
                                        g.wait_ge(s_ve.h, ve_ot(e))
                                        (s_sp if e % 2 == 0 else s_p2).dma(g.dma_start(
                                            S["outf"][e][:, zlo:zlo + TC], otb[e % 2][:]))
                                    else:
                                        if e + 2 < NDT:
                                            g.wait_ge(s_ve.h, ve_cmb(e))
                                            (s_o1 if e % 2 == 0 else s_o2).dma(g.dma_start(
                                                ofl[e % 2][:],
                                                S["outf"][e + 2][:, zlo:zlo + TC]))
                                        g.wait_ge(s_ve.h, ve_cmb(e))
                                        (s_sp if e % 2 == 0 else s_p2).dma(g.dma_start(
                                            S["outc"][e][:, zlo:zlo + TC], cmb[e % 2][:]))

                                nxt = ((dirn, ch + 1) if ch + 1 < NCH else
                                       ((1, 0) if dirn == 0 else None))
                                for d in range(NDT):
                                    for n in range(DS):
                                        g.wait_ge(s_ve.h, ve_tts(d, n))
                                        gi = d * DS + n
                                        if gi >= 4:
                                            pn = divmod(gi - 4, DS)
                                            g.wait_ge(s_pe.h, pe_y(*pn))
                                        elif prev_pe_end[0] is not None:
                                            g.wait_ge(s_pe.h, prev_pe_end[0])
                                        s_gp.inc(g.tensor_tensor(
                                            pbf[n % 4][:],
                                            hall[:, n * TC:(n + 1) * TC],
                                            cbuf[:, n * TC:(n + 1) * TC], ALU.mult))
                                    if d == 0 and nxt is not None:
                                        g.wait_ge(s_ac.h, ac_sz(NDT - 1))
                                        emit_xload(g, *nxt)
                                    if d >= 1:
                                        emit_io(d - 1)
                                emit_io(NDT - 1)

                            prev_tts_end[0] = ve_tts(NDT - 1, DS - 1)
                            prev_gp_end[0] = [gp_p(NDT - 1, n) for n in range(DS)]
                            prev_pe_end[0] = pe_y(NDT - 1, DS - 1)

            with ExitStack() as ph:
                ocst = [sb(f"ocst{i}", [128, NDT * 512], BF16, ph) for i in range(2)]
                wost = [sb(f"wost{i}", [128, NDT * 128], BF16, ph) for i in range(FT)]
                with bk() as blk:
                    lw0, ld0, pe1, ve1 = s_lw.n, s_ld.n, s_pe.n, s_ve.n

                    @blk.gpsimd
                    def _(g):
                        gate(g)
                        for ch in range(N5):
                            if ch >= 2:
                                g.wait_ge(s_pe.h, pe1 + (ch - 1) * FT)
                            dma_mark(("oc", ch), s_gd, g.dma_start(
                                ocst[ch % 2][:, 0:NDT * C5],
                                _dap(S["outc"], ch * C5,
                                     [[T, 128], [128 * T, NDT], [1, C5]])))

                    @blk.sync
                    def _(sy):
                        gate(sy)
                        for dm in range(FT):
                            if dm >= 2:
                                dma_wait(sy, ("wo", dm - 2))
                            dma_mark(("wo", dm), s_lw if dm % 2 == 0 else s_w2,
                                     sy.dma_start(
                                wost[dm][:, 0:NDT * 128],
                                _dap(S["woutT"], dm * 128,
                                     [[D, 128], [128 * D, NDT], [1, 128]])))

                    @blk.tensor
                    def _(t):
                        for ch in range(N5):
                            dma_wait(t, ("oc", ch))
                            for dm in range(FT):
                                q = ch * FT + dm
                                dma_wait(t, ("wo", dm))
                                if q >= 4:
                                    t.wait_ge(s_ve.h, ve1 + q - 3)
                                for k in range(NDT):
                                    i = t.matmul(psum[dm % 4][:, 0:C5],
                                                 wost[dm][:, k * 128:(k + 1) * 128],
                                                 ocst[ch % 2][:, k * C5:(k + 1) * C5],
                                                 start=(k == 0), stop=(k == NDT - 1))
                                s_pe.inc(i)

                    @blk.vector
                    def _(v):
                        for ch in range(N5):
                            for dm in range(FT):
                                q = ch * FT + dm
                                v.wait_ge(s_pe.h, pe1 + q + 1)
                                s_ve.inc(v.tensor_tensor(
                                    res[dm][:, ch * C5:(ch + 1) * C5],
                                    res[dm][:, ch * C5:(ch + 1) * C5],
                                    psum[dm % 4][:, 0:C5], ALU.add))

            dbg_dump(2)

        with ExitStack() as pl1:
            rowm1 = sb("rowm1", [1, T], F32, pl1)
            rowr1 = sb("rowr1", [1, T], F32, pl1)
            rowt1 = sb("rowt1", [1, 2048], F32, pl1)
            xn1 = [sb(f"xn1_{i}", [128, T], BF16, pl1) for i in range(FT)]
            bc_m1 = sb("bc_m1", [128, T], F32, pl1)
            bc_r1 = sb("bc_r1", [128, T], F32, pl1)
            with ExitStack() as ph:
                sqt = sb("sqt1", [128, T], BF16, ph)
                big32 = [sb(f"bgb{i}", [128, T], F32, ph) for i in range(1)]
                grow = sb("grow1", [1, 128 * FT], F32, ph)
                marks1 = ln_stats_scale(xn1, sqt, big32, bc_m1, bc_r1, grow,
                                        rowm1, rowr1, rowt1, 2, None, 8, 9, finish=False)
                ln_cc_start(marks1, 2)

            with ExitStack() as ph:
                qkv = [sb(f"qkv{i}", [128, T], BF16, ph) for i in range(9)]
                kvT = sb("kvT", [128, NKT * 65], BF16, ph)
                pwt = ExitStack()
                watt = [sb(f"watt{k}", [128, D + 2 * HD], BF16, pwt) for k in range(FT)]

                with bk() as blk:
                    lw0, pe1, ac1 = s_lw.n, s_pe.n, s_ac.n
                    NJ = 9 * N5

                    @blk.sync
                    def _(sy):
                        gate(sy)
                        for k in range(FT):
                            s_lw.dma(sy.dma_start(watt[k][:], S["wattnT"][k]))

                    @blk.tensor
                    def _(t):
                        gate(t)
                        t.wait_ge(s_lw.h, s_lw.n)
                        for jt in range(9):
                            for ch in range(N5):
                                q = jt * N5 + ch
                                if q >= 4:
                                    t.wait_ge(s_ac.h, ac1 + q - 3)
                                for k in range(FT):
                                    i = t.matmul(psum[q % 4][:, 0:C5],
                                                 watt[k][:, jt * 128:(jt + 1) * 128],
                                                 xn1[k][:, ch * C5:(ch + 1) * C5],
                                                 start=(k == 0), stop=(k == FT - 1))
                                s_pe.inc(i)

                    @blk.scalar
                    def _(sc):
                        for jt in range(9):
                            for ch in range(N5):
                                q = jt * N5 + ch
                                sc.wait_ge(s_pe.h, pe1 + q + 1)
                                s_ac.inc(sc.copy(qkv[jt][:, ch * C5:(ch + 1) * C5],
                                                 psum[q % 4][:, 0:C5]))

                    @blk.gpsimd
                    def _(g):
                        gate(g)
                        s_gp.inc(g.memset(kvT[:], 1.0))

                    pe2 = s_pe.n
                    ve2 = s_ve.n
                    @blk.tensor
                    def _(t):
                        t.wait_ge(s_ac.h, s_ac.n)
                        t.wait_ge(s_gp.h, s_gp.n)
                        for kt in range(NKT):
                            if kt >= 2:
                                t.wait_ge(s_ve.h, ve2 + kt - 1)
                            s_pe.inc(t.transpose(psb(4 + kt % 2, 128, 64),
                                                 qkv[8][64:128, kt * 128:(kt + 1) * 128],
                                                 ident_b[64:128, 64:128]))

                    @blk.vector
                    def _(v):
                        for kt in range(NKT):
                            v.wait_ge(s_pe.h, pe2 + kt + 1)
                            s_ve.inc(v.tensor_copy(kvT[:, kt * 65:kt * 65 + 64],
                                                   psb(4 + kt % 2, 128, 64)))

                pwt.close()
                with ExitStack() as ph2:
                    ptile = [sb(f"pt{i}", [128, 512], BF16, ph2) for i in range(4)]
                    arow = sb("arow", [64, 512], F32, ph2)
                    kk = sb("kk", [128, T], BF16, ph2)
                    atto = [sb(f"atto{i}", [64, 512], BF16, ph2) for i in range(2)]
                    dmask = sb("dmask", [128, 512 * 4], BF16, ph2)
                    growf = [sb(f"growf{i}", [1, 512], F32, ph2) for i in range(2)]
                    onesf = sb("onesf", [1, 64], F32, ph2)

                    seq = []
                    hq = []
                    for h in range(NH):
                        for qc in range(T // 512):
                            nkt = (qc + 1) * 512 // 128
                            hq.append((h, qc, nkt))
                            for kt in range(nkt):
                                seq.append((h, qc, kt, kt * 128 >= qc * 512))

                    with bk() as blk:
                        @blk.gpsimd
                        def _(g):
                            gate(g)
                            s_ax.dma(g.dma_start(kk[64:128, :], qkv[8][0:64, :]))
                            g.memset(onesf[:], 1.0)
                            last = None
                            for j, off in enumerate((0, 128, 256, 384)):
                                last = g.affine_select(
                                    dmask[:, j * 512:j * 512 + 512],
                                    onesq_b[:, 0:1].broadcast_to([128, 512]),
                                    [[1, 512]], ALU.is_ge, 0.0, base=-off,
                                    channel_multiplier=-1)
                            s_gp.inc(last)

                    with bk() as blk:
                        pe3, ac3, ve3, bc3, gp3 = s_pe.n, s_ac.n, s_ve.n, s_bc.n, s_gp.n
                        axc = [s_ax.n, s_g2.n]
                        odd_idx = {}
                        _c = 0
                        for _ih in range(len(hq)):
                            if hq[_ih][0] % 2 == 1:
                                odd_idx[_ih] = _c
                                p = _c % 2
                                axc[p] += 16
                                dmk[("amv", _ih)] = ((s_ax if p == 0 else s_g2).h, axc[p])
                                _c += 1
                        odd_list = sorted(odd_idx, key=lambda k: odd_idx[k])
                        n_diag_before = []
                        nd = 0
                        for (h, qc, kt, diag) in seq:
                            n_diag_before.append(nd)
                            if diag:
                                nd += 1

                        hq_of_tile = []
                        for i_hq, (h, qc, nkt) in enumerate(hq):
                            for kt in range(nkt):
                                hq_of_tile.append(i_hq)
                        last_tile_of = {}
                        for i, ihq in enumerate(hq_of_tile):
                            last_tile_of[ihq] = i

                        acpos = {}
                        _c = 0
                        for i in range(len(seq)):
                            _c += 1
                            acpos[("exp", i)] = _c
                            if i == last_tile_of[hq_of_tile[i]]:
                                _c += 1
                                acpos[("den", hq_of_tile[i])] = _c
                        pe_ops = []
                        for i in range(len(seq)):
                            pe_ops.append(("sc", i))
                            j = i - 2
                            if j >= 0:
                                pe_ops.append(("pv", j))
                                if j == last_tile_of[hq_of_tile[j]]:
                                    pe_ops.append(("bc", hq_of_tile[j]))
                        for j in (len(seq) - 2, len(seq) - 1):
                            pe_ops.append(("pv", j))
                            if j == last_tile_of[hq_of_tile[j]]:
                                pe_ops.append(("bc", hq_of_tile[j]))
                        pepos = {}
                        for _c, op in enumerate(pe_ops):
                            pepos[op] = _c + 1
                        vepos = {}
                        _c = 0
                        for i, (h, qc, kt, diag) in enumerate(seq):
                            if diag:
                                _c += 1
                                vepos[("mask", i)] = _c
                            if i == last_tile_of[hq_of_tile[i]]:
                                _c += 1
                                vepos[("rc", hq_of_tile[i])] = _c
                                _c += 1
                                vepos[("mul", hq_of_tile[i])] = _c

                        def ve_evac_cnt(i_hq):
                            return ve3 + vepos[("mul", i_hq)]

                        @blk.tensor
                        def _(t):
                            gate(t)
                            for op in pe_ops:
                                kind, i = op
                                if kind == "bc":
                                    t.wait_ge(s_ac.h, ac3 + acpos[("den", i)])
                                    if i >= 2:
                                        t.wait_ge(s_ve.h, ve3 + vepos[("rc", i - 2)])
                                    s_pe.inc(t.matmul(
                                        psum[4 + i % 2][0:64, 0:512],
                                        onesf[:], growf[i % 2][:],
                                        start=True, stop=True))
                                    continue
                                h, qc, kt, diag = seq[i]
                                jq, rq = h // 2, (h % 2) * 64
                                ihq = hq_of_tile[i]
                                if kind == "sc":
                                    if i >= 4:
                                        t.wait_ge(s_ac.h, ac3 + acpos[("exp", i - 4)])
                                    ksrc = (qkv[8][0:64, kt * 128:(kt + 1) * 128]
                                            if rq == 0
                                            else kk[64:128, kt * 128:(kt + 1) * 128])
                                    t.wait_ge(s_ax.h, s_ax.n)
                                    s_pe.inc(t.matmul(
                                        psum[i % 4][:, 0:512],
                                        ksrc,
                                        qkv[jq][rq:rq + 64, qc * 512:(qc + 1) * 512],
                                        start=True, stop=True))
                                else:
                                    if diag:
                                        t.wait_ge(s_ve.h, ve3 + vepos[("mask", i)])
                                    else:
                                        t.wait_ge(s_ac.h, ac3 + acpos[("exp", i)])
                                    if kt == 0 and i >= 1:
                                        prev = ihq - 2
                                        if prev >= 0:
                                            t.wait_ge(s_ve.h, ve_evac_cnt(prev))
                                    s_pe.inc(t.matmul(
                                        psum[6 + ihq % 2][0:65, 0:512],
                                        kvT[:, kt * 65:(kt + 1) * 65],
                                        ptile[i % 4][:],
                                        start=(kt == 0), stop=(kt == hq[ihq][2] - 1)))

                        @blk.scalar
                        def _(sc):
                            for i, (h, qc, kt, diag) in enumerate(seq):
                                ihq = hq_of_tile[i]
                                sc.wait_ge(s_pe.h, pe3 + pepos[("sc", i)])
                                if i >= 4:
                                    sc.wait_ge(s_pe.h, pe3 + pepos[("pv", i - 4)])
                                s_ac.inc(sc.activation(ptile[i % 4][:],
                                                       psum[i % 4][:, 0:512],
                                                       AF.Exp, scale=0.125))
                                if i == last_tile_of[ihq]:
                                    sc.wait_ge(s_pe.h, pe3 + pepos[("pv", i)])
                                    if ihq >= 2:
                                        sc.wait_ge(s_pe.h, pe3 + pepos[("bc", ihq - 2)])
                                    s_ac.inc(sc.activation(growf[ihq % 2][:],
                                                           psum[6 + ihq % 2][64:65, 0:512],
                                                           AF.Copy))

                        @blk.vector
                        def _(v):
                            for i, (h, qc, kt, diag) in enumerate(seq):
                                ihq = hq_of_tile[i]
                                if diag:
                                    v.wait_ge(s_ac.h, ac3 + acpos[("exp", i)])
                                    moff = (kt * 128 - qc * 512) // 128
                                    s_ve.inc(v.tensor_tensor(
                                        ptile[i % 4][:], ptile[i % 4][:],
                                        dmask[:, moff * 512:(moff + 1) * 512], ALU.mult))
                                if i == last_tile_of[ihq]:
                                    h_, qc_ = hq[ihq][0], hq[ihq][1]
                                    jq, rq = h_ // 2, (h_ % 2) * 64
                                    v.wait_ge(s_pe.h, pe3 + pepos[("bc", ihq)])
                                    s_ve.inc(v.reciprocal(arow[:],
                                                          psum[4 + ihq % 2][0:64, 0:512]))
                                    v.wait_ge(s_ve.h, s_ve.n)
                                    if rq == 0:
                                        s_ve.inc(v.tensor_tensor(
                                            xn1[jq][0:64, qc_ * 512:(qc_ + 1) * 512],
                                            psum[6 + ihq % 2][0:64, 0:512], arow[:],
                                            ALU.mult))
                                    else:
                                        oi = odd_idx[ihq]
                                        if oi >= 2:
                                            dma_wait(v, ("amv", odd_list[oi - 2]))
                                        s_ve.inc(v.tensor_tensor(
                                            atto[oi % 2][:],
                                            psum[6 + ihq % 2][0:64, 0:512], arow[:],
                                            ALU.mult))

                        @blk.gpsimd
                        def _(g):
                            for i_hq in odd_list:
                                oi = odd_idx[i_hq]
                                h_, qc_ = hq[i_hq][0], hq[i_hq][1]
                                g.wait_ge(s_ve.h, ve_evac_cnt(i_hq))
                                ((s_ax if oi % 2 == 0 else s_g2)).dma(
                                    g.dma_start(
                                    xn1[h_ // 2][64:128, qc_ * 512:(qc_ + 1) * 512],
                                    atto[oi % 2][:]))

            ln_scale_calc(marks1, None, 8, 9)

            xq1 = [sb(f"xq1_{i}", [128, T], BF16, pl1) for i in range(FT)]
            with ExitStack() as ph:
                big32 = [sb(f"bgq1{i}", [128, T], F32, ph) for i in range(1)]
                ln_quant_sweep(xq1, big32, bc_m1, bc_r1, sinv_col=scmc[:, 8:9])

            with ExitStack() as ph2:
                wprj = [sb(f"wprj{k}", [128, D], BF16, ph2) for k in range(FT)]
                with bk() as blk:
                    pe1, ve1 = s_pe.n, s_ve.n

                    @blk.sync
                    def _(sy):
                        gate(sy)
                        for k in range(FT):
                            s_lw.dma(sy.dma_start(wprj[k][:], S["wprojT"][k]))

                    @blk.tensor
                    def _(t):
                        gate(t)
                        t.wait_ge(s_lw.h, s_lw.n)
                        for dm in range(FT):
                            for ch in range(N5):
                                q = dm * N5 + ch
                                if q >= 4:
                                    t.wait_ge(s_ve.h, ve1 + q - 3)
                                for k in range(FT):
                                    i = t.matmul(psum[q % 4][:, 0:C5],
                                                 wprj[k][:, dm * 128:(dm + 1) * 128],
                                                 xn1[k][:, ch * C5:(ch + 1) * C5],
                                                 start=(k == 0), stop=(k == FT - 1))
                                s_pe.inc(i)

                    @blk.vector
                    def _(v):
                        for dm in range(FT):
                            for ch in range(N5):
                                q = dm * N5 + ch
                                v.wait_ge(s_pe.h, pe1 + q + 1)
                                s_ve.inc(v.tensor_tensor(
                                    res[dm][:, ch * C5:(ch + 1) * C5],
                                    res[dm][:, ch * C5:(ch + 1) * C5],
                                    psum[q % 4][:, 0:C5], ALU.add))


            dbg_dump(3)

            with ExitStack() as ph:
                fwt = [sb(f"fwt{k}", [128, D], BF16, ph) for k in range(FT)]
                with bk() as blk:
                    pe1, ve1 = s_pe.n, s_ve.n

                    @blk.sync
                    def _(sy):
                        gate(sy)
                        for k in range(FT):
                            s_lw.dma(sy.dma_start(fwt[k][:], S["fwT"][k]))

                    @blk.vector
                    def _(v):
                        gate(v)
                        s_ve.inc(v.tensor_tensor(stat[:, 56:56 + FT], fsc[:, 0:FT],
                                                 scmc[:, 9:10].broadcast_to([128, FT]),
                                                 ALU.mult))
                        for dm in range(FT):
                            s_ve.inc(v.tensor_scalar(res[dm][:], res[dm][:],
                                                     fbc[:, dm:dm + 1], None, ALU.add))

                    ve2b = s_ve.n
                    @blk.tensor
                    def _(t):
                        gate(t)
                        t.wait_ge(s_lw.h, s_lw.n)
                        t.wait_ge(s_ve.h, s_ve.n)
                        for dm in range(FT):
                            for ch in range(N5):
                                q = dm * N5 + ch
                                if q >= 4:
                                    t.wait_ge(s_ve.h, ve2b + q - 3)
                                for k in range(FT):
                                    i = t.matmul(psum[q % 4][:, 0:C5],
                                                 fwt[k][:, dm * 128:(dm + 1) * 128],
                                                 xq1[k][:, ch * C5:(ch + 1) * C5],
                                                 start=(k == 0), stop=(k == FT - 1))
                                s_pe.inc(i)

                    @blk.vector
                    def _(v):
                        for dm in range(FT):
                            for ch in range(N5):
                                q = dm * N5 + ch
                                v.wait_ge(s_pe.h, pe1 + q + 1)
                                s_ve.inc(v.scalar_tensor_tensor(
                                    res[dm][:, ch * C5:(ch + 1) * C5],
                                    psum[q % 4][:, 0:C5], stat[:, 56 + dm:57 + dm],
                                    res[dm][:, ch * C5:(ch + 1) * C5],
                                    ALU.mult, ALU.add))

        dbg_dump(4)

        with ExitStack() as ph:
            ostg = [sb(f"ostg{i}", [128, D], F32, ph) for i in range(2)]
            with bk() as blk:
                pe1, ac1, st1 = s_pe.n, s_ac.n, s_st.n
                fwd_marks(("ost",), [s_st, s_s2], [s_st.n, s_s2.n], NTT)

                @blk.tensor
                def _(t):
                    gate(t)
                    for i in range(NTT):
                        for j in range(FT):
                            q = i * FT + j
                            if q >= 4:
                                t.wait_ge(s_ac.h, ac1 + q - 3)
                            s_pe.inc(t.transpose(psum[q % 4][:, 0:128],
                                                 res[j][:, i * 128:(i + 1) * 128],
                                                 ident_f[:]))

                @blk.scalar
                def _(sc):
                    for i in range(NTT):
                        if i >= 2:
                            dma_wait(sc, ("ost", i - 2))
                        for j in range(FT):
                            q = i * FT + j
                            sc.wait_ge(s_pe.h, pe1 + q + 1)
                            s_ac.inc(sc.copy(ostg[i % 2][:, j * 128:(j + 1) * 128],
                                             psum[q % 4][:, 0:128]))

                @blk.sync
                def _(sy):
                    for i in range(NTT):
                        sy.wait_ge(s_ac.h, ac1 + (i + 1) * FT)
                        (s_st if i % 2 == 0 else s_s2).dma(
                            sy.dma_start(out_h[i * 128:(i + 1) * 128, :],
                                         ostg[i % 2][:]))
                    sy.wait_ge(s_st.h, s_st.n)
                    sy.wait_ge(s_s2.h, s_s2.n)

        nc.all_engine_barrier()
        for s in sems.values():
            nc.gpsimd.sem_clear(s.h)
        nc.all_engine_barrier()

    return nc


_NAME_MAP = {
    "L0_attn_W_in": "i_W_in", "L0_attn_conv_w": "i_conv_w", "L0_attn_conv_b": "i_conv_b",
    "L0_attn_Wx": "i_Wx", "L0_attn_Wdt": "i_Wdt", "L0_attn_bdt": "i_bdt",
    "L0_attn_A_log": "i_A_log", "L0_attn_D": "i_Dd",
    "L0_attn_conv_w_b": "i_conv_w_b", "L0_attn_conv_b_b": "i_conv_b_b",
    "L0_attn_Wx_b": "i_Wx_b", "L0_attn_Wdt_b": "i_Wdt_b", "L0_attn_bdt_b": "i_bdt_b",
    "L0_attn_A_b_log": "i_A_b_log", "L0_attn_D_b": "i_Dd_b",
    "L0_attn_W_out": "i_W_out",
    "L0_ffn_clip_x": "i_clip_x", "L0_ffn_clip_down": "i_clip_down",
    "L0_ffn_w_gate": "i_w_gate", "L0_ffn_w_up": "i_w_up", "L0_ffn_w_down": "i_w_down",
    "L1_attn_w_attn": "i_w_attn", "L1_attn_w_proj": "i_w_proj",
    "L1_ffn_weight_int8": "i_fw_int8", "L1_ffn_weight_scale": "i_fw_scale",
    "L1_ffn_bias": "i_fbias",
}

_NC_CACHE = {}


def kernel(**inputs):
    B = inputs["L0_attn_hidden"].shape[0]
    T = inputs["L0_attn_hidden"].shape[1]
    if T not in _NC_CACHE:
        _NC_CACHE[T] = build_nc(T=T, n_cores=B)
    nc = _NC_CACHE[T]
    shared = {}
    for k, v in inputs.items():
        if k == "L0_attn_hidden":
            continue
        a = np.ascontiguousarray(v)
        if k == "L0_attn_conv_w" or k == "L0_attn_conv_w_b":
            pass
        shared[_NAME_MAP[k]] = a
    in_maps = []
    for b in range(B):
        m = dict(shared)
        m["x"] = np.ascontiguousarray(inputs["L0_attn_hidden"][b])
        in_maps.append(m)
    r = run_bass_kernel_spmd(nc, in_maps, core_ids=list(range(B)))
    return np.stack([r.results[b]["out"] for b in range(B)], axis=0)
